# version 16
# baseline (speedup 1.0000x reference)
import numpy as np
import ml_dtypes
from contextlib import ExitStack

import concourse.bass as bass
import concourse.mybir as mybir
from concourse.bass_utils import run_bass_kernel_spmd

F32 = mybir.dt.float32
BF16 = mybir.dt.bfloat16
U8 = mybir.dt.uint8
AF = mybir.ActivationFunctionType
ALU = mybir.AluOpType
AX = mybir.AxisListType

S, D, H, HD = 4096, 1024, 16, 64
DFF, DI = 4096, 2816
NT, NDT = S // 128, D // 128
QC, NQC = 512, S // 512
NFT = DFF // 128
NFILL = 0
NIT = DI // 128


class Ctx:

    def __init__(self, nc):
        self.nc = nc
        self.counts = {}
        self.cur = None
        self.obj = None
        self.last_wait = {}
        self.mem = {}

    def begin(self, eng_name, eng_obj):
        self.cur = eng_name
        self.obj = eng_obj
        self.counts = {k: 0 for k in self.counts}
        self.last_wait = {}
        self.mem = {}

    def sem(self, handle):
        self.counts.setdefault(id(handle), 0)
        return handle

    def n(self, sem):
        return self.counts.get(id(sem), 0)

    def ins(self, eng, fn, waits=(), inc=None, inc_n=1):
        if inc is not None:
            self.counts[id(inc)] = self.counts.get(id(inc), 0) + inc_n
        if self.cur != eng:
            return None
        for sem, val in waits:
            if val <= 0 or self.last_wait.get(id(sem), -1) >= val:
                continue
            self.obj.wait_ge(sem, val)
            self.last_wait[id(sem)] = val
        inst = fn(self.obj)
        if inc is not None:
            inst.then_inc(inc, inc_n)
        return inst

    def dma(self, eng, out, in_, waits=(), inc=None):
        return self.ins(eng, lambda e: e.dma_start(out=out, in_=in_),
                        waits=waits, inc=inc, inc_n=16)


def build_program(debug_taps=(), phases=("A", "ATT", "BP", "B", "C", "D")):
    nc = bass.Bass("TRN2", target_bir_lowering=False,
                   detect_race_conditions=False)

    def din(name, shape, dt=F32):
        return nc.dram_tensor(name, shape, dt, kind="ExternalInput").ap()

    x_d = din("x", [S, D])
    l0_wq = din("l0_wq", [D, D]); l0_wk = din("l0_wk", [D, D]); l0_wv = din("l0_wv", [D, D])
    l0_bq = din("l0_bq", [D]); l0_bk = din("l0_bk", [D]); l0_bv = din("l0_bv", [D])
    l0_wo = din("l0_wo", [D, D]); l0_bo = din("l0_bo", [D])
    l0_gam = din("l0_gam", [D]); l0_bet = din("l0_bet", [D])
    l0_w1 = din("l0_w1", [DFF, D]); l0_b1 = din("l0_b1", [DFF])
    l0_w2 = din("l0_w2", [D, DFF]); l0_b2 = din("l0_b2", [D])
    l1_wq = din("l1_wq", [D, D]); l1_bq = din("l1_bq", [D])
    l1_wk0 = din("l1_wk0", [D, D]); l1_wk1 = din("l1_wk1", [D, D])
    l1_bk0 = din("l1_bk0", [D]); l1_bk1 = din("l1_bk1", [D])
    l1_wv0 = din("l1_wv0", [D, D]); l1_wv1 = din("l1_wv1", [D, D])
    l1_bv0 = din("l1_bv0", [D]); l1_bv1 = din("l1_bv1", [D])
    l1_wo = din("l1_wo", [D, D]); l1_bo = din("l1_bo", [D])
    l1_lng = din("l1_lng", [D]); l1_lnb = din("l1_lnb", [D])
    l1_wg = din("l1_wg", [DI, D]); l1_wu = din("l1_wu", [DI, D])
    l1_wd = din("l1_wd", [D, DI])
    idn_f = din("idn_f", [128, 128]); idn_b = din("idn_b", [128, 128], BF16)

    out_d = nc.dram_tensor("out", [S, D], F32, kind="ExternalOutput").ap()

    def scratch(name, shape, dt):
        kind = "ExternalOutput" if name in debug_taps else "Internal"
        return nc.dram_tensor(name, shape, dt, kind=kind).ap()

    xnTf_d = scratch("xnTf", [D, S], F32)
    xnTb_d = scratch("xnTb", [D, S], BF16)
    qT_d = scratch("qT", [D, S], BF16)
    kT_d = scratch("kT", [D, S], BF16)
    s0_d = scratch("s0", [D, S], F32)
    x1_d = scratch("x1", [S, D], F32)
    xn1T_d = scratch("xn1T", [D, S], BF16)
    xq1T_d = scratch("xq1T", [D, S], BF16)
    s1_d = scratch("s1", [D, S], BF16)
    w1qT_d = scratch("w1qT", [D, DFF], BF16)
    w2qT_d = scratch("w2qT", [DFF, D], BF16)
    h_d = scratch("hblk", [DFF, S], BF16)
    wgqT_d = scratch("wgqT", [D, DI], BF16)
    wuqT_d = scratch("wuqT", [D, DI], BF16)
    wdqT_d = scratch("wdqT", [DI, D], BF16)

    es = ExitStack()
    sb = lambda name, shape, dt=F32: es.enter_context(nc.sbuf_tensor(name, shape, dt))
    psa = lambda name: es.enter_context(nc.psum_tensor(name, [128, 1024], F32))

    sidf = sb("sidf", [128, 128])
    sidb = sb("sidb", [128, 128], BF16)
    sid64 = sb("sid64", [128, 128])
    onecol_b = sb("onecol_b", [128, 1], BF16)
    onerow_b = sb("onerow_b", [1, 128], BF16)
    onerow_f = sb("onerow_f", [1, 128], F32)
    fillA = sb("fillA", [128, 256])
    fillB = sb("fillB", [128, 256])
    cols = {}
    for nm in ("l0_bq", "l0_bk", "l0_bo", "l0_gam", "l0_bet", "l0_b2",
               "l1_bq", "l1_bo", "l1_lng", "l1_lnb"):
        cols[nm] = sb("c_" + nm, [128, NDT])
    cols["l0_b1"] = sb("c_l0_b1", [128, NFT])
    rows = {}
    for nm in ("l0_bv", "l1_bk0", "l1_bk1", "l1_bv0", "l1_bv1"):
        rows[nm] = sb("r_" + nm, [1, D], BF16)
    smf = sb("smf", [128, 64])
    smr = sb("smr", [1, 3 * QC])

    v_sb = sb("v_sb", [128, NT * H * 65], BF16)
    vv = v_sb[:].rearrange("p (t h c) -> p t h c", h=H, c=65)

    def av2(off, n, dt, parts=128):
        size = {F32: 4, BF16: 2}[dt]
        assert off % 32 == 0 and (off + n * size) // 2 <= NT * H * 65, (off, n)
        return v_sb[0:parts, off // 2: off // 2 + (n * size) // 2].bitcast(dt)

    ARENA = 124928
    arena = sb("arena", [128, ARENA], U8)

    def av(off, n, dt, parts=128):
        size = {F32: 4, BF16: 2}[dt]
        assert off % 32 == 0 and off + n * size <= ARENA, (off, n)
        return arena[0:parts, off: off + n * size].bitcast(dt)

    P = [psa(f"ps{i}") for i in range(4)]
    Pb = [p[:].bitcast(BF16) for p in P]

    sems = {}
    for nm in ("s_in", "s_w", "s_sp", "s_pe", "s_ve", "s_ac", "s_out", "s_gp"):
        sems[nm] = es.enter_context(nc.semaphore(nm))
    s_in = sems["s_in"]; s_w = sems["s_w"]; s_sp = sems["s_sp"]
    s_pe = sems["s_pe"]; s_ve = sems["s_ve"]; s_ac = sems["s_ac"]
    s_out = sems["s_out"]; s_gp = sems["s_gp"]

    C = Ctx(nc)
    for s_ in sems.values():
        C.sem(s_)

    def preamble():
        C.dma("sync", sidf[:], idn_f, inc=s_in)
        C.dma("sync", sidb[:], idn_b, inc=s_in)
        for nm, src in (("l0_bq", l0_bq), ("l0_bk", l0_bk), ("l0_bo", l0_bo),
                        ("l0_gam", l0_gam), ("l0_bet", l0_bet), ("l0_b2", l0_b2),
                        ("l1_bq", l1_bq), ("l1_bo", l1_bo),
                        ("l1_lng", l1_lng), ("l1_lnb", l1_lnb)):
            for t in range(NDT):
                C.dma("sync", cols[nm][:, t:t + 1], src[t * 128:(t + 1) * 128], inc=s_in)
        for t in range(NFT):
            C.dma("sync", cols["l0_b1"][:, t:t + 1], l0_b1[t * 128:(t + 1) * 128], inc=s_in)
        C.ins("vector", lambda v: v.memset(fillB[:], 0.0))
        C.ins("vector", lambda v: v.memset(fillA[:], 0.0))
        C.ins("vector", lambda v: v.tensor_scalar(out=sid64[:], in0=sidf[:],
              scalar1=1.0 / 64.0, scalar2=None, op0=ALU.mult),
              waits=[(s_in, 32)])
        C.ins("vector", lambda v: v.memset(onecol_b[:], 1.0))
        C.ins("vector", lambda v: v.memset(onerow_b[:], 1.0))
        C.ins("vector", lambda v: v.memset(onerow_f[:], 1.0), inc=s_ve)
        rstage = av(ARENA - 4096, D, F32, parts=1)
        for nm, src in (("l0_bv", l0_bv), ("l1_bk0", l1_bk0), ("l1_bk1", l1_bk1),
                        ("l1_bv0", l1_bv0), ("l1_bv1", l1_bv1)):
            C.dma("sync", rstage, src, waits=[(s_ve, C.n(s_ve))], inc=s_in)
            scl = (1.0 / 64.0) if nm.startswith("l1") else 1.0
            C.ins("vector",
                  lambda v, nm=nm, scl=scl: v.tensor_scalar(
                      out=rows[nm][:], in0=rstage, scalar1=scl, scalar2=None,
                      op0=ALU.mult),
                  waits=[(s_in, C.n(s_in))], inc=s_ve)

    A_xtok, A_xn, A_sq = 0, 8192, 12320
    A_ring = 16544
    A_stgf, A_stgb = 33184, 41408
    A_qstg = 45536
    A_wst = 47616
    A_W = 55808

    def ringv(k, t0, nt):
        sl = t0 % 8
        assert sl + nt <= 8
        base = A_ring + k * 2048 + sl * 256
        return av(base - sl * 256, 8 * 128, BF16)[:, sl * 128:(sl + nt) * 128] \
            if False else arena[0:128, A_ring + k * 2048: A_ring + (k + 1) * 2048] \
            .bitcast(BF16)[:, sl * 128:(sl + nt) * 128]

    def phase_A():
        wq_v = lambda k: av(A_W, 8 * 1024, BF16)[:, k * 1024:(k + 1) * 1024]
        wk_v = lambda k: av(A_W + 16384, 8 * 1024, BF16)[:, k * 1024:(k + 1) * 1024]
        wv_v = lambda k: av(A_W + 32768, 8 * 1024, BF16)[:, k * 1024:(k + 1) * 1024]
        for wsrc, wdst in ((l0_wq, wq_v), (l0_wk, wk_v), (l0_wv, wv_v)):
            for t in range(NDT):
                stg = av(A_wst + (t % 2) * 4096, 1024, F32)
                C.dma("scalar", stg, wsrc[t * 128:(t + 1) * 128, :],
                      waits=[(s_ve, C.mem.get(("wst", t % 2), 0))], inc=s_w)
                C.ins("vector",
                      lambda v, t=t, stg=stg, wdst=wdst: v.tensor_copy(
                          out=wdst(t), in_=stg),
                      waits=[(s_w, C.n(s_w))], inc=s_ve)
                C.mem[("wst", t % 2)] = C.n(s_ve)
        nw_ve = C.n(s_ve)

        for t in range(NT + 1):
            b = (t % 2) * 8
            smc = lambda i, b=b: smf[:, b + i:b + i + 1]
            u = t - 1
            bu = (u % 2) * 8
            smu = lambda i, bu=bu: smf[:, bu + i:bu + i + 1]
            xt = av(A_xtok + (t % 2) * 4096, 1024, F32)
            xu = av(A_xtok + (u % 2) * 4096, 1024, F32)
            xn = av(A_xn, 1024, F32)
            sq = av(A_sq, 1024, F32)

            if t < NT:
                C.dma("sync", xt, x_d[t * 128:(t + 1) * 128, :],
                      waits=[(s_ve, C.mem.get(("napply", t - 2), 0))], inc=s_in)
                n_in = C.n(s_in)
                C.ins("vector", lambda v: v.tensor_tensor(out=sq, in0=xt, in1=xt, op=ALU.mult),
                      waits=[(s_in, n_in)])
                C.ins("vector", lambda v: v.reduce_sum(out=smc(0), in_=xt, axis=AX.X))
                C.ins("vector", lambda v: v.reduce_sum(out=smc(1), in_=sq, axis=AX.X))
                C.ins("vector", lambda v: v.tensor_scalar(out=smc(2), in0=smc(0),
                      scalar1=1.0 / D, scalar2=None, op0=ALU.mult))
            if t == 0:
                C.ins("vector", lambda v: v.memset(vv[:, :, :, 64:65], 1.0), inc=s_ve)
                C.ins("vector", lambda v: v.tensor_copy(out=av(A_stgf, 1024, F32), in_=xt))
            if t >= 1:
                C.ins("vector", lambda v: v.tensor_scalar(out=xn, in0=xu,
                      scalar1=smu(2), scalar2=smu(7), op0=ALU.subtract, op1=ALU.mult),
                      waits=[(s_pe, C.mem.get(("ntr", u - 1), 0))], inc=s_ve)
                C.mem[("napply", u)] = C.n(s_ve)
                napply = C.n(s_ve)
                for k in range(NDT):
                    C.ins("tensor", lambda te, k=k, u=u: te.transpose(
                        P[u % 2][:, k * 128:(k + 1) * 128],
                        in_=xn[:, k * 128:(k + 1) * 128], identity=sidf[:]),
                        waits=[(s_ve, napply), (s_ve, C.mem.get(("trp_war", u % 2), 0))],
                        inc=s_pe if k == NDT - 1 else None)
                C.mem[("ntr", u)] = C.n(s_pe)
            if t < NT:
                C.ins("vector", lambda v: v.tensor_scalar(out=smc(3), in0=smc(2),
                      scalar1=smc(2), scalar2=-1.0, op0=ALU.mult, op1=ALU.mult))
            if t == 0:
                C.ins("vector", lambda v: v.tensor_copy(out=av(A_stgf + 4096, 1024, F32), in_=sq))
            if t >= 1:
                stf = av(A_stgf + (u % 2) * 4096, 1024, F32)
                C.ins("vector", lambda v, u=u, stf=stf: v.tensor_copy(out=stf, in_=P[u % 2][:]),
                      waits=[(s_pe, C.mem[("ntr", u)]),
                             (s_sp, C.mem.get(("stg_war", u % 2), 0))])
            if t < NT:
                C.ins("vector", lambda v: v.tensor_scalar(out=smc(4), in0=smc(1),
                      scalar1=1.0 / D, scalar2=1e-5, op0=ALU.mult, op1=ALU.add))
            if t == 0:
                C.ins("vector", lambda v: v.tensor_copy(out=av(A_stgb, 1024, BF16), in_=xt))
            if t >= 1:
                stb = av(A_stgb + (u % 2) * 2048, 1024, BF16)
                C.ins("vector", lambda v, u=u, stb=stb: v.tensor_copy(out=stb, in_=P[u % 2][:]))
            if t < NT:
                C.ins("vector", lambda v: v.tensor_tensor(out=smc(5), in0=smc(4),
                      in1=smc(3), op=ALU.add), inc=s_ve)
                C.ins("scalar", lambda sc: sc.activation(out=smc(6), in_=smc(5),
                      func=AF.Sqrt), waits=[(s_ve, C.n(s_ve))], inc=s_ac)
            if t >= 1:
                ringdst = arena[0:128, A_ring: A_ring + 16384].bitcast(BF16) \
                    .rearrange("p (k c) -> p k c", c=1024)[:, :, (u % 8) * 128:(u % 8) * 128 + 128]
                C.ins("vector", lambda v, u=u, ringdst=ringdst: v.tensor_copy(
                      out=ringdst, in_=P[u % 2][:].rearrange("p (k c) -> p k c", c=128)),
                      waits=[(s_pe, C.mem.get(("qkv_done", u // 4 - 2), 0))],
                      inc=s_ve)
                C.mem[("trp_war", u % 2)] = C.n(s_ve)
                C.mem[("ring", u)] = C.n(s_ve)
                nvr = C.n(s_ve)
                stf = av(A_stgf + (u % 2) * 4096, 1024, F32)
                stb = av(A_stgb + (u % 2) * 2048, 1024, BF16)
                for k in range(NDT):
                    C.dma("gpsimd", xnTf_d[k * 128:(k + 1) * 128, u * 128:(u + 1) * 128],
                          stf[:, k * 128:(k + 1) * 128], waits=[(s_ve, nvr)], inc=s_sp)
                    C.dma("gpsimd", xnTb_d[k * 128:(k + 1) * 128, u * 128:(u + 1) * 128],
                          stb[:, k * 128:(k + 1) * 128], waits=[(s_ve, nvr)], inc=s_sp)
                C.mem[("stg_war", u % 2)] = C.n(s_sp)
            if t < NT:
                C.ins("vector", lambda v: v.reciprocal(out=smc(7), in_=smc(6)),
                      waits=[(s_ac, C.n(s_ac))])

            if t % 4 == 0 and t >= 4:
                q = t // 4 - 1
                for wi, (wfn, bias_nm, dst_d) in enumerate(
                        ((wq_v, "l0_bq", qT_d), (wk_v, "l0_bk", kT_d))):
                    for m in range(NDT):
                        pslot = P[2][:, (m % 2) * 512:(m % 2) * 512 + 512]
                        for k in range(NDT):
                            C.ins("tensor", lambda te, m=m, k=k, q=q, wfn=wfn, pslot=pslot:
                                  te.matmul(pslot,
                                            lhsT=wfn(k)[:, m * 128:(m + 1) * 128],
                                            rhs=ringv(k, q * 4, 4),
                                            start=(k == 0), stop=(k == NDT - 1)),
                                  waits=[(s_ve, max(nw_ve, C.mem[("ring", q * 4 + 3)])),
                                         (s_ve, C.mem.get(("qke", m % 2), 0))],
                                  inc=s_pe if k == NDT - 1 else None)
                        npe = C.n(s_pe)
                        stg = av(A_qstg + (m % 2) * 1024, 512, BF16)
                        C.ins("vector", lambda v, stg=stg, pslot=pslot, bias_nm=bias_nm, m=m:
                              v.tensor_scalar(out=stg, in0=pslot,
                                              scalar1=cols[bias_nm][:, m:m + 1],
                                              scalar2=None, op0=ALU.add),
                              waits=[(s_pe, npe),
                                     (s_sp, C.mem.get(("qstg_war", m % 2), 0))],
                              inc=s_ve)
                        C.mem[("qke", m % 2)] = C.n(s_ve)
                        nse = C.n(s_ve)
                        C.dma("gpsimd", dst_d[m * 128:(m + 1) * 128, q * 512:(q + 1) * 512],
                              stg, waits=[(s_ve, nse)], inc=s_sp)
                        C.mem[("qstg_war", m % 2)] = C.n(s_sp)
                for ts4 in range(4):
                    tile = q * 4 + ts4
                    for n2 in range(2):
                        pslot = P[3][:, (n2 % 2) * 512:(n2 % 2) * 512 + 512]
                        for k in range(NDT):
                            C.ins("tensor", lambda te, k=k, tile=tile, pslot=pslot, n2=n2:
                                  te.matmul(pslot,
                                            lhsT=ringv(k, tile, 1),
                                            rhs=wv_v(k)[:, n2 * 512:(n2 + 1) * 512],
                                            start=(k == 0), stop=False),
                                  waits=[(s_ve, max(nw_ve, C.mem[("ring", q * 4 + 3)])),
                                         (s_ve, C.mem.get(("ve_war", n2), 0))])
                        C.ins("tensor", lambda te, pslot=pslot, n2=n2:
                              te.matmul(pslot, lhsT=onerow_b[:],
                                        rhs=rows["l0_bv"][:, n2 * 512:(n2 + 1) * 512],
                                        start=False, stop=True),
                              inc=s_pe)
                        npe = C.n(s_pe)
                        for hh in range(8):
                            h = n2 * 8 + hh
                            C.ins("vector", lambda v, tile=tile, h=h, hh=hh, pslot=pslot:
                                  v.tensor_copy(out=vv[:, tile, h, 0:64],
                                                in_=pslot[:, hh * 64:(hh + 1) * 64]),
                                  waits=[(s_pe, npe)],
                                  inc=s_ve if hh == 7 else None)
                        C.mem[("ve_war", n2)] = C.n(s_ve)
                C.mem[("qkv_done", q)] = C.n(s_pe)

    T_k = 0
    T_q = 16384
    T_pt = 37888
    T_ctxu = 46080
    T_ctx = 47104
    T_wo = 55296
    T_xnf = 71680
    T_y = 88064
    T_yb = 104448
    T_s = 112640

    def phase_ATT():
        fence_pe = C.n(s_pe)
        fence_sp = C.n(s_sp)
        wo_v = lambda k: av(T_wo, 8 * 1024, BF16)[:, k * 1024:(k + 1) * 1024]
        for t in range(NDT):
            stg = av(T_xnf + (t % 2) * 4096, 1024, F32)
            C.dma("scalar", stg, l0_wo[t * 128:(t + 1) * 128, :],
                  waits=[(s_pe, fence_pe), (s_sp, fence_sp),
                         (s_ve, C.mem.get(("wos", t % 2), 0))], inc=s_w)
            C.ins("vector", lambda v, t=t, stg=stg: v.tensor_copy(out=wo_v(t), in_=stg),
                  waits=[(s_w, C.n(s_w)), (s_pe, fence_pe), (s_sp, fence_sp)],
                  inc=s_ve)
            C.mem[("wos", t % 2)] = C.n(s_ve)
        C.mem["xnf_war0"] = C.n(s_ve)

        n_spill_done = C.n(s_sp)
        nv0 = C.n(s_ve)
        C.mem[("p01_free",)] = nv0
        C.mem[("ctx_war", 0)] = nv0
        C.mem[("ctx_war", 1)] = nv0
        C.mem[("bc_war", 0)] = nv0
        C.mem[("bc_war", 1)] = nv0
        C.mem["xnf_war"] = (s_ve, C.n(s_ve))

        kslot = lambda i: av(T_k + i * 8192, 4096, BF16)
        qvs = lambda i: av(T_q + i * 8192, 8 * 512, BF16) \
            .rearrange("p (k c) -> p k c", c=512)

        U_xn = 33792
        U_w = 116736
        U_xnv = lambda i: av(U_xn, 8 * 256, BF16).rearrange(
            "p (k c) -> p k c", c=256)
        U_wv = lambda i: av(U_w + i * 4096, 8 * 256, BF16).rearrange(
            "p (k c) -> p k c", c=256)
        U_hstg = lambda i: av(32768 + i * 512, 256, BF16)
        hu = {"pos": 0}
        C.mem[("u_rw", 0)] = C.n(s_ve)
        C.mem[("u_rw", 1)] = C.n(s_ve)
        bp_fence = C.n(s_sp)

        def emit_hu(nmm):
            for _ in range(nmm):
                pos = hu["pos"]
                if pos >= 4096:
                    return
                unit, k = pos // 8, pos % 8
                tc, ft = unit // 32, unit % 32
                if k == 0:
                    if ft == 0:
                        xv = U_xnv(tc % 2)
                        for kk in range(NDT):
                            C.dma("sync", xv[:, kk, :],
                                  xnTb_d[kk * 128:(kk + 1) * 128,
                                         tc * 256:(tc + 1) * 256],
                                  waits=[(s_sp, bp_fence),
                                         (s_pe, C.mem.get(("u_xnw",), 0))],
                                  inc=s_in)
                        C.mem[("u_xni",)] = C.n(s_in)
                    if ft % 2 == 0:
                        wv_ = U_wv((ft // 2) % 2)
                        for kk in range(NDT):
                            C.dma("sync", wv_[:, kk, :],
                                  w1qT_d[kk * 128:(kk + 1) * 128,
                                         ft * 128:ft * 128 + 256],
                                  waits=[(s_sp, bp_fence),
                                         (s_pe, C.mem.get(("u_ww", (ft // 2) % 2), 0))],
                                  inc=s_in)
                        C.mem[("u_wi", (ft // 2) % 2)] = C.n(s_in)
                up = P[3][:, 0:256] if unit % 2 == 0 else P[3][:, 512:768]
                C.ins("tensor", lambda te, up=up, k=k, ft=ft, tc=tc:
                      te.matmul(up,
                                lhsT=U_wv((ft // 2) % 2)[:, k, (ft % 2) * 128:(ft % 2) * 128 + 128],
                                rhs=U_xnv(tc % 2)[:, k, :],
                                start=(k == 0), stop=(k == 7)),
                      waits=[(s_in, max(C.mem.get(("u_xni",), 0),
                                        C.mem.get(("u_wi", (ft // 2) % 2), 0))),
                             (s_ve, C.mem.get(("u_rw", unit % 2), 0))],
                      inc=s_pe if k == 7 else None)
                if k == 7:
                    nue = C.n(s_pe)
                    if ft == 31:
                        C.mem[("u_xnw",)] = nue
                    if ft % 2 == 1:
                        C.mem[("u_ww", (ft // 2) % 2)] = nue
                    stg = U_hstg(unit % 2)
                    C.ins("vector", lambda v, up=up, stg=stg, ft=ft: v.tensor_scalar(
                          out=stg, in0=up, scalar1=cols["l0_b1"][:, ft:ft + 1],
                          scalar2=0.0, op0=ALU.add, op1=ALU.max),
                          waits=[(s_pe, nue),
                                 (s_sp, C.mem.get(("u_hw", unit % 2), 0))],
                          inc=s_ve)
                    C.mem[("u_rw", unit % 2)] = C.n(s_ve)
                    nrl = C.n(s_ve)
                    C.dma("gpsimd", h_d[ft * 128:(ft + 1) * 128,
                                        tc * 256:(tc + 1) * 256], stg,
                          waits=[(s_ve, nrl)], inc=s_sp)
                    C.mem[("u_hw", unit % 2)] = C.n(s_sp)
                hu["pos"] = pos + 1

        def dma_K(h):
            hb = (h % 2) * 64
            sl = kslot(h % 2)[hb:hb + 64, :]
            C.dma("sync", sl, kT_d[h * 64:h * 64 + 64, :],
                  waits=[(s_sp, n_spill_done),
                         (s_pe, max(fence_pe, C.mem.get(("k_war", h % 2), 0)))],
                  inc=s_in)
            C.mem[("k_in", h)] = C.n(s_in)
            C.mem[("k_in2", h)] = C.n(s_in)

        for qc in range(NQC):
            qv = qvs(qc % 2)
            for h in range(H):
                hb = (h % 2) * 64
                C.dma("sync", qv[hb:hb + 64, h // 2, :],
                      qT_d[h * 64:(h + 1) * 64, qc * QC:(qc + 1) * QC],
                      waits=[(s_sp, n_spill_done),
                             (s_pe, max(fence_pe, C.mem.get(("q_war", qc % 2), 0)))],
                      inc=s_in)
            nq_in = C.n(s_in)
            xnf = av(T_xnf, 8 * 512, F32).rearrange("p (k c) -> p k c", c=512)
            wsem, wval = C.mem["xnf_war"]
            for k in range(NDT):
                C.dma("sync", xnf[:, k, :], xnTf_d[k * 128:(k + 1) * 128, qc * QC:(qc + 1) * QC],
                      waits=[(wsem, wval), (s_ve, C.mem.get("xnf_war0", 0)),
                             (s_pe, fence_pe)], inc=s_in)
            nxn_in = C.n(s_in)

            dma_K(0)
            dma_K(1)

            def emit_tail(hh):
                C.ins("vector", lambda v, hh=hh: v.reciprocal(
                      out=smr[:, 0:QC],
                      in_=P[2][64:65, (hh % 2) * 512:(hh % 2) * 512 + 512]),
                      waits=[(s_pe, C.mem[("npv", hh)])], inc=s_ve)
                nrc = C.n(s_ve)
                C.ins("tensor", lambda te, hh=hh: te.matmul(
                      P[2][64:128, (hh % 2) * 512:(hh % 2) * 512 + 512],
                      lhsT=onerow_f[:, 0:64], rhs=smr[:, 0:QC],
                      start=True, stop=True, skip_group_check=True,
                      tile_position=(0, 64)),
                      waits=[(s_ve, nrc), (s_ve, C.mem.get(("bc_war", hh % 2), 0))],
                      inc=s_pe)
                nbc = C.n(s_pe)
                C.ins("vector", lambda v, hh=hh: v.tensor_copy(
                      out=av(T_ctxu, 512, BF16, parts=64),
                      in_=P[2][0:64, (hh % 2) * 512:(hh % 2) * 512 + 512]),
                      waits=[(s_pe, nbc)])
                C.ins("vector", lambda v, hh=hh: v.tensor_tensor(
                      out=av(T_ctx, 8 * 512, BF16)
                      .rearrange("p (k c) -> p k c", c=512)[(hh % 2) * 64:(hh % 2) * 64 + 64, hh // 2, :],
                      in0=av(T_ctxu, 512, BF16, parts=64),
                      in1=P[2][64:128, (hh % 2) * 512:(hh % 2) * 512 + 512],
                      op=ALU.mult), inc=s_ve)
                C.mem[("ctx_war", hh % 2)] = C.n(s_ve)
                C.mem[("bc_war", hh % 2)] = C.n(s_ve)

            for h in range(H):
                def emit_pv(g, h=h):
                    ptg = av(T_pt + (g % 4) * 2048, 1024, BF16)
                    nac = C.mem[("exp_done", g % 4)]
                    for j in range(2):
                        sbk = g * 2 + j
                        C.ins("tensor", lambda te, sbk=sbk, j=j, g=g, h=h, ptg=ptg:
                              te.matmul(P[2][0:65, (h % 2) * 512:(h % 2) * 512 + 512],
                                        lhsT=vv[:, sbk, h, :],
                                        rhs=ptg[:, j * 512:(j + 1) * 512],
                                        start=(sbk == 0), stop=(sbk == 31)),
                              waits=[(s_ac, nac),
                                     (s_ve, C.mem.get(("ctx_war", h % 2), 0)),
                                     (s_ve, C.mem.get(("pv_war",), 0))],
                              inc=s_pe if sbk == 31 else None)
                    C.mem[("pt_war", g % 4)] = C.n(s_pe)

                for g in range(16):
                    pg = P[g % 2]
                    ptg = av(T_pt + (g % 4) * 2048, 1024, BF16)
                    for j in range(2):
                        sbk = g * 2 + j
                        hb = (h % 2) * 64
                        lhsT = kslot(h % 2)[hb:hb + 64, sbk * 128:(sbk + 1) * 128]
                        C.ins("tensor", lambda te, lhsT=lhsT, pg=pg, j=j, h=h:
                              te.matmul(pg[:, j * 512:(j + 1) * 512],
                                        lhsT=lhsT,
                                        rhs=qv[(h % 2) * 64:(h % 2) * 64 + 64, h // 2, :],
                                        start=True, stop=True),
                              waits=[(s_in, max(C.mem[("k_in", h)], nq_in)),
                                     (s_ac, C.mem.get(("sc_war", g % 2), 0)),
                                     (s_ve, C.mem.get(("p01_free",), 0))],
                              inc=s_pe if j == 1 else None)
                    npe = C.n(s_pe)
                    C.ins("scalar", lambda s, pg=pg, ptg=ptg: s.activation(
                          out=ptg, in_=pg[:], func=AF.Exp, scale=0.125),
                          waits=[(s_pe, npe), (s_pe, C.mem.get(("pt_war", g % 4), 0))],
                          inc=s_ac)
                    C.mem[("sc_war", g % 2)] = C.n(s_ac)
                    C.mem[("exp_done", g % 4)] = C.n(s_ac)
                    emit_hu(2)
                    if g >= 2:
                        emit_pv(g - 2)
                    if g == 2 and h >= 1:
                        emit_tail(h - 1)
                emit_pv(14)
                emit_pv(15)
                npv = C.n(s_pe)
                C.mem[("npv", h)] = npv
                C.mem[("k_war", h % 2)] = npv
                if h + 2 < H:
                    dma_K(h + 2)
            emit_tail(H - 1)
            nctx = C.n(s_ve)

            ctxv = av(T_ctx, 8 * 512, BF16).rearrange("p (k c) -> p k c", c=512)
            yv = av(T_y, 8 * 512, F32).rearrange("p (k c) -> p k c", c=512)
            ybv = av(T_yb, 8 * 512, BF16).rearrange("p (k c) -> p k c", c=512)
            sv_ = av(T_s, 2 * 512, F32).rearrange("p (k c) -> p k c", c=512)
            for m in range(NDT):
                pm = P[2][:, (m % 2) * 512:(m % 2) * 512 + 512]
                for k in range(NDT):
                    C.ins("tensor", lambda te, m=m, k=k, pm=pm: te.matmul(
                          pm, lhsT=wo_v(k)[:, m * 128:(m + 1) * 128],
                          rhs=ctxv[:, k, :],
                          start=(k == 0), stop=(k == NDT - 1)),
                          waits=[(s_ve, nctx), (s_ve, C.mem.get(("att_war", m % 2), 0))],
                          inc=s_pe if k == NDT - 1 else None)
                npm = C.n(s_pe)
                C.ins("vector", lambda v, m=m, pm=pm: v.scalar_tensor_tensor(
                      out=yv[:, m, :], in0=pm, scalar=cols["l0_bo"][:, m:m + 1],
                      in1=xnf[:, m, :], op0=ALU.add, op1=ALU.add),
                      waits=[(s_pe, npm), (s_in, nxn_in)], inc=s_ve)
                C.mem[("att_war", m % 2)] = C.n(s_ve)
                C.ins("vector", lambda v, m=m: v.tensor_copy(out=ybv[:, m, :], in_=yv[:, m, :]),
                      inc=s_ve)
            ny = C.n(s_ve)
            y2v = av(T_xnf, 8 * 512, BF16).rearrange("p (k c) -> p k c", c=512)
            for m in range(NDT):
                C.ins("vector", lambda v, m=m: v.tensor_tensor(
                      out=y2v[:, m, :], in0=ybv[:, m, :], in1=ybv[:, m, :], op=ALU.mult),
                      inc=s_ve if m == NDT - 1 else None)
            ny2 = C.n(s_ve)
            for m in range(NDT):
                C.ins("tensor", lambda te, m=m: te.matmul(
                      P[0][0:1, 0:512], lhsT=onecol_b[:], rhs=ybv[:, m, :],
                      start=(m == 0), stop=(m == NDT - 1)),
                      waits=[(s_ve, ny), (s_ac, C.n(s_ac))],
                      inc=s_pe if m == NDT - 1 else None)
            for m in range(NDT):
                C.ins("tensor", lambda te, m=m: te.matmul(
                      P[0][0:1, 512:1024], lhsT=onecol_b[:], rhs=y2v[:, m, :],
                      start=(m == 0), stop=(m == NDT - 1)),
                      waits=[(s_ve, ny2)],
                      inc=s_pe if m == NDT - 1 else None)
            nst = C.n(s_pe)
            C.mem["xnf_war"] = (s_pe, nst)
            C.ins("vector", lambda v: v.tensor_scalar(
                  out=smr[:, QC:2 * QC], in0=P[0][0:1, 0:512],
                  scalar1=1.0 / D, scalar2=None, op0=ALU.mult),
                  waits=[(s_pe, nst)])
            C.ins("vector", lambda v: v.tensor_tensor(
                  out=smr[:, 0:QC], in0=smr[:, QC:2 * QC], in1=smr[:, QC:2 * QC],
                  op=ALU.mult))
            C.ins("vector", lambda v: v.tensor_scalar(
                  out=smr[:, 2 * QC:3 * QC], in0=P[0][0:1, 512:1024],
                  scalar1=1.0 / D, scalar2=None, op0=ALU.mult))
            C.ins("vector", lambda v: v.scalar_tensor_tensor(
                  out=smr[:, 2 * QC:3 * QC], in0=smr[:, 2 * QC:3 * QC],
                  scalar=1e-3, in1=smr[:, 0:QC],
                  op0=ALU.add, op1=ALU.subtract), inc=s_ve)
            C.mem[("pv_war",)] = C.n(s_ve)
            C.ins("scalar", lambda s: s.activation(
                  out=smr[:, 0:QC], in_=smr[:, 2 * QC:3 * QC], func=AF.Sqrt),
                  waits=[(s_ve, C.n(s_ve))], inc=s_ac)
            C.ins("vector", lambda v: v.reciprocal(
                  out=smr[:, 2 * QC:3 * QC], in_=smr[:, 0:QC]),
                  waits=[(s_ac, C.n(s_ac))], inc=s_ve)
            nrr = C.n(s_ve)
            C.ins("tensor", lambda te: te.matmul(P[0][:, 0:512], lhsT=onerow_f[:],
                  rhs=smr[:, QC:2 * QC], start=True, stop=True),
                  waits=[(s_ve, nrr), (s_ac, C.n(s_ac))],
                  inc=s_pe)
            C.ins("tensor", lambda te: te.matmul(P[1][:, 0:512], lhsT=onerow_f[:],
                  rhs=smr[:, 2 * QC:3 * QC], start=True, stop=True),
                  inc=s_pe)
            nbc2 = C.n(s_pe)
            for m in range(NDT):
                sl = sv_[:, m % 2, :]
                C.ins("vector", lambda v, m=m, sl=sl: v.tensor_tensor(
                      out=sl, in0=yv[:, m, :], in1=P[0][:, 0:512],
                      op=ALU.subtract),
                      waits=[(s_pe, nbc2), (s_sp, C.mem.get(("s_war", m % 2), 0))])
                C.ins("vector", lambda v, m=m, sl=sl: v.tensor_tensor(
                      out=sl, in0=sl, in1=P[1][:, 0:512],
                      op=ALU.mult))
                C.ins("vector", lambda v, m=m, sl=sl: v.tensor_scalar(
                      out=sl, in0=sl,
                      scalar1=cols["l0_gam"][:, m:m + 1],
                      scalar2=cols["l0_bet"][:, m:m + 1],
                      op0=ALU.mult, op1=ALU.add),
                      inc=s_ve)
                nsv = C.n(s_ve)
                C.dma("gpsimd", s0_d[m * 128:(m + 1) * 128, qc * QC:(qc + 1) * QC],
                      sl, waits=[(s_ve, nsv)], inc=s_sp)
                C.mem[("s_war", m % 2)] = C.n(s_sp)
            nsv = C.n(s_ve)
            C.mem[("q_war", qc % 2)] = npv
            C.mem[("p01_free",)] = nsv
            C.mem[("ctx_war", 0)] = ny
            C.mem[("ctx_war", 1)] = ny

    B_xn = 65536
    B_h = 69664
    B_g = 86048
    B_s = 94240
    B_x = 102432
    B_x1 = 110624
    B_sg = 114720

    def fillop():
        C.ins("vector", lambda v: v.tensor_copy(out=fillA[:], in_=fillB[:]))

    def phase_BP():
        fence_pe = C.n(s_pe)
        fence_ve = C.n(s_ve)
        fence_ac = C.n(s_ac)
        BP_stage = 65536
        BP_pos = 81920
        BP_neg = 90112
        BP_wqb = 98304

        def w_slices():
            yield ("w1", lambda t: l0_w1[t * 128:(t + 1) * 128, :], 32, 1024, 16)
            yield ("w2", lambda t: l0_w2[(t // 2) * 128:(t // 2 + 1) * 128,
                                         (t % 2) * 2048:(t % 2) * 2048 + 2048],
                   16, 2048, 48)

        for nm, srcf, ntile, fd, cb in w_slices():
            for t in range(ntile):
                stg = av(BP_stage + (t % 2) * 8192, fd, F32)
                C.dma("scalar", stg, srcf(t),
                      waits=[(s_pe, fence_pe),
                             (s_ve, C.mem.get(("bps", t % 2), fence_ve))], inc=s_w)
                C.ins("vector", lambda v, stg=stg, cb=cb, t=t: v.reduce_sum(
                      out=smf[:, cb + t:cb + t + 1], in_=stg, axis=AX.X,
                      apply_absolute_value=True),
                      waits=[(s_w, C.n(s_w)), (s_ac, fence_ac)], inc=s_ve)
                C.mem[("bps", t % 2)] = C.n(s_ve)
        fillop()
        C.ins("vector", lambda v: v.reduce_sum(out=smf[:, 15:16],
              in_=smf[:, 16:48], axis=AX.X))
        C.ins("vector", lambda v: v.reduce_sum(out=smf[:, 14:15],
              in_=smf[:, 48:64], axis=AX.X), inc=s_ve)
        nacc = C.n(s_ve)
        C.ins("tensor", lambda te: te.transpose(P[0][0:1, 0:128],
              in_=smf[:, 15:16], identity=sidf[:]),
              waits=[(s_ve, nacc)])
        C.ins("tensor", lambda te: te.transpose(P[0][0:1, 128:256],
              in_=smf[:, 14:15], identity=sidf[:]), inc=s_pe)
        ntp = C.n(s_pe)
        C.ins("vector", lambda v: v.reduce_sum(out=smr[:, 0:1],
              in_=P[0][0:1, 0:128], axis=AX.X), waits=[(s_pe, ntp)])
        C.ins("vector", lambda v: v.reduce_sum(out=smr[:, 1:2],
              in_=P[0][0:1, 128:256], axis=AX.X), inc=s_ve)
        ntot = C.n(s_ve)
        C.ins("scalar", lambda sc: sc.copy(out=smr[:, 2:3], in_=smr[:, 0:1]),
              waits=[(s_ve, ntot)])
        C.ins("scalar", lambda sc: sc.copy(out=smr[:, 3:4], in_=smr[:, 1:2]),
              inc=s_ac)
        C.ins("vector", lambda v: v.tensor_scalar(out=smr[:, 4:5], in0=smr[:, 2:3],
              scalar1=1.0 / (DFF * D), scalar2=1e-5, op0=ALU.mult, op1=ALU.add),
              waits=[(s_ac, C.n(s_ac))])
        C.ins("vector", lambda v: v.tensor_scalar(out=smr[:, 5:6], in0=smr[:, 3:4],
              scalar1=1.0 / (DFF * D), scalar2=1e-5, op0=ALU.mult, op1=ALU.add),
              inc=s_ve)
        C.ins("scalar", lambda sc: sc.copy(out=smr[:, 6:7], in_=smr[:, 4:5]),
              waits=[(s_ve, C.n(s_ve))])
        C.ins("scalar", lambda sc: sc.copy(out=smr[:, 7:8], in_=smr[:, 5:6]),
              inc=s_ac)
        C.ins("vector", lambda v: v.reciprocal(out=smr[:, 8:9], in_=smr[:, 6:7]),
              waits=[(s_ac, C.n(s_ac))])
        C.ins("vector", lambda v: v.reciprocal(out=smr[:, 9:10], in_=smr[:, 7:8]),
              inc=s_ve)
        nrg = C.n(s_ve)
        C.ins("tensor", lambda te: te.matmul(P[0][:, 0:1], lhsT=onerow_f[:],
              rhs=smr[:, 8:9], start=True, stop=True), waits=[(s_ve, nrg)])
        C.ins("tensor", lambda te: te.matmul(P[0][:, 1:2], lhsT=onerow_f[:],
              rhs=smr[:, 9:10], start=True, stop=True), inc=s_pe)
        nbg = C.n(s_pe)
        C.ins("vector", lambda v: v.tensor_copy(out=smf[:, 12:14], in_=P[0][:, 0:2]),
              waits=[(s_pe, nbg)], inc=s_ve)
        nrgc = C.n(s_ve)

        for nm, srcf, ntile, fd, cb in w_slices():
            rg = smf[:, 12:13] if nm == "w1" else smf[:, 13:14]
            for t in range(ntile):
                stg = av(BP_stage + (t % 2) * 8192, fd, F32)
                C.dma("scalar", stg, srcf(t),
                      waits=[(s_ve, C.mem.get(("bps2", t % 2), nrgc))], inc=s_w)
                nst = C.n(s_w)
                pos = av(BP_pos, fd, F32)
                neg = av(BP_neg, fd, F32)
                wqb = av(BP_wqb + (t % 2) * 4096, fd, BF16)
                C.ins("vector", lambda v, stg=stg, pos=pos, rg=rg: v.tensor_scalar(
                      out=pos, in0=stg, scalar1=rg, scalar2=0.5,
                      op0=ALU.mult, op1=ALU.is_gt), waits=[(s_w, nst)])
                C.ins("vector", lambda v, stg=stg, neg=neg, rg=rg: v.tensor_scalar(
                      out=neg, in0=stg, scalar1=rg, scalar2=-0.5,
                      op0=ALU.mult, op1=ALU.is_lt))
                C.ins("vector", lambda v, pos=pos, neg=neg, wqb=wqb: v.tensor_tensor(
                      out=wqb, in0=pos, in1=neg, op=ALU.subtract),
                      waits=[(s_pe, C.mem.get(("bpt", t % 2), 0))], inc=s_ve)
                C.mem[("bps2", t % 2)] = C.n(s_ve)
                nwq = C.n(s_ve)
                ntr = fd // 128
                for j in range(ntr):
                    C.ins("tensor", lambda te, j=j, t=t, wqb=wqb: te.transpose(
                          Pb[2 + (t % 2)][:, (j % 8) * 128:(j % 8) * 128 + 128],
                          in_=wqb[:, j * 128:(j + 1) * 128], identity=sidb[:]),
                          waits=[(s_ve, nwq),
                                 (s_ve, C.mem.get(("bpe", t % 2), 0))],
                          inc=s_pe if (j % 8 == 7 or j == ntr - 1) else None)
                    if j % 8 == 7 or j == ntr - 1:
                        npe = C.n(s_pe)
                        j0 = (j // 8) * 8
                        nj = j - j0 + 1
                        stg2 = av(BP_wqb + 8192 + (t % 2) * 2048, 1024, BF16)
                        C.ins("vector", lambda v, stg2=stg2, nj=nj, t=t: v.tensor_copy(
                              out=stg2[:, 0:nj * 128], in_=Pb[2 + (t % 2)][:, 0:nj * 128]),
                              waits=[(s_pe, npe),
                                     (s_sp, C.mem.get(("bpw", t % 2), 0))], inc=s_ve)
                        C.mem[("bpe", t % 2)] = C.n(s_ve)
                        nbe = C.n(s_ve)
                        for jj in range(nj):
                            if nm == "w1":
                                dst = w1qT_d[(j0 + jj) * 128:(j0 + jj + 1) * 128,
                                             t * 128:(t + 1) * 128]
                            else:
                                fi = (t % 2) * 16 + j0 + jj
                                dst = w2qT_d[fi * 128:(fi + 1) * 128,
                                             (t // 2) * 128:(t // 2) * 128 + 128]
                            C.dma("gpsimd", dst, stg2[:, jj * 128:(jj + 1) * 128],
                                  waits=[(s_ve, nbe)], inc=s_sp)
                        C.mem[("bpw", t % 2)] = C.n(s_sp)
                C.mem[("bpt", t % 2)] = C.n(s_pe)

    def phase_B():
        fence_pe = C.n(s_pe)
        fence_ve = C.n(s_ve)
        fence_sp = C.n(s_sp)
        w2qT = av(0, 32 * 1024, BF16).rearrange("p (j d) -> p j d", d=1024)
        for j in range(NFT):
            C.dma("sync", w2qT[:, j, :], w2qT_d[j * 128:(j + 1) * 128, :],
                  waits=[(s_pe, fence_pe), (s_ve, fence_ve), (s_sp, fence_sp)],
                  inc=s_in)
        nw2 = C.n(s_in)
        for tc in range(16):
            hv = av(B_h, 32 * 256, BF16).rearrange("p (f c) -> p f c", c=256)
            for j in range(NFT):
                C.dma("scalar", hv[:, j, :],
                      h_d[j * 128:(j + 1) * 128, tc * 256:(tc + 1) * 256],
                      waits=[(s_sp, fence_sp), (s_pe, C.mem.get(("bh_done",), 0))],
                      inc=s_w)
            nhv = C.n(s_w)
            sc_ = av(B_s, 8 * 256, F32).rearrange("p (k c) -> p k c", c=256)
            for k in range(NDT):
                C.dma("gpsimd", sc_[:, k, :],
                      s0_d[k * 128:(k + 1) * 128, tc * 256:(tc + 1) * 256],
                      waits=[(s_ve, C.mem.get(("bs_war",), fence_ve))], inc=s_sp)
            nsc = C.n(s_sp)
            gv = av(B_g, 8 * 256, F32).rearrange("p (k c) -> p k c", c=256)
            for m in range(NDT):
                pslot = P[1][:, (m % 2) * 512:(m % 2) * 512 + 256]
                for j in range(NFT):
                    C.ins("tensor", lambda te, m=m, j=j, pslot=pslot:
                          te.matmul(pslot, lhsT=w2qT[:, j, m * 128:(m + 1) * 128],
                                    rhs=hv[:, j, :], start=(j == 0), stop=(j == NFT - 1)),
                          waits=[(s_in, nw2), (s_w, nhv),
                                 (s_ac, C.mem.get(("bgo", m % 2), 0))],
                          inc=s_pe if j == NFT - 1 else None)
                ng = C.n(s_pe)
                C.ins("scalar", lambda sc2, m=m, pslot=pslot: sc2.activation(
                      out=gv[:, m, :], in_=pslot, func=AF.Silu,
                      bias=cols["l0_b2"][:, m:m + 1]),
                      waits=[(s_pe, ng), (s_ve, C.mem.get(("bg_war",), 0))], inc=s_ac)
                C.mem[("bgo", m % 2)] = C.n(s_ac)
            C.mem[("bh_done",)] = C.n(s_pe)
            ngd = C.n(s_ac)
            sgv = av(B_sg, 8 * 256, F32).rearrange("p (k c) -> p k c", c=256)
            for m in range(NDT):
                C.ins("vector", lambda v, m=m: v.tensor_tensor(
                      out=sgv[:, m, :], in0=gv[:, m, :], in1=sc_[:, m, :], op=ALU.mult),
                      waits=[(s_ac, ngd), (s_sp, nsc),
                             (s_pe, C.mem.get(("bsg_pe",), 0))],
                      inc=s_ve if m == NDT - 1 else None)
            C.mem[("bs_war",)] = C.n(s_ve)
            C.mem[("bg_war",)] = C.n(s_ve)
            nsg = C.n(s_ve)
            for pt in range(2):
                xbuf = av(B_x + pt * 4096, 1024, F32)
                C.dma("sync", xbuf, x_d[tc * 256 + pt * 128: tc * 256 + pt * 128 + 128, :],
                      waits=[(s_ve, C.mem.get(("bx_war", pt), 0))], inc=s_in)
                nxb = C.n(s_in)
                for m in range(NDT):
                    C.ins("tensor", lambda te, m=m, pt=pt: te.transpose(
                          P[3][:, m * 128:(m + 1) * 128],
                          in_=sgv[:, m, pt * 128:pt * 128 + 128], identity=sidf[:]),
                          waits=[(s_ve, nsg), (s_ve, C.mem.get(("bx1_done",), 0))],
                          inc=s_pe if m == NDT - 1 else None)
                ntp = C.n(s_pe)
                C.mem[("bsg_pe",)] = ntp
                x1b = av(B_x1, 1024, F32)
                C.ins("vector", lambda v, pt=pt, xbuf=xbuf, x1b=x1b: v.tensor_tensor(
                      out=x1b, in0=xbuf, in1=P[3][:], op=ALU.add),
                      waits=[(s_pe, ntp), (s_in, nxb),
                             (s_sp, C.mem.get(("bx1_war",), 0))], inc=s_ve)
                C.mem[("bx_war", pt)] = C.n(s_ve)
                C.mem[("bx1_done",)] = C.n(s_ve)
                nx1 = C.n(s_ve)
                C.dma("gpsimd", x1_d[tc * 256 + pt * 128: tc * 256 + pt * 128 + 128, :],
                      x1b, waits=[(s_ve, nx1)], inc=s_sp)
                C.mem[("bx1_war",)] = C.n(s_sp)

    def emit_fp4(src, dst_bf, W, bu, sep=None, pre_done=False, dst_waits=()):
        nb = W // 16
        sep = sep or fillop
        if not pre_done:
            C.ins("vector", lambda v: v.reduce_max(
                  out=bu["bm"], in_=src.rearrange("p (b k) -> p b k", k=16),
                  axis=AX.X, apply_absolute_value=True))
            sep()
            C.ins("vector", lambda v: v.tensor_scalar(out=bu["sc"], in0=bu["bm"],
                  scalar1=1.0 / 6.0, scalar2=1e-12, op0=ALU.mult, op1=ALU.max))
            sep()
            C.ins("vector", lambda v: v.reciprocal(out=bu["rsc"], in_=bu["sc"]))
            sep()
        C.ins("vector", lambda v: v.tensor_copy(out=bu["scb"], in_=bu["sc"]))
        rscb = bu["rsc"].rearrange("p (b o) -> p b o", o=1).broadcast_to([128, nb, 16])
        scb = bu["scb"].rearrange("p (b o) -> p b o", o=1).broadcast_to([128, nb, 16])
        src3 = src.rearrange("p (b k) -> p b k", k=16)
        zz3 = bu["zz"].rearrange("p (b k) -> p b k", k=16)
        C.ins("vector", lambda v: v.tensor_tensor(out=zz3, in0=src3, in1=rscb,
              op=ALU.mult), waits=[(s_ac, C.n(s_ac))], inc=s_ve)
        nz = C.n(s_ve)
        C.ins("scalar", lambda sc2: sc2.activation(out=bu["aa"], in_=bu["zz"],
              func=AF.Abs), waits=[(s_ve, nz)])
        C.ins("scalar", lambda sc2: sc2.activation(out=bu["sgn"], in_=bu["zz"],
              func=AF.Sign), inc=s_ac)
        na = C.n(s_ac)
        aa, T1, T2, T3 = bu["aa"], bu["T1"], bu["T2"], bu["T3"]
        C.ins("vector", lambda v: v.tensor_scalar(out=T1, in0=aa, scalar1=0.25,
              scalar2=None, op0=ALU.is_gt), waits=[(s_ac, na)])
        C.ins("vector", lambda v: v.scalar_tensor_tensor(out=T2, in0=aa,
              scalar=0.75, in1=T1, op0=ALU.is_gt, op1=ALU.add))
        C.ins("vector", lambda v: v.scalar_tensor_tensor(out=T1, in0=aa,
              scalar=1.25, in1=T2, op0=ALU.is_gt, op1=ALU.add))
        C.ins("vector", lambda v: v.scalar_tensor_tensor(out=T2, in0=aa,
              scalar=1.75, in1=T1, op0=ALU.is_gt, op1=ALU.add))
        C.ins("vector", lambda v: v.tensor_scalar(out=T1, in0=aa, scalar1=2.5,
              scalar2=None, op0=ALU.is_gt))
        C.ins("vector", lambda v: v.scalar_tensor_tensor(out=T3, in0=aa,
              scalar=3.5, in1=T1, op0=ALU.is_gt, op1=ALU.add))
        C.ins("vector", lambda v: v.tensor_scalar(out=T1, in0=aa, scalar1=5.0,
              scalar2=2.0, op0=ALU.is_gt, op1=ALU.mult))
        C.ins("vector", lambda v: v.scalar_tensor_tensor(out=T2, in0=T2,
              scalar=0.5, in1=T3, op0=ALU.mult, op1=ALU.add))
        C.ins("vector", lambda v: v.tensor_tensor(out=T3, in0=T2, in1=T1,
              op=ALU.add))
        C.ins("vector", lambda v: v.tensor_tensor(out=T1, in0=T3, in1=bu["sgn"],
              op=ALU.mult))
        dst3 = dst_bf.rearrange("p (b k) -> p b k", k=16)
        t13 = T1.rearrange("p (b k) -> p b k", k=16)
        return C.ins("vector", lambda v: v.tensor_tensor(out=dst3, in0=t13,
              in1=scb, op=ALU.mult), waits=list(dst_waits), inc=s_ve)

    CP_x = 0
    CP_xn1 = 8192
    CP_sq = 12320
    CP_xnb = 16544
    CP_stg = 18592
    CP_q = 22688
    CP_qs = 47296
    CP_xq = 48128
    CM_wa = 51200
    CM_wb = 67584
    CM_xn = 83968
    CM_ks = 100352
    CM_vs = 108544
    CM_st0 = 116736
    CM_state = 120832
    CF_q = 0
    CF_out = 8192
    CF_t = 16384
    CF_s1 = 32768
    C3_stg = 0
    C3_q = 24576
    C3_wqb = 92160
    C3_qs = 103424
    I3_stg = 0
    I3_wqb = 8192
    I3_q = 12288
    I3_qs = 24576
    I3_srcb = 25600
    I3_stg3 = 27648

    def phase_C():
        fence_pe = C.n(s_pe)
        fence_sp = C.n(s_sp)
        fence_ve = C.n(s_ve)
        fence_ac = C.n(s_ac)
        wa = av(CM_wa, 8 * 1024, BF16).rearrange("p (k c) -> p k c", c=1024)
        wb = av(CM_wb, 8 * 1024, BF16).rearrange("p (k c) -> p k c", c=1024)
        nc1_ve = C.n(s_ve)
        nc1_pe = C.n(s_pe)
        nc1_ac = C.n(s_ac)
        for wi_ in range(2):
            for half_ in range(2):
                C.mem[("c_ke", wi_, half_)] = nc1_ve
        C.mem[("c_sacc",)] = nc1_ve

        def load_l1_w(dst, src, scale, with_I):
            for k in range(NDT):
                stg = av2(CF_t + (k % 2) * 4096, 1024, F32)
                C.dma("scalar", stg, src[k * 128:(k + 1) * 128, :],
                      waits=[(s_ve, C.mem.get(("c_lw", k % 2), nc1_ve)),
                             (s_pe, C.mem.get(("c_lwp",), nc1_pe))], inc=s_w)
                nw = C.n(s_w)
                C.ins("vector", lambda v, stg=stg, dst=dst, k=k, scale=scale:
                      v.tensor_scalar(out=dst[:, k, :], in0=stg, scalar1=scale,
                                      scalar2=None, op0=ALU.mult),
                      waits=[(s_w, nw), (s_pe, C.mem.get(("c_lwp",), nc1_pe))])
                if with_I:
                    C.ins("vector", lambda v, stg=stg, dst=dst, k=k, scale=scale:
                          v.scalar_tensor_tensor(
                              out=dst[:, k, k * 128:(k + 1) * 128],
                              in0=stg[:, k * 128:(k + 1) * 128], scalar=scale,
                              op0=ALU.mult, op1=ALU.add,
                              in1=sidf[:] if scale == 1.0 else sid64[:]))
                C.ins("vector", lambda v:
                      v.tensor_copy(out=fillA[:, 0:1], in_=fillB[:, 0:1]), inc=s_ve)
                C.mem[("c_lw", k % 2)] = C.n(s_ve)

        xncv = lambda i: av(CM_xn + i * 8192, 8 * 512, BF16).rearrange(
            "p (k c) -> p k c", c=512)
        ksv = av(CM_ks, 4 * 1024, BF16).rearrange("p (a c) -> p a c", c=1024)
        vsv = av(CM_vs, 4 * 1024, BF16).rearrange("p (a c) -> p a c", c=1024)

        def c2_dma(ch):
            xnc = xncv(ch % 2)
            for k in range(NDT):
                C.dma("scalar", xnc[:, k, :],
                      xn1T_d[k * 128:(k + 1) * 128, ch * 512:(ch + 1) * 512],
                      waits=[(s_sp, C.n(s_sp)),
                             (s_pe, C.mem.get(("c_xw", ch % 2), 0))], inc=s_w)
            C.mem[("c_xin", ch % 2)] = C.n(s_w)

        def c2_compute(ch, brow, vrow, stk):
            xnc = xncv(ch % 2)
            nlw = C.mem.get(("c_nlw",), 0)
            nxi = C.mem[("c_xin", ch % 2)]
            for sub in range(4):
                for half in range(2):
                    for wi, (wv_, dstv, bro) in enumerate(
                            ((wa, ksv, brow), (wb, vsv, vrow))):
                        pslot = P[wi][:, (half % 2) * 512:(half % 2) * 512 + 512]
                        for k in range(NDT):
                            C.ins("tensor", lambda te, k=k, sub=sub, half=half,
                                  wv_=wv_, pslot=pslot, xnc=xnc:
                                  te.matmul(pslot,
                                            lhsT=xnc[:, k, sub * 128:(sub + 1) * 128],
                                            rhs=wv_[:, k, half * 512:(half + 1) * 512],
                                            start=(k == 0), stop=False),
                                  waits=[(s_ve, nlw), (s_w, nxi),
                                         (s_ac, nc1_ac),
                                         (s_ve, C.mem.get(("c_ke", wi, half), 0))])
                        C.ins("tensor", lambda te, pslot=pslot, bro=bro, half=half:
                              te.matmul(pslot, lhsT=onerow_b[:],
                                        rhs=bro[:, half * 512:(half + 1) * 512],
                                        start=False, stop=True),
                              inc=s_pe)
                        npk = C.n(s_pe)
                        C.ins("vector", lambda v, dstv=dstv, sub=sub, half=half,
                              pslot=pslot: v.tensor_copy(
                                  out=dstv[:, sub, half * 512:(half + 1) * 512],
                                  in_=pslot),
                              waits=[(s_pe, npk)], inc=s_ve)
                        C.mem[("c_ke", wi, half)] = C.n(s_ve)
                C.mem[("c_nkv", sub)] = C.n(s_ve)
            pst = P[2]
            for h in range(H):
                for sub in range(4):
                    C.ins("tensor", lambda te, h=h, sub=sub, pst=pst:
                          te.matmul(pst[0:64, h * 64:(h + 1) * 64],
                                    lhsT=ksv[:, sub, h * 64:(h + 1) * 64],
                                    rhs=vsv[:, sub, h * 64:(h + 1) * 64],
                                    start=(sub == 0), stop=(sub == 3)),
                          waits=[(s_ve, C.mem[("c_nkv", 3)]),
                                 (s_ve, C.mem.get(("c_sacc",), 0))],
                          inc=s_pe if h == H - 1 else None)
            nsp = C.n(s_pe)
            if ch == 0:
                C.ins("vector", lambda v, pst=pst, stk=stk: v.tensor_copy(
                      out=stk, in_=pst[0:64, :]), waits=[(s_pe, nsp)], inc=s_ve)
            else:
                C.ins("vector", lambda v, pst=pst, stk=stk: v.tensor_tensor(
                      out=stk, in0=pst[0:64, :], in1=stk, op=ALU.add),
                      waits=[(s_pe, nsp)], inc=s_ve)
            C.mem[("c_sacc",)] = C.n(s_ve)
            C.mem[("c_xw", ch % 2)] = C.n(s_pe)

        def emit_c2_chunk(ch, brow, vrow, stk):
            c2_dma(ch)
            c2_compute(ch, brow, vrow, stk)

        qi3 = {k: av(I3_q + i * 2048, 1024, BF16) for i, k in
               enumerate(("zz", "aa", "sgn", "T1", "T2", "T3"))}
        qi3s = {"bm": av(I3_qs, 64, F32), "sc": av(I3_qs + 256, 64, F32),
                "rsc": av(I3_qs + 512, 64, F32), "scb": av(I3_qs + 768, 64, BF16)}
        i3_srcb = av(I3_srcb, 1024, BF16)
        i3 = {"d": 0, "q": 0, "t": 0}
        NI3 = 44

        def c3i_step(n):
            for _ in range(n):
                u = i3["d"]
                if u < NI3:
                    wsrc = l1_wg if u < 22 else l1_wu
                    t3 = u % 22
                    stg = av(I3_stg + (u % 2) * 4096, 1024, F32)
                    C.dma("scalar", stg, wsrc[t3 * 128:(t3 + 1) * 128, :],
                          waits=[(s_ve, C.mem.get(("i3s", u % 2), 0))], inc=s_w)
                    C.mem[("i3d", u % 2)] = C.n(s_w)
                    i3["d"] += 1
                u = i3["q"]
                if u < NI3 and (u <= i3["d"] - 2 or i3["d"] == NI3):
                    stg = av(I3_stg + (u % 2) * 4096, 1024, F32)
                    wqb = av(I3_wqb + (u % 2) * 2048, 1024, BF16)
                    C.ins("vector", lambda v, stg=stg: v.tensor_copy(
                          out=i3_srcb, in_=stg),
                          waits=[(s_w, C.mem[("i3d", u % 2)]), (s_ac, C.n(s_ac))])
                    emit_fp4(i3_srcb, wqb, 1024, {**qi3, **qi3s},
                             sep=lambda: C.ins("vector", lambda v: v.tensor_copy(
                                 out=fillA[:], in_=fillB[:])),
                             dst_waits=[(s_ve, C.mem.get(("i3e", u % 2), 0))])
                    C.mem[("i3s", u % 2)] = C.n(s_ve)
                    C.mem[("i3q", u % 2)] = C.n(s_ve)
                    i3["q"] += 1
                u = i3["t"]
                if u < NI3 and (u <= i3["q"] - 2 or i3["q"] == NI3):
                    t3 = u % 22
                    wdst_d = wgqT_d if u < 22 else wuqT_d
                    wqb = av(I3_wqb + (u % 2) * 2048, 1024, BF16)
                    for j in range(8):
                        C.ins("tensor", lambda te, j=j, wqb=wqb: te.transpose(
                              Pb[3][:, 1024 + j * 128:1024 + (j + 1) * 128],
                              in_=wqb[:, j * 128:(j + 1) * 128], identity=sidb[:]),
                              waits=[(s_ve, C.mem[("i3q", u % 2)]),
                                     (s_ve, C.mem.get(("i3ev",), 0))],
                              inc=s_pe if j == 7 else None)
                    npe3 = C.n(s_pe)
                    stg3i = av(I3_stg3, 1024, BF16)
                    C.ins("vector", lambda v, stg3i=stg3i: v.tensor_copy(
                          out=stg3i, in_=Pb[3][:, 1024:2048]),
                          waits=[(s_pe, npe3), (s_sp, C.mem.get(("i3w",), 0))],
                          inc=s_ve)
                    C.mem[("i3e", u % 2)] = C.n(s_ve)
                    C.mem[("i3ev",)] = C.n(s_ve)
                    nev3 = C.n(s_ve)
                    for j in range(8):
                        C.dma("gpsimd",
                              wdst_d[j * 128:(j + 1) * 128, t3 * 128:(t3 + 1) * 128],
                              stg3i[:, j * 128:(j + 1) * 128],
                              waits=[(s_ve, nev3)], inc=s_sp)
                    C.mem[("i3w",)] = C.n(s_sp)
                    i3["t"] += 1

        C.mem[("c_lwp",)] = C.n(s_pe)
        load_l1_w(wa, l1_wk0, 1.0 / 64.0, True)
        load_l1_w(wb, l1_wv0, 1.0 / 64.0, True)
        C.mem[("c_nlw",)] = C.n(s_ve)

        qb = {k: av(CP_q + i * 2048, 1024, BF16) for i, k in
              enumerate(("zz", "aa", "sgn", "T1", "T2", "T3"))}
        qs = {k: av(CP_qs + i * 256, 64, F32) for i, k in
              enumerate(("bm", "sc", "rsc"))}
        qs["scb"] = av(CP_qs + 768, 64, BF16)
        for t in range(NT + 1):
            b = (t % 2) * 8
            smc = lambda i, b=b: smf[:, b + i:b + i + 1]
            u = t - 1
            bu_ = (u % 2) * 8
            smu = lambda i, bu_=bu_: smf[:, bu_ + i:bu_ + i + 1]
            xt = av(CP_x + (t % 2) * 4096, 1024, F32)
            xu = av(CP_x + (u % 2) * 4096, 1024, F32)
            xn1 = av(CP_xn1, 1024, F32)
            sq = av(CP_sq, 1024, F32)
            xnb = av(CP_xnb, 1024, BF16)
            xq = av(CP_xq, 1024, BF16)
            if t < NT:
                C.dma("sync", xt, x1_d[t * 128:(t + 1) * 128, :],
                      waits=[(s_ve, C.mem.get(("c_napply", t - 2), fence_ve)),
                             (s_sp, C.mem.get(("c_x1w", t), fence_sp))], inc=s_in)
                n_in = C.n(s_in)
                C.ins("vector", lambda v: v.tensor_tensor(out=sq, in0=xt, in1=xt,
                      op=ALU.mult), waits=[(s_in, n_in), (s_pe, fence_pe),
                                           (s_ac, fence_ac)])
                C.ins("vector", lambda v: v.reduce_sum(out=smc(0), in_=xt, axis=AX.X))
                C.ins("vector", lambda v: v.reduce_sum(out=smc(1), in_=sq, axis=AX.X))
                C.ins("vector", lambda v: v.tensor_scalar(out=smc(2), in0=smc(0),
                      scalar1=1.0 / D, scalar2=None, op0=ALU.mult))
            if t == 0:
                C.ins("vector", lambda v: v.tensor_copy(out=av(CP_xn1, 1024, F32), in_=xt))
            if t >= 1:
                C.ins("vector", lambda v: v.tensor_scalar(out=xn1, in0=xu,
                      scalar1=smu(2), scalar2=smu(7), op0=ALU.subtract, op1=ALU.mult),
                      waits=[(s_ac, C.mem.get(("c_quant_ac", u - 1), 0))], inc=s_ve)
                C.mem[("c_napply", u)] = C.n(s_ve)
            if t < NT:
                C.ins("vector", lambda v: v.tensor_scalar(out=smc(3), in0=smc(2),
                      scalar1=smc(2), scalar2=-1.0, op0=ALU.mult, op1=ALU.mult))
            if t >= 1:
                C.ins("vector", lambda v: v.tensor_copy(out=xnb, in_=xn1), inc=s_ve)
                nxnb = C.n(s_ve)
                C.ins("vector", lambda v: v.reduce_max(
                      out=qs["bm"], in_=xnb.rearrange("p (b k) -> p b k", k=16),
                      axis=AX.X, apply_absolute_value=True))
            if t == 0:
                C.ins("vector", lambda v: v.tensor_copy(out=av(CP_sq, 1024, F32), in_=xt))
            if t < NT:
                C.ins("vector", lambda v: v.tensor_scalar(out=smc(4), in0=smc(1),
                      scalar1=1.0 / D, scalar2=1e-5, op0=ALU.mult, op1=ALU.add))
            if t == 0:
                C.ins("vector", lambda v: v.tensor_copy(out=av(CP_xnb, 1024, BF16), in_=xt))
            if t >= 1:
                for k in range(NDT):
                    C.ins("tensor", lambda te, k=k: te.transpose(
                          Pb[3][:, k * 128:(k + 1) * 128],
                          in_=xnb[:, k * 128:(k + 1) * 128], identity=sidb[:]),
                          waits=[(s_ve, nxnb), (s_ve, C.mem.get(("c_tw", 0), 0))],
                          inc=s_pe if k == NDT - 1 else None)
                ntp1 = C.n(s_pe)
                stg1 = av(CP_stg, 1024, BF16)
                C.ins("vector", lambda v, stg1=stg1: v.tensor_copy(out=stg1, in_=Pb[3][:, 0:1024]),
                      waits=[(s_pe, ntp1), (s_sp, C.mem.get(("c_sw", 0), 0))], inc=s_ve)
                C.mem[("c_tw", 0)] = C.n(s_ve)
                nst1 = C.n(s_ve)
                for k in range(NDT):
                    C.dma("gpsimd", xn1T_d[k * 128:(k + 1) * 128, u * 128:(u + 1) * 128],
                          stg1[:, k * 128:(k + 1) * 128], waits=[(s_ve, nst1)], inc=s_sp)
                C.mem[("c_sw", 0)] = C.n(s_sp)
            if t < NT:
                C.ins("vector", lambda v: v.tensor_tensor(out=smc(5), in0=smc(4),
                      in1=smc(3), op=ALU.add), inc=s_ve)
                C.ins("scalar", lambda sc2: sc2.activation(out=smc(6), in_=smc(5),
                      func=AF.Sqrt), waits=[(s_ve, C.n(s_ve))], inc=s_ac)
            if t >= 1:
                C.ins("vector", lambda v: v.tensor_scalar(out=qs["sc"], in0=qs["bm"],
                      scalar1=1.0 / 6.0, scalar2=1e-12, op0=ALU.mult, op1=ALU.max))
                fillop()
                C.ins("vector", lambda v: v.reciprocal(out=qs["rsc"], in_=qs["sc"]))
                fillop()
                emit_fp4(xnb, xq, 1024, {**qb, **qs}, pre_done=True,
                         dst_waits=[(s_ve, C.mem.get(("c_tw", 1), 0))])
                nq = C.n(s_ve)
                C.mem[("c_quant_ac", u)] = C.n(s_ac)
                for k in range(NDT):
                    C.ins("tensor", lambda te, k=k: te.transpose(
                          Pb[3][:, 1024 + k * 128:1024 + (k + 1) * 128],
                          in_=xq[:, k * 128:(k + 1) * 128], identity=sidb[:]),
                          waits=[(s_ve, nq), (s_ve, C.mem.get(("c_tw", 1), 0))],
                          inc=s_pe if k == NDT - 1 else None)
                ntp2 = C.n(s_pe)
                stg2 = av(CP_stg + 2048, 1024, BF16)
                C.ins("vector", lambda v, stg2=stg2: v.tensor_copy(out=stg2, in_=Pb[3][:, 1024:2048]),
                      waits=[(s_pe, ntp2), (s_sp, C.mem.get(("c_sw", 1), 0))], inc=s_ve)
                C.mem[("c_tw", 1)] = C.n(s_ve)
                nst2 = C.n(s_ve)
                for k in range(NDT):
                    C.dma("gpsimd", xq1T_d[k * 128:(k + 1) * 128, u * 128:(u + 1) * 128],
                          stg2[:, k * 128:(k + 1) * 128], waits=[(s_ve, nst2)], inc=s_sp)
                C.mem[("c_sw", 1)] = C.n(s_sp)
            if t < NT:
                C.ins("vector", lambda v: v.reciprocal(out=smc(7), in_=smc(6)),
                      waits=[(s_ac, C.n(s_ac))])
            if t % 4 == 0 and t >= 4:
                ch_t = t // 4 - 1
                c2_dma(ch_t)
                if ch_t >= 1:
                    c2_compute(ch_t - 1, rows["l1_bk0"], rows["l1_bv0"],
                               av(CM_st0, 1024, F32, parts=64))
        c2_compute(NQC - 1, rows["l1_bk0"], rows["l1_bv0"],
                   av(CM_st0, 1024, F32, parts=64))

        C.mem[("c_lwp",)] = C.n(s_pe)
        load_l1_w(wa, l1_wk1, 1.0 / 64.0, True)
        load_l1_w(wb, l1_wv1, 1.0 / 64.0, True)
        C.mem[("c_nlw",)] = C.n(s_ve)
        c2_dma(0)
        for ch in range(NQC):
            if ch + 1 < NQC:
                c2_dma(ch + 1)
            c2_compute(ch, rows["l1_bk1"], rows["l1_bv1"],
                       av2(40960, 1024, F32, parts=64))
            c3i_step(3)
        nsboth = C.n(s_ve)
        for h in range(H):
            C.ins("vector", lambda v, h=h: v.tensor_tensor(
                  out=av(CM_state, 8 * 64, BF16)[(h % 2) * 64:(h % 2) * 64 + 64,
                                                 (h // 2) * 64:(h // 2) * 64 + 64],
                  in0=av2(40960, 1024, F32, parts=64)[:, h * 64:(h + 1) * 64],
                  in1=av(CM_st0, 1024, F32, parts=64)[:, h * 64:(h + 1) * 64],
                  op=ALU.mult),
                  waits=[(s_ve, nsboth)], inc=s_ve if h == H - 1 else None)
        nstate = C.n(s_ve)

        C.mem[("c_lwp",)] = C.n(s_pe)
        load_l1_w(wa, l1_wq, 1.0, True)
        load_l1_w(wb, l1_wo, 1.0, False)
        nlw2 = C.n(s_ve)
        qbf = av2(CF_q, 8 * 512, BF16).rearrange("p (k c) -> p k c", c=512)
        outv = av2(CF_out, 8 * 512, BF16).rearrange("p (k c) -> p k c", c=512)
        tstg = av2(CF_t, 8 * 512, F32).rearrange("p (k c) -> p k c", c=512)
        s1c = av2(CF_s1, 8 * 512, BF16).rearrange("p (k c) -> p k c", c=512)
        y2v2 = ksv
        lnov = vsv
        for ch in range(NQC):
            xnc = xncv(ch % 2)
            for k in range(NDT):
                C.dma("sync", xnc[:, k, :],
                      xn1T_d[k * 128:(k + 1) * 128, ch * 512:(ch + 1) * 512],
                      waits=[(s_pe, C.mem.get(("c_xw", ch % 2), 0))], inc=s_in)
            nxi = C.n(s_in)
            for m in range(NDT):
                pslot = P[0][:, (m % 2) * 512:(m % 2) * 512 + 512]
                for k in range(NDT):
                    C.ins("tensor", lambda te, m=m, k=k, pslot=pslot, xnc=xnc:
                          te.matmul(pslot, lhsT=wa[:, k, m * 128:(m + 1) * 128],
                                    rhs=xnc[:, k, :], start=(k == 0), stop=(k == NDT - 1)),
                          waits=[(s_ve, nlw2), (s_in, nxi),
                                 (s_ve, C.mem.get(("c_qe", m % 2), 0))],
                          inc=s_pe if k == NDT - 1 else None)
                nq_ = C.n(s_pe)
                C.ins("vector", lambda v, m=m, pslot=pslot: v.tensor_scalar(
                      out=qbf[:, m, :], in0=pslot,
                      scalar1=cols["l1_bq"][:, m:m + 1], scalar2=None, op0=ALU.add),
                      waits=[(s_pe, nq_)], inc=s_ve)
                C.mem[("c_qe", m % 2)] = C.n(s_ve)
            nqb = C.n(s_ve)
            for h in range(H):
                hb = (h % 2) * 64
                pslot = P[1][0:64, (h % 2) * 512:(h % 2) * 512 + 512]
                C.ins("tensor", lambda te, h=h, hb=hb, pslot=pslot:
                      te.matmul(pslot,
                                lhsT=av(CM_state, 8 * 64, BF16)[hb:hb + 64,
                                     (h // 2) * 64:(h // 2) * 64 + 64],
                                rhs=qbf[hb:hb + 64, h // 2, :],
                                start=True, stop=True),
                      waits=[(s_ve, max(nstate, nqb)),
                             (s_ve, C.mem.get(("c_oe", h % 2), 0))],
                      inc=s_pe)
                noh = C.n(s_pe)
                C.ins("vector", lambda v, h=h, hb=hb, pslot=pslot: v.tensor_copy(
                      out=outv[hb:hb + 64, h // 2, :], in_=pslot),
                      waits=[(s_pe, noh)], inc=s_ve)
                C.mem[("c_oe", h % 2)] = C.n(s_ve)
            nov = C.n(s_ve)
            for m in range(NDT):
                C.ins("vector", lambda v, m=m: v.tensor_tensor(
                      out=y2v2[:, m // 2, (m % 2) * 512:(m % 2) * 512 + 512],
                      in0=outv[:, m, :], in1=outv[:, m, :], op=ALU.mult),
                      inc=s_ve if m == NDT - 1 else None)
            ny2 = C.n(s_ve)
            for m in range(NDT):
                C.ins("tensor", lambda te, m=m: te.matmul(
                      P[2][0:1, 0:512], lhsT=onecol_b[:], rhs=outv[:, m, :],
                      start=(m == 0), stop=(m == NDT - 1)),
                      waits=[(s_ve, nov), (s_ve, C.mem.get(("c_rowr",), 0))],
                      inc=s_pe if m == NDT - 1 else None)
            for m in range(NDT):
                C.ins("tensor", lambda te, m=m: te.matmul(
                      P[2][0:1, 512:1024], lhsT=onecol_b[:],
                      rhs=y2v2[:, m // 2, (m % 2) * 512:(m % 2) * 512 + 512],
                      start=(m == 0), stop=(m == NDT - 1)),
                      waits=[(s_ve, ny2)],
                      inc=s_pe if m == NDT - 1 else None)
            nstt = C.n(s_pe)
            C.ins("vector", lambda v: v.tensor_scalar(
                  out=smr[:, QC:2 * QC], in0=P[2][0:1, 0:512],
                  scalar1=1.0 / D, scalar2=None, op0=ALU.mult),
                  waits=[(s_pe, nstt)])
            C.ins("vector", lambda v: v.tensor_tensor(
                  out=smr[:, 0:QC], in0=smr[:, QC:2 * QC], in1=smr[:, QC:2 * QC],
                  op=ALU.mult))
            C.ins("vector", lambda v: v.tensor_scalar(
                  out=smr[:, 2 * QC:3 * QC], in0=P[2][0:1, 512:1024],
                  scalar1=1.0 / D, scalar2=None, op0=ALU.mult))
            C.ins("vector", lambda v: v.scalar_tensor_tensor(
                  out=smr[:, 2 * QC:3 * QC], in0=smr[:, 2 * QC:3 * QC],
                  scalar=1e-5, in1=smr[:, 0:QC],
                  op0=ALU.add, op1=ALU.subtract), inc=s_ve)
            C.mem[("c_rowr",)] = C.n(s_ve)
            C.ins("scalar", lambda sc2: sc2.activation(
                  out=smr[:, 0:QC], in_=smr[:, 2 * QC:3 * QC], func=AF.Sqrt),
                  waits=[(s_ve, C.n(s_ve))], inc=s_ac)
            C.ins("vector", lambda v: v.reciprocal(
                  out=smr[:, 2 * QC:3 * QC], in_=smr[:, 0:QC]),
                  waits=[(s_ac, C.n(s_ac))], inc=s_ve)
            nrr = C.n(s_ve)
            C.ins("tensor", lambda te: te.matmul(P[2][:, 0:512], lhsT=onerow_f[:],
                  rhs=smr[:, QC:2 * QC], start=True, stop=True),
                  waits=[(s_ve, nrr)])
            C.ins("tensor", lambda te: te.matmul(P[3][:, 0:512], lhsT=onerow_f[:],
                  rhs=smr[:, 2 * QC:3 * QC], start=True, stop=True), inc=s_pe)
            nbc = C.n(s_pe)
            for m in range(NDT):
                C.ins("vector", lambda v, m=m: v.tensor_tensor(
                      out=tstg[:, m, :], in0=outv[:, m, :], in1=P[2][:, 0:512],
                      op=ALU.subtract), waits=[(s_pe, nbc)])
                C.ins("vector", lambda v, m=m: v.tensor_tensor(
                      out=tstg[:, m, :], in0=tstg[:, m, :], in1=P[3][:, 0:512],
                      op=ALU.mult))
                C.ins("vector", lambda v, m=m: v.tensor_scalar(
                      out=lnov[:, m // 2, (m % 2) * 512:(m % 2) * 512 + 512],
                      in0=tstg[:, m, :],
                      scalar1=cols["l1_lng"][:, m:m + 1],
                      scalar2=cols["l1_lnb"][:, m:m + 1],
                      op0=ALU.mult, op1=ALU.add),
                      inc=s_ve if m == NDT - 1 else None)
            nln = C.n(s_ve)
            for m in range(NDT):
                pslot = P[1][:, (m % 2) * 512:(m % 2) * 512 + 512]
                for k in range(NDT):
                    C.ins("tensor", lambda te, m=m, k=k, pslot=pslot:
                          te.matmul(pslot, lhsT=wb[:, k, m * 128:(m + 1) * 128],
                                    rhs=lnov[:, k // 2, (k % 2) * 512:(k % 2) * 512 + 512],
                                    start=(k == 0), stop=(k == NDT - 1)),
                          waits=[(s_ve, nln), (s_ve, C.mem.get(("c_s1e", m % 2), 0))],
                          inc=s_pe if k == NDT - 1 else None)
                ns1 = C.n(s_pe)
                C.ins("vector", lambda v, m=m, pslot=pslot: v.tensor_scalar(
                      out=s1c[:, m, :], in0=pslot,
                      scalar1=cols["l1_bo"][:, m:m + 1], scalar2=None, op0=ALU.add),
                      waits=[(s_pe, ns1), (s_sp, C.mem.get(("c_s1w",), 0))], inc=s_ve)
                C.mem[("c_s1e", m % 2)] = C.n(s_ve)
            ns1c = C.n(s_ve)
            for m in range(NDT):
                C.dma("gpsimd", s1_d[m * 128:(m + 1) * 128, ch * 512:(ch + 1) * 512],
                      s1c[:, m, :], waits=[(s_ve, ns1c)], inc=s_sp)
            C.mem[("c_s1w",)] = C.n(s_sp)
            C.mem[("c_xw", ch % 2)] = C.n(s_pe)
            c3i_step(3)
        while i3["t"] < NI3:
            c3i_step(1)

        q3 = {k: av(C3_q + i * 5632, 2816, BF16) for i, k in
              enumerate(("zz", "aa", "sgn", "T1", "T2", "T3", "srcb"))}
        q3s = {k: av(C3_qs + i * 704, 176, F32) for i, k in
              enumerate(("bm", "sc", "rsc"))}
        q3s["scb"] = av(C3_qs + 3 * 704, 176, BF16)
        nfc_ve = C.n(s_ve)
        for wi, (wsrc, wdst_d, nrow, fd) in enumerate(
                ((l1_wd, wdqT_d, 8, 2816),)):
            for t in range(nrow):
                stg = av(C3_stg + (t % 2) * 11264, fd, F32)
                C.dma("scalar", stg, wsrc[t * 128:(t + 1) * 128, :],
                      waits=[(s_ve, C.mem.get(("c3s", t % 2), nfc_ve))], inc=s_w)
                nw3 = C.n(s_w)
                wqb = av(C3_wqb + (t % 2) * 5632, fd, BF16)
                srcb = av(109696, 2816, BF16)[:, 0:fd]
                C.ins("vector", lambda v, stg=stg, srcb=srcb: v.tensor_copy(
                      out=srcb, in_=stg), waits=[(s_w, nw3), (s_ac, C.n(s_ac))])
                sub = {k: (q3[k][:, 0:fd] if k in q3 else None) for k in q3}
                sub.update({k: q3s[k][:, 0:fd // 16] for k in q3s})
                emit_fp4(srcb, wqb, fd, sub,
                         sep=lambda: C.ins("vector", lambda v: v.tensor_copy(
                             out=fillA[:], in_=fillB[:])),
                         dst_waits=[(s_ve, C.mem.get(("c3e", t % 2), 0))])
                C.mem[("c3s", t % 2)] = C.n(s_ve)
                nq3 = C.n(s_ve)
                ntr = fd // 128
                for j in range(ntr):
                    C.ins("tensor", lambda te, j=j, t=t, wqb=wqb: te.transpose(
                          Pb[2 + (t % 2)][:, (j % 8) * 128:(j % 8) * 128 + 128],
                          in_=wqb[:, j * 128:(j + 1) * 128], identity=sidb[:]),
                          waits=[(s_ve, nq3), (s_ve, C.mem.get(("c3e", t % 2), 0))],
                          inc=s_pe if (j % 8 == 7 or j == ntr - 1) else None)
                    if j % 8 == 7 or j == ntr - 1:
                        npe3 = C.n(s_pe)
                        j0 = (j // 8) * 8
                        nj = j - j0 + 1
                        stg3 = av(105568, 2048, BF16)
                        C.ins("vector", lambda v, nj=nj, t=t, stg3=stg3: v.tensor_copy(
                              out=stg3[:, 0:nj * 128], in_=Pb[2 + (t % 2)][:, 0:nj * 128]),
                              waits=[(s_pe, npe3), (s_sp, C.mem.get(("c3w",), 0))],
                              inc=s_ve)
                        C.mem[("c3e", t % 2)] = C.n(s_ve)
                        nev3 = C.n(s_ve)
                        for jj in range(nj):
                            C.dma("gpsimd",
                                  wdst_d[(j0 + jj) * 128:(j0 + jj + 1) * 128,
                                         t * 128:(t + 1) * 128],
                                  stg3[:, jj * 128:(jj + 1) * 128],
                                  waits=[(s_ve, nev3)], inc=s_sp)
                        C.mem[("c3w",)] = C.n(s_sp)

    D_h = 90112
    D_z = 117760
    D_a = 118784
    D_sgn = 119808
    D_T1 = 120832
    D_T2 = 121856
    D_T3 = 122880
    D_hq = 93184
    D_hqT = 95232
    D_sm = 123904

    def phase_D():
        fence_pe = C.n(s_pe)
        fence_ve = C.n(s_ve)
        fence_sp = C.n(s_sp)
        wgq = av(0, 8 * 2816, BF16).rearrange("p (k c) -> p k c", c=2816)
        wuq = av(45056, 8 * 2816, BF16).rearrange("p (k c) -> p k c", c=2816)
        wdq = av2(0, 22 * 1024, BF16).rearrange("p (j c) -> p j c", c=1024)
        for k in range(NDT):
            C.dma("sync", wgq[:, k, :], wgqT_d[k * 128:(k + 1) * 128, :],
                  waits=[(s_pe, fence_pe), (s_ve, fence_ve), (s_sp, fence_sp)],
                  inc=s_in)
            C.dma("sync", wuq[:, k, :], wuqT_d[k * 128:(k + 1) * 128, :], inc=s_in)
        for j in range(NIT):
            C.dma("sync", wdq[:, j, :], wdqT_d[j * 128:(j + 1) * 128, :], inc=s_in)
        nwl = C.n(s_in)
        qd = {"zz": av(D_z, 512, BF16), "aa": av(D_a, 512, BF16),
              "sgn": av(D_sgn, 512, BF16), "T1": av(D_T1, 512, BF16),
              "T2": av(D_T2, 512, BF16), "T3": av(D_T3, 512, BF16),
              "bm": av(D_sm, 32, F32), "sc": av(D_sm + 128, 32, F32),
              "rsc": av(D_sm + 256, 32, F32),
              "scb": av(D_sm + 384, 32, BF16)}
        xqc = av2(49152, 8 * 256, BF16).rearrange("p (k c) -> p k c", c=256)
        s1c = av2(45056, 8 * 256, BF16).rearrange("p (k c) -> p k c", c=256)
        gsg = av2(57344, 8 * 256, F32).rearrange("p (k c) -> p k c", c=256)
        hqTs = lambda i: av(D_hqT + i * 11264, 22 * 256, BF16).rearrange(
            "p (j c) -> p j c", c=256)
        sg1s = lambda i: av(D_hqT + i * 11264, 8 * 256, F32).rearrange(
            "p (k c) -> p k c", c=256)
        ICW = [512, 512, 512, 512, 512, 256]
        for tc in range(17):
            if tc < 16:
                hqT = hqTs(tc % 2)
                for k in range(NDT):
                    C.dma("scalar", xqc[:, k, :],
                          xq1T_d[k * 128:(k + 1) * 128, tc * 256:(tc + 1) * 256],
                          waits=[(s_in, nwl), (s_pe, C.mem.get(("d_xqw",), 0))],
                          inc=s_w)
                nxq = C.n(s_w)
                for pt in range(2):
                    for ic in range(6):
                        W = ICW[ic]
                        i0_ = ic * 512
                        pg = P[0][:, (ic % 2) * 512:(ic % 2) * 512 + W]
                        pu = P[1][:, (ic % 2) * 512:(ic % 2) * 512 + W]
                        for k in range(NDT):
                            C.ins("tensor", lambda te, k=k, pg=pg, pt=pt, i0_=i0_, W=W:
                                  te.matmul(pg, lhsT=xqc[:, k, pt * 128:(pt + 1) * 128],
                                            rhs=wgq[:, k, i0_:i0_ + W],
                                            start=(k == 0), stop=(k == NDT - 1)),
                                  waits=[(s_w, nxq),
                                         (s_ac, C.mem.get(("d_sil", ic % 2), 0))],
                                  inc=s_pe if k == NDT - 1 else None)
                            C.ins("tensor", lambda te, k=k, pu=pu, pt=pt, i0_=i0_, W=W:
                                  te.matmul(pu, lhsT=xqc[:, k, pt * 128:(pt + 1) * 128],
                                            rhs=wuq[:, k, i0_:i0_ + W],
                                            start=(k == 0), stop=(k == NDT - 1)),
                                  waits=[(s_ve, C.mem.get(("d_hm", ic % 2), 0))],
                                  inc=s_pe if k == NDT - 1 else None)
                        ngu = C.n(s_pe)
                        gsil = av(D_h, 512, F32)[:, 0:W]
                        hbuf = av(D_h + 2048, 512, BF16)[:, 0:W]
                        C.ins("scalar", lambda sc2, pg=pg, gsil=gsil: sc2.activation(
                              out=gsil, in_=pg, func=AF.Silu),
                              waits=[(s_pe, ngu), (s_ve, C.mem.get(("d_gs",), 0))],
                              inc=s_ac)
                        C.mem[("d_sil", ic % 2)] = C.n(s_ac)
                        nsil = C.n(s_ac)
                        C.ins("vector", lambda v, pu=pu, gsil=gsil, hbuf=hbuf:
                              v.tensor_tensor(out=hbuf, in0=pu, in1=gsil, op=ALU.mult),
                              waits=[(s_ac, nsil)], inc=s_ve)
                        C.mem[("d_gs",)] = C.n(s_ve)
                        C.mem[("d_hm", ic % 2)] = C.n(s_ve)
                        hq = av(D_hq + (ic % 2) * 1024, 512, BF16)[:, 0:W]
                        sub = {k2: (v2[:, 0:W] if k2 in ("zz", "aa", "sgn", "T1",
                                                          "T2", "T3")
                                    else v2[:, 0:W // 16]) for k2, v2 in qd.items()}
                        emit_fp4(hbuf, hq, W, sub,
                                 dst_waits=[(s_ve, C.mem.get(("d_hqe", ic % 2), 0))])
                        C.mem[("d_nhq", ic)] = C.n(s_ve)

                        def emit_tp(ic2, pt=pt, tc=tc, hqT=hqT):
                            W2 = ICW[ic2]
                            ntr = W2 // 128
                            hq2 = av(D_hq + (ic2 % 2) * 1024, 512, BF16)[:, 0:W2]
                            for j in range(ntr):
                                C.ins("tensor", lambda te, j=j, ic2=ic2, hq2=hq2: te.transpose(
                                      Pb[2][:, (ic2 % 2) * 1024 + j * 128:(ic2 % 2) * 1024 + (j + 1) * 128],
                                      in_=hq2[:, j * 128:(j + 1) * 128], identity=sidb[:]),
                                      waits=[(s_ve, C.mem[("d_nhq", ic2)]),
                                             (s_ve, C.mem.get(("d_hqe2", ic2 % 2), 0))],
                                      inc=s_pe if j == ntr - 1 else None)
                            ntp = C.n(s_pe)
                            C.ins("vector", lambda v, ic2=ic2, pt=pt, ntr=ntr, hqT=hqT:
                                  v.tensor_copy(
                                      out=hqT[:, ic2 * 4:ic2 * 4 + ntr, pt * 128:(pt + 1) * 128],
                                      in_=Pb[2][:, (ic2 % 2) * 1024:(ic2 % 2) * 1024 + ntr * 128]
                                      .rearrange("p (a c) -> p a c", c=128)),
                                  waits=[(s_pe, ntp),
                                         (s_pe, C.mem.get(("d_hqTw", tc % 2), 0))],
                                  inc=s_ve)
                            C.mem[("d_hqe", ic2 % 2)] = C.n(s_ve)
                            C.mem[("d_hqe2", ic2 % 2)] = C.n(s_ve)
                        if ic >= 1:
                            emit_tp(ic - 1)
                    emit_tp(5)
                C.mem[("d_xqw",)] = C.n(s_pe)
                C.mem[("d_nhqT", tc)] = C.n(s_ve)
            if tc >= 1:
                u = tc - 1
                hqT = hqTs(u % 2)
                sg1 = sg1s(u % 2)
                for k in range(NDT):
                    C.dma("gpsimd", s1c[:, k, :],
                          s1_d[k * 128:(k + 1) * 128, u * 256:(u + 1) * 256],
                          waits=[(s_ve, C.mem.get(("d_s1w",), 0))], inc=s_sp)
                ns1 = C.n(s_sp)
                nhqT = C.mem[("d_nhqT", u)]
                for m in range(NDT):
                    pd = P[3][:, (m % 2) * 512:(m % 2) * 512 + 256]
                    for j in range(NIT):
                        C.ins("tensor", lambda te, m=m, j=j, pd=pd, hqT=hqT:
                              te.matmul(pd, lhsT=wdq[:, j, m * 128:(m + 1) * 128],
                                        rhs=hqT[:, j, :], start=(j == 0),
                                        stop=(j == NIT - 1)),
                              waits=[(s_ve, nhqT),
                                     (s_ac, C.mem.get(("d_g1", m % 2), 0)),
                                     (s_ve, C.mem.get(("d_x2",), 0))],
                              inc=s_pe if j == NIT - 1 else None)
                    nd = C.n(s_pe)
                    C.ins("scalar", lambda sc2, m=m, pd=pd: sc2.activation(
                          out=gsg[:, m, :], in_=pd, func=AF.Silu),
                          waits=[(s_pe, nd), (s_ve, C.mem.get(("d_sgm",), 0))],
                          inc=s_ac)
                    C.mem[("d_g1", m % 2)] = C.n(s_ac)
                    C.mem[("d_g1m", m)] = C.n(s_ac)
                ndown = C.n(s_pe)
                for m in range(NDT):
                    C.ins("vector", lambda v, m=m, sg1=sg1: v.tensor_tensor(
                          out=sg1[:, m, :], in0=gsg[:, m, :], in1=s1c[:, m, :],
                          op=ALU.mult),
                          waits=[(s_ac, C.mem[("d_g1m", m)]), (s_sp, ns1),
                                 (s_pe, ndown)],
                          inc=s_ve if m == NDT - 1 else None)
                C.mem[("d_s1w",)] = C.n(s_ve)
                C.mem[("d_sgm",)] = C.n(s_ve)
                nsg = C.n(s_ve)
                for pt in range(2):
                    x1b = av2(53248, 1024, F32)
                    C.dma("sync", x1b,
                          x1_d[u * 256 + pt * 128:u * 256 + pt * 128 + 128, :],
                          waits=[(s_ve, C.mem.get(("d_x1w",), 0))], inc=s_in)
                    nx1 = C.n(s_in)
                    for m in range(NDT):
                        C.ins("tensor", lambda te, m=m, pt=pt, sg1=sg1: te.transpose(
                              P[3][:, m * 128:(m + 1) * 128],
                              in_=sg1[:, m, pt * 128:pt * 128 + 128], identity=sidf[:]),
                              waits=[(s_ve, nsg), (s_ve, C.mem.get(("d_x2",), 0)),
                                     (s_ac, C.mem[("d_g1m", NDT - 1)])],
                              inc=s_pe if m == NDT - 1 else None)
                    ntx = C.n(s_pe)
                    x2b = x1b
                    C.ins("vector", lambda v, x2b=x2b: v.tensor_tensor(
                          out=x2b, in0=x2b, in1=P[3][:], op=ALU.add),
                          waits=[(s_pe, ntx), (s_in, nx1),
                                 (s_out, C.mem.get(("d_ow",), 0))], inc=s_ve)
                    C.mem[("d_x2",)] = C.n(s_ve)
                    C.mem[("d_x1w",)] = C.n(s_ve)
                    nx2 = C.n(s_ve)
                    C.dma("sync", out_d[u * 256 + pt * 128:u * 256 + pt * 128 + 128, :],
                          x2b, waits=[(s_ve, nx2)], inc=s_out)
                    C.mem[("d_ow",)] = C.n(s_out)
                C.mem[("d_hqTw", u % 2)] = C.n(s_pe)

    def program():
        preamble()
        if "A" in phases:
            phase_A()
        if "BP" in phases:
            phase_BP()
        if "ATT" in phases:
            phase_ATT()
        if "B" in phases:
            phase_B()
        if "C" in phases:
            phase_C()
        if "D" in phases:
            phase_D()
        C.ins("sync", lambda e: e.nop(),
              waits=[(s_out, C.n(s_out)), (s_sp, C.n(s_sp)),
                     (s_in, C.n(s_in))])

    with nc.Block() as block:
        @block.sync
        def _(e):
            C.begin("sync", e); program()

        @block.vector
        def _(e):
            C.begin("vector", e); program()

        @block.tensor
        def _(e):
            C.begin("tensor", e); program()

        @block.scalar
        def _(e):
            C.begin("scalar", e); program()

        @block.gpsimd
        def _(e):
            C.begin("gpsimd", e); program()

    es.close()
    nc.finalize()
    return nc


def _prep_inputs(inputs):
    f = np.float32
    asc = np.ascontiguousarray
    base = {
        "l0_wq": asc(inputs["L0_attn_Wq"].reshape(D, D), dtype=f),
        "l0_wk": asc(inputs["L0_attn_Wk"].reshape(D, D), dtype=f),
        "l0_wv": asc(inputs["L0_attn_Wv"].reshape(D, D), dtype=f),
        "l0_bq": asc(inputs["L0_attn_bq"].reshape(D), dtype=f),
        "l0_bk": asc(inputs["L0_attn_bk"].reshape(D), dtype=f),
        "l0_bv": asc(inputs["L0_attn_bv"].reshape(D), dtype=f),
        "l0_wo": asc(inputs["L0_attn_Wo"].reshape(D, D), dtype=f),
        "l0_bo": asc(inputs["L0_attn_bo"], dtype=f),
        "l0_gam": asc(inputs["L0_attn_gamma"], dtype=f),
        "l0_bet": asc(inputs["L0_attn_beta"], dtype=f),
        "l0_w1": asc(inputs["L0_ffn_W1"], dtype=f),
        "l0_b1": asc(inputs["L0_ffn_b1"], dtype=f),
        "l0_w2": asc(inputs["L0_ffn_W2"], dtype=f),
        "l0_b2": asc(inputs["L0_ffn_b2"], dtype=f),
        "l1_wq": asc(inputs["L1_attn_Wq"], dtype=f),
        "l1_bq": asc(inputs["L1_attn_bq"], dtype=f),
        "l1_wk0": asc(inputs["L1_attn_Wk"][0], dtype=f),
        "l1_wk1": asc(inputs["L1_attn_Wk"][1], dtype=f),
        "l1_bk0": asc(inputs["L1_attn_bk"][0], dtype=f),
        "l1_bk1": asc(inputs["L1_attn_bk"][1], dtype=f),
        "l1_wv0": asc(inputs["L1_attn_Wv"][0], dtype=f),
        "l1_wv1": asc(inputs["L1_attn_Wv"][1], dtype=f),
        "l1_bv0": asc(inputs["L1_attn_bv"][0], dtype=f),
        "l1_bv1": asc(inputs["L1_attn_bv"][1], dtype=f),
        "l1_wo": asc(inputs["L1_attn_Wo"], dtype=f),
        "l1_bo": asc(inputs["L1_attn_bo"], dtype=f),
        "l1_lng": asc(inputs["L1_attn_ln_g"], dtype=f),
        "l1_lnb": asc(inputs["L1_attn_ln_b"], dtype=f),
        "l1_wg": asc(inputs["L1_ffn_w_gate"], dtype=f),
        "l1_wu": asc(inputs["L1_ffn_w_up"], dtype=f),
        "l1_wd": asc(inputs["L1_ffn_w_down"], dtype=f),
        "idn_f": np.eye(128, dtype=f),
        "idn_b": np.eye(128, dtype=ml_dtypes.bfloat16),
    }
    x = np.ascontiguousarray(np.asarray(inputs["L0_attn_x"], dtype=f))
    maps = []
    for b in range(8):
        m = dict(base)
        m["x"] = asc(x[b])
        maps.append(m)
    return maps


def kernel(**inputs):
    nc = build_program()
    in_maps = _prep_inputs(inputs)
    res = run_bass_kernel_spmd(nc, in_maps, core_ids=list(range(8)))
    return np.stack([np.asarray(res.results[b]["out"], dtype=np.float32)
                     for b in range(8)], axis=0)


# version 17
# speedup vs baseline: 1.0353x; 1.0353x over previous
import numpy as np
import ml_dtypes
from contextlib import ExitStack

import concourse.bass as bass
import concourse.mybir as mybir
from concourse.bass_utils import run_bass_kernel_spmd

F32 = mybir.dt.float32
BF16 = mybir.dt.bfloat16
U8 = mybir.dt.uint8
AF = mybir.ActivationFunctionType
ALU = mybir.AluOpType
AX = mybir.AxisListType

S, D, H, HD = 4096, 1024, 16, 64
DFF, DI = 4096, 2816
NT, NDT = S // 128, D // 128
QC, NQC = 512, S // 512
NFT = DFF // 128
NFILL = 0
NIT = DI // 128


class Ctx:

    def __init__(self, nc):
        self.nc = nc
        self.counts = {}
        self.cur = None
        self.obj = None
        self.last_wait = {}
        self.mem = {}

    def begin(self, eng_name, eng_obj):
        self.cur = eng_name
        self.obj = eng_obj
        self.counts = {k: 0 for k in self.counts}
        self.last_wait = {}
        self.mem = {}

    def sem(self, handle):
        self.counts.setdefault(id(handle), 0)
        return handle

    def n(self, sem):
        return self.counts.get(id(sem), 0)

    def ins(self, eng, fn, waits=(), inc=None, inc_n=1):
        if inc is not None:
            self.counts[id(inc)] = self.counts.get(id(inc), 0) + inc_n
        if self.cur != eng:
            return None
        for sem, val in waits:
            if val <= 0 or self.last_wait.get(id(sem), -1) >= val:
                continue
            self.obj.wait_ge(sem, val)
            self.last_wait[id(sem)] = val
        inst = fn(self.obj)
        if inc is not None:
            inst.then_inc(inc, inc_n)
        return inst

    def dma(self, eng, out, in_, waits=(), inc=None):
        return self.ins(eng, lambda e: e.dma_start(out=out, in_=in_),
                        waits=waits, inc=inc, inc_n=16)


def build_program(debug_taps=(), phases=("A", "ATT", "BP", "B", "C", "D")):
    nc = bass.Bass("TRN2", target_bir_lowering=False,
                   detect_race_conditions=False)

    def din(name, shape, dt=F32):
        return nc.dram_tensor(name, shape, dt, kind="ExternalInput").ap()

    x_d = din("x", [S, D])
    l0_wq = din("l0_wq", [D, D]); l0_wk = din("l0_wk", [D, D]); l0_wv = din("l0_wv", [D, D])
    l0_bq = din("l0_bq", [D]); l0_bk = din("l0_bk", [D]); l0_bv = din("l0_bv", [D])
    l0_wo = din("l0_wo", [D, D]); l0_bo = din("l0_bo", [D])
    l0_gam = din("l0_gam", [D]); l0_bet = din("l0_bet", [D])
    l0_w1 = din("l0_w1", [DFF, D]); l0_b1 = din("l0_b1", [DFF])
    l0_w2 = din("l0_w2", [D, DFF]); l0_b2 = din("l0_b2", [D])
    l1_wq = din("l1_wq", [D, D]); l1_bq = din("l1_bq", [D])
    l1_wk0 = din("l1_wk0", [D, D]); l1_wk1 = din("l1_wk1", [D, D])
    l1_bk0 = din("l1_bk0", [D]); l1_bk1 = din("l1_bk1", [D])
    l1_wv0 = din("l1_wv0", [D, D]); l1_wv1 = din("l1_wv1", [D, D])
    l1_bv0 = din("l1_bv0", [D]); l1_bv1 = din("l1_bv1", [D])
    l1_wo = din("l1_wo", [D, D]); l1_bo = din("l1_bo", [D])
    l1_lng = din("l1_lng", [D]); l1_lnb = din("l1_lnb", [D])
    l1_wg = din("l1_wg", [DI, D]); l1_wu = din("l1_wu", [DI, D])
    l1_wd = din("l1_wd", [D, DI])
    idn_f = din("idn_f", [128, 128]); idn_b = din("idn_b", [128, 128], BF16)

    out_d = nc.dram_tensor("out", [S, D], F32, kind="ExternalOutput").ap()

    def scratch(name, shape, dt):
        kind = "ExternalOutput" if name in debug_taps else "Internal"
        return nc.dram_tensor(name, shape, dt, kind=kind).ap()

    xnTf_d = scratch("xnTf", [D, S], F32)
    xnTb_d = scratch("xnTb", [D, S], BF16)
    qT_d = scratch("qT", [D, S], BF16)
    kT_d = scratch("kT", [D, S], BF16)
    s0_d = scratch("s0", [D, S], F32)
    x1_d = scratch("x1", [S, D], F32)
    xn1T_d = scratch("xn1T", [D, S], BF16)
    xq1T_d = scratch("xq1T", [D, S], BF16)
    s1_d = scratch("s1", [D, S], BF16)
    w1qT_d = scratch("w1qT", [D, DFF], BF16)
    w2qT_d = scratch("w2qT", [DFF, D], BF16)
    h_d = scratch("hblk", [DFF, S], BF16)
    wgqT_d = scratch("wgqT", [D, DI], BF16)
    wuqT_d = scratch("wuqT", [D, DI], BF16)
    wdqT_d = scratch("wdqT", [DI, D], BF16)

    es = ExitStack()
    sb = lambda name, shape, dt=F32: es.enter_context(nc.sbuf_tensor(name, shape, dt))
    psa = lambda name: es.enter_context(nc.psum_tensor(name, [128, 1024], F32))

    sidf = sb("sidf", [128, 128])
    sidb = sb("sidb", [128, 128], BF16)
    sid64 = sb("sid64", [128, 128])
    onecol_b = sb("onecol_b", [128, 1], BF16)
    onerow_b = sb("onerow_b", [1, 128], BF16)
    onerow_f = sb("onerow_f", [1, 128], F32)
    fillA = sb("fillA", [128, 256])
    fillB = sb("fillB", [128, 256])
    cols = {}
    for nm in ("l0_bq", "l0_bk", "l0_bo", "l0_gam", "l0_bet", "l0_b2",
               "l1_bq", "l1_bo", "l1_lng", "l1_lnb"):
        cols[nm] = sb("c_" + nm, [128, NDT])
    cols["l0_b1"] = sb("c_l0_b1", [128, NFT])
    rows = {}
    for nm in ("l0_bv", "l1_bk0", "l1_bk1", "l1_bv0", "l1_bv1"):
        rows[nm] = sb("r_" + nm, [1, D], BF16)
    smf = sb("smf", [128, 64])
    smr = sb("smr", [1, 3 * QC])

    v_sb = sb("v_sb", [128, NT * H * 65], BF16)
    vv = v_sb[:].rearrange("p (t h c) -> p t h c", h=H, c=65)

    def av2(off, n, dt, parts=128):
        size = {F32: 4, BF16: 2}[dt]
        assert off % 32 == 0 and (off + n * size) // 2 <= NT * H * 65, (off, n)
        return v_sb[0:parts, off // 2: off // 2 + (n * size) // 2].bitcast(dt)

    ARENA = 124928
    arena = sb("arena", [128, ARENA], U8)

    def av(off, n, dt, parts=128):
        size = {F32: 4, BF16: 2}[dt]
        assert off % 32 == 0 and off + n * size <= ARENA, (off, n)
        return arena[0:parts, off: off + n * size].bitcast(dt)

    P = [psa(f"ps{i}") for i in range(4)]
    Pb = [p[:].bitcast(BF16) for p in P]

    sems = {}
    for nm in ("s_in", "s_w", "s_sp", "s_pe", "s_ve", "s_ac", "s_out", "s_gp"):
        sems[nm] = es.enter_context(nc.semaphore(nm))
    s_in = sems["s_in"]; s_w = sems["s_w"]; s_sp = sems["s_sp"]
    s_pe = sems["s_pe"]; s_ve = sems["s_ve"]; s_ac = sems["s_ac"]
    s_out = sems["s_out"]; s_gp = sems["s_gp"]

    C = Ctx(nc)
    for s_ in sems.values():
        C.sem(s_)

    def preamble():
        C.dma("sync", sidf[:], idn_f, inc=s_in)
        C.dma("sync", sidb[:], idn_b, inc=s_in)
        for nm, src in (("l0_bq", l0_bq), ("l0_bk", l0_bk), ("l0_bo", l0_bo),
                        ("l0_gam", l0_gam), ("l0_bet", l0_bet), ("l0_b2", l0_b2),
                        ("l1_bq", l1_bq), ("l1_bo", l1_bo),
                        ("l1_lng", l1_lng), ("l1_lnb", l1_lnb)):
            for t in range(NDT):
                C.dma("sync", cols[nm][:, t:t + 1], src[t * 128:(t + 1) * 128], inc=s_in)
        for t in range(NFT):
            C.dma("sync", cols["l0_b1"][:, t:t + 1], l0_b1[t * 128:(t + 1) * 128], inc=s_in)
        C.ins("vector", lambda v: v.memset(fillB[:], 0.0))
        C.ins("vector", lambda v: v.memset(fillA[:], 0.0))
        C.ins("vector", lambda v: v.tensor_scalar(out=sid64[:], in0=sidf[:],
              scalar1=1.0 / 64.0, scalar2=None, op0=ALU.mult),
              waits=[(s_in, 32)])
        C.ins("vector", lambda v: v.memset(onecol_b[:], 1.0))
        C.ins("vector", lambda v: v.memset(onerow_b[:], 1.0))
        C.ins("vector", lambda v: v.memset(onerow_f[:], 1.0), inc=s_ve)
        rstage = av(ARENA - 4096, D, F32, parts=1)
        for nm, src in (("l0_bv", l0_bv), ("l1_bk0", l1_bk0), ("l1_bk1", l1_bk1),
                        ("l1_bv0", l1_bv0), ("l1_bv1", l1_bv1)):
            C.dma("sync", rstage, src, waits=[(s_ve, C.n(s_ve))], inc=s_in)
            scl = (1.0 / 64.0) if nm.startswith("l1") else 1.0
            C.ins("vector",
                  lambda v, nm=nm, scl=scl: v.tensor_scalar(
                      out=rows[nm][:], in0=rstage, scalar1=scl, scalar2=None,
                      op0=ALU.mult),
                  waits=[(s_in, C.n(s_in))], inc=s_ve)

    A_xtok, A_xn, A_sq = 0, 8192, 12320
    A_ring = 16544
    A_stgf, A_stgb = 33184, 41408
    A_qstg = 45536
    A_wst = 47616
    A_W = 55808

    def ringv(k, t0, nt):
        sl = t0 % 8
        assert sl + nt <= 8
        base = A_ring + k * 2048 + sl * 256
        return av(base - sl * 256, 8 * 128, BF16)[:, sl * 128:(sl + nt) * 128] \
            if False else arena[0:128, A_ring + k * 2048: A_ring + (k + 1) * 2048] \
            .bitcast(BF16)[:, sl * 128:(sl + nt) * 128]

    def phase_A():
        wq_v = lambda k: av(A_W, 8 * 1024, BF16)[:, k * 1024:(k + 1) * 1024]
        wk_v = lambda k: av(A_W + 16384, 8 * 1024, BF16)[:, k * 1024:(k + 1) * 1024]
        wv_v = lambda k: av(A_W + 32768, 8 * 1024, BF16)[:, k * 1024:(k + 1) * 1024]
        for wsrc, wdst in ((l0_wq, wq_v), (l0_wk, wk_v), (l0_wv, wv_v)):
            for t in range(NDT):
                stg = av(A_wst + (t % 2) * 4096, 1024, F32)
                C.dma("scalar", stg, wsrc[t * 128:(t + 1) * 128, :],
                      waits=[(s_ve, C.mem.get(("wst", t % 2), 0))], inc=s_w)
                C.ins("vector",
                      lambda v, t=t, stg=stg, wdst=wdst: v.tensor_copy(
                          out=wdst(t), in_=stg),
                      waits=[(s_w, C.n(s_w))], inc=s_ve)
                C.mem[("wst", t % 2)] = C.n(s_ve)
        nw_ve = C.n(s_ve)

        for t in range(NT + 1):
            b = (t % 2) * 8
            smc = lambda i, b=b: smf[:, b + i:b + i + 1]
            u = t - 1
            bu = (u % 2) * 8
            smu = lambda i, bu=bu: smf[:, bu + i:bu + i + 1]
            xt = av(A_xtok + (t % 2) * 4096, 1024, F32)
            xu = av(A_xtok + (u % 2) * 4096, 1024, F32)
            xn = av(A_xn, 1024, F32)
            sq = av(A_sq, 1024, F32)

            if t < NT:
                C.dma("sync", xt, x_d[t * 128:(t + 1) * 128, :],
                      waits=[(s_ve, C.mem.get(("napply", t - 2), 0))], inc=s_in)
                n_in = C.n(s_in)
                C.ins("vector", lambda v: v.tensor_tensor(out=sq, in0=xt, in1=xt, op=ALU.mult),
                      waits=[(s_in, n_in)])
                C.ins("vector", lambda v: v.reduce_sum(out=smc(0), in_=xt, axis=AX.X))
                C.ins("vector", lambda v: v.reduce_sum(out=smc(1), in_=sq, axis=AX.X))
                C.ins("vector", lambda v: v.tensor_scalar(out=smc(2), in0=smc(0),
                      scalar1=1.0 / D, scalar2=None, op0=ALU.mult))
            if t == 0:
                C.ins("vector", lambda v: v.memset(vv[:, :, :, 64:65], 1.0), inc=s_ve)
                C.ins("vector", lambda v: v.tensor_copy(out=av(A_stgf, 1024, F32), in_=xt))
            if t >= 1:
                C.ins("vector", lambda v: v.tensor_scalar(out=xn, in0=xu,
                      scalar1=smu(2), scalar2=smu(7), op0=ALU.subtract, op1=ALU.mult),
                      waits=[(s_pe, C.mem.get(("ntr", u - 1), 0))], inc=s_ve)
                C.mem[("napply", u)] = C.n(s_ve)
                napply = C.n(s_ve)
                for k in range(NDT):
                    C.ins("tensor", lambda te, k=k, u=u: te.transpose(
                        P[u % 2][:, k * 128:(k + 1) * 128],
                        in_=xn[:, k * 128:(k + 1) * 128], identity=sidf[:]),
                        waits=[(s_ve, napply), (s_ve, C.mem.get(("trp_war", u % 2), 0))],
                        inc=s_pe if k == NDT - 1 else None)
                C.mem[("ntr", u)] = C.n(s_pe)
            if t < NT:
                C.ins("vector", lambda v: v.tensor_scalar(out=smc(3), in0=smc(2),
                      scalar1=smc(2), scalar2=-1.0, op0=ALU.mult, op1=ALU.mult))
            if t == 0:
                C.ins("vector", lambda v: v.tensor_copy(out=av(A_stgf + 4096, 1024, F32), in_=sq))
            if t >= 1:
                stf = av(A_stgf + (u % 2) * 4096, 1024, F32)
                C.ins("vector", lambda v, u=u, stf=stf: v.tensor_copy(out=stf, in_=P[u % 2][:]),
                      waits=[(s_pe, C.mem[("ntr", u)]),
                             (s_sp, C.mem.get(("stg_war", u % 2), 0))])
            if t < NT:
                C.ins("vector", lambda v: v.tensor_scalar(out=smc(4), in0=smc(1),
                      scalar1=1.0 / D, scalar2=1e-5, op0=ALU.mult, op1=ALU.add))
            if t == 0:
                C.ins("vector", lambda v: v.tensor_copy(out=av(A_stgb, 1024, BF16), in_=xt))
            if t >= 1:
                stb = av(A_stgb + (u % 2) * 2048, 1024, BF16)
                C.ins("vector", lambda v, u=u, stb=stb: v.tensor_copy(out=stb, in_=P[u % 2][:]))
            if t < NT:
                C.ins("vector", lambda v: v.tensor_tensor(out=smc(5), in0=smc(4),
                      in1=smc(3), op=ALU.add), inc=s_ve)
                C.ins("scalar", lambda sc: sc.activation(out=smc(6), in_=smc(5),
                      func=AF.Sqrt), waits=[(s_ve, C.n(s_ve))], inc=s_ac)
            if t >= 1:
                ringdst = arena[0:128, A_ring: A_ring + 16384].bitcast(BF16) \
                    .rearrange("p (k c) -> p k c", c=1024)[:, :, (u % 8) * 128:(u % 8) * 128 + 128]
                C.ins("vector", lambda v, u=u, ringdst=ringdst: v.tensor_copy(
                      out=ringdst, in_=P[u % 2][:].rearrange("p (k c) -> p k c", c=128)),
                      waits=[(s_pe, C.mem.get(("qkv_done", u // 4 - 2), 0))],
                      inc=s_ve)
                C.mem[("trp_war", u % 2)] = C.n(s_ve)
                C.mem[("ring", u)] = C.n(s_ve)
                nvr = C.n(s_ve)
                stf = av(A_stgf + (u % 2) * 4096, 1024, F32)
                stb = av(A_stgb + (u % 2) * 2048, 1024, BF16)
                for k in range(NDT):
                    C.dma("gpsimd", xnTf_d[k * 128:(k + 1) * 128, u * 128:(u + 1) * 128],
                          stf[:, k * 128:(k + 1) * 128], waits=[(s_ve, nvr)], inc=s_sp)
                    C.dma("gpsimd", xnTb_d[k * 128:(k + 1) * 128, u * 128:(u + 1) * 128],
                          stb[:, k * 128:(k + 1) * 128], waits=[(s_ve, nvr)], inc=s_sp)
                C.mem[("stg_war", u % 2)] = C.n(s_sp)
            if t < NT:
                C.ins("vector", lambda v: v.reciprocal(out=smc(7), in_=smc(6)),
                      waits=[(s_ac, C.n(s_ac))])

            if t % 4 == 0 and t >= 4:
                q = t // 4 - 1
                for wi, (wfn, bias_nm, dst_d) in enumerate(
                        ((wq_v, "l0_bq", qT_d), (wk_v, "l0_bk", kT_d))):
                    for m in range(NDT):
                        pslot = P[2][:, (m % 2) * 512:(m % 2) * 512 + 512]
                        for k in range(NDT):
                            C.ins("tensor", lambda te, m=m, k=k, q=q, wfn=wfn, pslot=pslot:
                                  te.matmul(pslot,
                                            lhsT=wfn(k)[:, m * 128:(m + 1) * 128],
                                            rhs=ringv(k, q * 4, 4),
                                            start=(k == 0), stop=(k == NDT - 1)),
                                  waits=[(s_ve, max(nw_ve, C.mem[("ring", q * 4 + 3)])),
                                         (s_ve, C.mem.get(("qke", m % 2), 0))],
                                  inc=s_pe if k == NDT - 1 else None)
                        npe = C.n(s_pe)
                        stg = av(A_qstg + (m % 2) * 1024, 512, BF16)
                        C.ins("vector", lambda v, stg=stg, pslot=pslot, bias_nm=bias_nm, m=m:
                              v.tensor_scalar(out=stg, in0=pslot,
                                              scalar1=cols[bias_nm][:, m:m + 1],
                                              scalar2=None, op0=ALU.add),
                              waits=[(s_pe, npe),
                                     (s_sp, C.mem.get(("qstg_war", m % 2), 0))],
                              inc=s_ve)
                        C.mem[("qke", m % 2)] = C.n(s_ve)
                        nse = C.n(s_ve)
                        C.dma("gpsimd", dst_d[m * 128:(m + 1) * 128, q * 512:(q + 1) * 512],
                              stg, waits=[(s_ve, nse)], inc=s_sp)
                        C.mem[("qstg_war", m % 2)] = C.n(s_sp)
                for ts4 in range(4):
                    tile = q * 4 + ts4
                    for n2 in range(2):
                        pslot = P[3][:, (n2 % 2) * 512:(n2 % 2) * 512 + 512]
                        for k in range(NDT):
                            C.ins("tensor", lambda te, k=k, tile=tile, pslot=pslot, n2=n2:
                                  te.matmul(pslot,
                                            lhsT=ringv(k, tile, 1),
                                            rhs=wv_v(k)[:, n2 * 512:(n2 + 1) * 512],
                                            start=(k == 0), stop=False),
                                  waits=[(s_ve, max(nw_ve, C.mem[("ring", q * 4 + 3)])),
                                         (s_ve, C.mem.get(("ve_war", n2), 0))])
                        C.ins("tensor", lambda te, pslot=pslot, n2=n2:
                              te.matmul(pslot, lhsT=onerow_b[:],
                                        rhs=rows["l0_bv"][:, n2 * 512:(n2 + 1) * 512],
                                        start=False, stop=True),
                              inc=s_pe)
                        npe = C.n(s_pe)
                        for hh in range(8):
                            h = n2 * 8 + hh
                            C.ins("vector", lambda v, tile=tile, h=h, hh=hh, pslot=pslot:
                                  v.tensor_copy(out=vv[:, tile, h, 0:64],
                                                in_=pslot[:, hh * 64:(hh + 1) * 64]),
                                  waits=[(s_pe, npe)],
                                  inc=s_ve if hh == 7 else None)
                        C.mem[("ve_war", n2)] = C.n(s_ve)
                C.mem[("qkv_done", q)] = C.n(s_pe)

    T_k = 0
    T_q = 16384
    T_pt = 37888
    T_ctxu = 46080
    T_ctx = 47104
    T_wo = 55296
    T_xnf = 71680
    T_y = 88064
    T_yb = 104448
    T_s = 112640

    def phase_ATT():
        fence_pe = C.n(s_pe)
        fence_sp = C.n(s_sp)
        wo_v = lambda k: av(T_wo, 8 * 1024, BF16)[:, k * 1024:(k + 1) * 1024]
        for t in range(NDT):
            stg = av(T_xnf + (t % 2) * 4096, 1024, F32)
            C.dma("scalar", stg, l0_wo[t * 128:(t + 1) * 128, :],
                  waits=[(s_pe, fence_pe), (s_sp, fence_sp),
                         (s_ve, C.mem.get(("wos", t % 2), 0))], inc=s_w)
            C.ins("vector", lambda v, t=t, stg=stg: v.tensor_copy(out=wo_v(t), in_=stg),
                  waits=[(s_w, C.n(s_w)), (s_pe, fence_pe), (s_sp, fence_sp)],
                  inc=s_ve)
            C.mem[("wos", t % 2)] = C.n(s_ve)
        C.mem["xnf_war0"] = C.n(s_ve)

        n_spill_done = C.n(s_sp)
        nv0 = C.n(s_ve)
        C.mem[("p01_free",)] = nv0
        C.mem[("ctx_war", 0)] = nv0
        C.mem[("ctx_war", 1)] = nv0
        C.mem[("bc_war", 0)] = nv0
        C.mem[("bc_war", 1)] = nv0
        C.mem["xnf_war"] = (s_ve, C.n(s_ve))

        kslot = lambda i: av(T_k + i * 8192, 4096, BF16)
        qvs = lambda i: av(T_q + i * 8192, 8 * 512, BF16) \
            .rearrange("p (k c) -> p k c", c=512)

        U_xn = 33792
        U_w = 116736
        U_xnv = lambda i: av(U_xn, 8 * 256, BF16).rearrange(
            "p (k c) -> p k c", c=256)
        U_wv = lambda i: av(U_w + i * 4096, 8 * 256, BF16).rearrange(
            "p (k c) -> p k c", c=256)
        U_hstg = lambda i: av(32768 + i * 512, 256, BF16)
        hu = {"pos": 0}
        C.mem[("u_rw", 0)] = C.n(s_ve)
        C.mem[("u_rw", 1)] = C.n(s_ve)
        bp_fence = C.n(s_sp)

        def emit_hu(nmm):
            for _ in range(nmm):
                pos = hu["pos"]
                if pos >= 4096:
                    return
                unit, k = pos // 8, pos % 8
                tc, ft = unit // 32, unit % 32
                if k == 0:
                    if ft == 0:
                        xv = U_xnv(tc % 2)
                        for kk in range(NDT):
                            C.dma("sync", xv[:, kk, :],
                                  xnTb_d[kk * 128:(kk + 1) * 128,
                                         tc * 256:(tc + 1) * 256],
                                  waits=[(s_sp, bp_fence),
                                         (s_pe, C.mem.get(("u_xnw",), 0))],
                                  inc=s_in)
                        C.mem[("u_xni",)] = C.n(s_in)
                    if ft % 2 == 0:
                        wv_ = U_wv((ft // 2) % 2)
                        for kk in range(NDT):
                            C.dma("sync", wv_[:, kk, :],
                                  w1qT_d[kk * 128:(kk + 1) * 128,
                                         ft * 128:ft * 128 + 256],
                                  waits=[(s_sp, bp_fence),
                                         (s_pe, C.mem.get(("u_ww", (ft // 2) % 2), 0))],
                                  inc=s_in)
                        C.mem[("u_wi", (ft // 2) % 2)] = C.n(s_in)
                up = P[3][:, 0:256] if unit % 2 == 0 else P[3][:, 512:768]
                C.ins("tensor", lambda te, up=up, k=k, ft=ft, tc=tc:
                      te.matmul(up,
                                lhsT=U_wv((ft // 2) % 2)[:, k, (ft % 2) * 128:(ft % 2) * 128 + 128],
                                rhs=U_xnv(tc % 2)[:, k, :],
                                start=(k == 0), stop=(k == 7)),
                      waits=[(s_in, max(C.mem.get(("u_xni",), 0),
                                        C.mem.get(("u_wi", (ft // 2) % 2), 0))),
                             (s_ve, C.mem.get(("u_rw", unit % 2), 0))],
                      inc=s_pe if k == 7 else None)
                if k == 7:
                    nue = C.n(s_pe)
                    if ft == 31:
                        C.mem[("u_xnw",)] = nue
                    if ft % 2 == 1:
                        C.mem[("u_ww", (ft // 2) % 2)] = nue
                    stg = U_hstg(unit % 2)
                    C.ins("vector", lambda v, up=up, stg=stg, ft=ft: v.tensor_scalar(
                          out=stg, in0=up, scalar1=cols["l0_b1"][:, ft:ft + 1],
                          scalar2=0.0, op0=ALU.add, op1=ALU.max),
                          waits=[(s_pe, nue),
                                 (s_sp, C.mem.get(("u_hw", unit % 2), 0))],
                          inc=s_ve)
                    C.mem[("u_rw", unit % 2)] = C.n(s_ve)
                    nrl = C.n(s_ve)
                    C.dma("gpsimd", h_d[ft * 128:(ft + 1) * 128,
                                        tc * 256:(tc + 1) * 256], stg,
                          waits=[(s_ve, nrl)], inc=s_sp)
                    C.mem[("u_hw", unit % 2)] = C.n(s_sp)
                hu["pos"] = pos + 1

        def dma_K(h):
            hb = (h % 2) * 64
            sl = kslot(h % 2)[hb:hb + 64, :]
            C.dma("sync", sl, kT_d[h * 64:h * 64 + 64, :],
                  waits=[(s_sp, n_spill_done),
                         (s_pe, max(fence_pe, C.mem.get(("k_war", h % 2), 0)))],
                  inc=s_in)
            C.mem[("k_in", h)] = C.n(s_in)
            C.mem[("k_in2", h)] = C.n(s_in)

        for qc in range(NQC):
            qv = qvs(qc % 2)
            dma_K(0)
            dma_K(1)
            for h in range(H):
                hb = (h % 2) * 64
                C.dma("sync", qv[hb:hb + 64, h // 2, :],
                      qT_d[h * 64:(h + 1) * 64, qc * QC:(qc + 1) * QC],
                      waits=[(s_sp, n_spill_done),
                             (s_pe, max(fence_pe, C.mem.get(("q_war", qc % 2), 0)))],
                      inc=s_in)
                C.mem[("q_in", h)] = C.n(s_in)
            nq_in = C.n(s_in)
            xnf = av(T_xnf, 8 * 512, F32).rearrange("p (k c) -> p k c", c=512)
            wsem, wval = C.mem["xnf_war"]
            for k in range(NDT):
                C.dma("sync", xnf[:, k, :], xnTf_d[k * 128:(k + 1) * 128, qc * QC:(qc + 1) * QC],
                      waits=[(wsem, wval), (s_ve, C.mem.get("xnf_war0", 0)),
                             (s_pe, fence_pe)], inc=s_in)
            nxn_in = C.n(s_in)
            for h in range(H):
                def emit_pv(g, h=h):
                    ptg = av(T_pt + (g % 4) * 2048, 1024, BF16)
                    nac = C.mem[("exp_done", g % 4)]
                    for j in range(2):
                        sbk = g * 2 + j
                        C.ins("tensor", lambda te, sbk=sbk, j=j, g=g, h=h, ptg=ptg:
                              te.matmul(P[2][0:65, (h % 2) * 512:(h % 2) * 512 + 512],
                                        lhsT=vv[:, sbk, h, :],
                                        rhs=ptg[:, j * 512:(j + 1) * 512],
                                        start=(sbk == 0), stop=(sbk == 31)),
                              waits=[(s_ac, nac),
                                     (s_ve, C.mem.get(("ctx_war", h % 2), 0)),
                                     (s_ve, C.mem.get(("pv_war",), 0))],
                              inc=s_pe if sbk == 31 else None)
                    C.mem[("pt_war", g % 4)] = C.n(s_pe)

                for g in range(16):
                    pg = P[g % 2]
                    ptg = av(T_pt + (g % 4) * 2048, 1024, BF16)
                    for j in range(2):
                        sbk = g * 2 + j
                        hb = (h % 2) * 64
                        lhsT = kslot(h % 2)[hb:hb + 64, sbk * 128:(sbk + 1) * 128]
                        C.ins("tensor", lambda te, lhsT=lhsT, pg=pg, j=j, h=h:
                              te.matmul(pg[:, j * 512:(j + 1) * 512],
                                        lhsT=lhsT,
                                        rhs=qv[(h % 2) * 64:(h % 2) * 64 + 64, h // 2, :],
                                        start=True, stop=True),
                              waits=[(s_in, max(C.mem[("k_in", h)],
                                                C.mem[("q_in", h)])),
                                     (s_ac, C.mem.get(("sc_war", g % 2), 0)),
                                     (s_ve, C.mem.get(("p01_free",), 0))],
                              inc=s_pe if j == 1 else None)
                    npe = C.n(s_pe)
                    C.ins("scalar", lambda s, pg=pg, ptg=ptg: s.activation(
                          out=ptg, in_=pg[:], func=AF.Exp, scale=0.125),
                          waits=[(s_pe, npe), (s_pe, C.mem.get(("pt_war", g % 4), 0))],
                          inc=s_ac)
                    C.mem[("sc_war", g % 2)] = C.n(s_ac)
                    C.mem[("exp_done", g % 4)] = C.n(s_ac)
                    emit_hu(2)
                    if g >= 1:
                        emit_pv(g - 1)
                emit_pv(15)
                npv = C.n(s_pe)
                C.mem[("k_war", h % 2)] = npv
                if h + 2 < H:
                    dma_K(h + 2)
                C.ins("vector", lambda v, h=h: v.reciprocal(
                      out=smr[:, 0:QC], in_=P[2][64:65, (h % 2) * 512:(h % 2) * 512 + 512]),
                      waits=[(s_pe, npv)], inc=s_ve)
                nrc = C.n(s_ve)
                C.ins("tensor", lambda te, h=h: te.matmul(
                      P[2][64:128, (h % 2) * 512:(h % 2) * 512 + 512],
                      lhsT=onerow_f[:, 0:64], rhs=smr[:, 0:QC],
                      start=True, stop=True, skip_group_check=True,
                      tile_position=(0, 64)),
                      waits=[(s_ve, nrc), (s_ve, C.mem.get(("bc_war", h % 2), 0))],
                      inc=s_pe)
                nbc = C.n(s_pe)
                C.ins("vector", lambda v, h=h: v.tensor_copy(
                      out=av(T_ctxu, 512, BF16, parts=64),
                      in_=P[2][0:64, (h % 2) * 512:(h % 2) * 512 + 512]),
                      waits=[(s_pe, nbc)])
                C.ins("vector", lambda v, h=h: v.tensor_tensor(
                      out=av(T_ctx, 8 * 512, BF16)
                      .rearrange("p (k c) -> p k c", c=512)[(h % 2) * 64:(h % 2) * 64 + 64, h // 2, :],
                      in0=av(T_ctxu, 512, BF16, parts=64),
                      in1=P[2][64:128, (h % 2) * 512:(h % 2) * 512 + 512],
                      op=ALU.mult), inc=s_ve)
                C.mem[("ctx_war", h % 2)] = C.n(s_ve)
                C.mem[("bc_war", h % 2)] = C.n(s_ve)
            nctx = C.n(s_ve)

            ctxv = av(T_ctx, 8 * 512, BF16).rearrange("p (k c) -> p k c", c=512)
            yv = av(T_y, 8 * 512, F32).rearrange("p (k c) -> p k c", c=512)
            ybv = av(T_yb, 8 * 512, BF16).rearrange("p (k c) -> p k c", c=512)
            sv_ = av(T_s, 2 * 512, F32).rearrange("p (k c) -> p k c", c=512)
            for m in range(NDT):
                pm = P[2][:, (m % 2) * 512:(m % 2) * 512 + 512]
                for k in range(NDT):
                    C.ins("tensor", lambda te, m=m, k=k, pm=pm: te.matmul(
                          pm, lhsT=wo_v(k)[:, m * 128:(m + 1) * 128],
                          rhs=ctxv[:, k, :],
                          start=(k == 0), stop=(k == NDT - 1)),
                          waits=[(s_ve, nctx), (s_ve, C.mem.get(("att_war", m % 2), 0))],
                          inc=s_pe if k == NDT - 1 else None)
                npm = C.n(s_pe)
                C.ins("vector", lambda v, m=m, pm=pm: v.scalar_tensor_tensor(
                      out=yv[:, m, :], in0=pm, scalar=cols["l0_bo"][:, m:m + 1],
                      in1=xnf[:, m, :], op0=ALU.add, op1=ALU.add),
                      waits=[(s_pe, npm), (s_in, nxn_in)], inc=s_ve)
                C.mem[("att_war", m % 2)] = C.n(s_ve)
                C.ins("vector", lambda v, m=m: v.tensor_copy(out=ybv[:, m, :], in_=yv[:, m, :]),
                      inc=s_ve)
            ny = C.n(s_ve)
            y2v = av(T_xnf, 8 * 512, BF16).rearrange("p (k c) -> p k c", c=512)
            for m in range(NDT):
                C.ins("vector", lambda v, m=m: v.tensor_tensor(
                      out=y2v[:, m, :], in0=ybv[:, m, :], in1=ybv[:, m, :], op=ALU.mult),
                      inc=s_ve if m == NDT - 1 else None)
            ny2 = C.n(s_ve)
            for m in range(NDT):
                C.ins("tensor", lambda te, m=m: te.matmul(
                      P[0][0:1, 0:512], lhsT=onecol_b[:], rhs=ybv[:, m, :],
                      start=(m == 0), stop=(m == NDT - 1)),
                      waits=[(s_ve, ny), (s_ac, C.n(s_ac))],
                      inc=s_pe if m == NDT - 1 else None)
            for m in range(NDT):
                C.ins("tensor", lambda te, m=m: te.matmul(
                      P[0][0:1, 512:1024], lhsT=onecol_b[:], rhs=y2v[:, m, :],
                      start=(m == 0), stop=(m == NDT - 1)),
                      waits=[(s_ve, ny2)],
                      inc=s_pe if m == NDT - 1 else None)
            nst = C.n(s_pe)
            C.mem["xnf_war"] = (s_pe, nst)
            C.ins("vector", lambda v: v.tensor_scalar(
                  out=smr[:, QC:2 * QC], in0=P[0][0:1, 0:512],
                  scalar1=1.0 / D, scalar2=None, op0=ALU.mult),
                  waits=[(s_pe, nst)])
            C.ins("vector", lambda v: v.tensor_tensor(
                  out=smr[:, 0:QC], in0=smr[:, QC:2 * QC], in1=smr[:, QC:2 * QC],
                  op=ALU.mult))
            C.ins("vector", lambda v: v.tensor_scalar(
                  out=smr[:, 2 * QC:3 * QC], in0=P[0][0:1, 512:1024],
                  scalar1=1.0 / D, scalar2=None, op0=ALU.mult))
            C.ins("vector", lambda v: v.scalar_tensor_tensor(
                  out=smr[:, 2 * QC:3 * QC], in0=smr[:, 2 * QC:3 * QC],
                  scalar=1e-3, in1=smr[:, 0:QC],
                  op0=ALU.add, op1=ALU.subtract), inc=s_ve)
            C.mem[("pv_war",)] = C.n(s_ve)
            C.ins("scalar", lambda s: s.activation(
                  out=smr[:, 0:QC], in_=smr[:, 2 * QC:3 * QC], func=AF.Sqrt),
                  waits=[(s_ve, C.n(s_ve))], inc=s_ac)
            C.ins("vector", lambda v: v.reciprocal(
                  out=smr[:, 2 * QC:3 * QC], in_=smr[:, 0:QC]),
                  waits=[(s_ac, C.n(s_ac))], inc=s_ve)
            nrr = C.n(s_ve)
            C.ins("tensor", lambda te: te.matmul(P[0][:, 0:512], lhsT=onerow_f[:],
                  rhs=smr[:, QC:2 * QC], start=True, stop=True),
                  waits=[(s_ve, nrr), (s_ac, C.n(s_ac))],
                  inc=s_pe)
            C.ins("tensor", lambda te: te.matmul(P[1][:, 0:512], lhsT=onerow_f[:],
                  rhs=smr[:, 2 * QC:3 * QC], start=True, stop=True),
                  inc=s_pe)
            nbc2 = C.n(s_pe)
            for m in range(NDT):
                sl = sv_[:, m % 2, :]
                C.ins("vector", lambda v, m=m, sl=sl: v.tensor_tensor(
                      out=sl, in0=yv[:, m, :], in1=P[0][:, 0:512],
                      op=ALU.subtract),
                      waits=[(s_pe, nbc2), (s_sp, C.mem.get(("s_war", m % 2), 0))])
                C.ins("vector", lambda v, m=m, sl=sl: v.tensor_tensor(
                      out=sl, in0=sl, in1=P[1][:, 0:512],
                      op=ALU.mult))
                C.ins("vector", lambda v, m=m, sl=sl: v.tensor_scalar(
                      out=sl, in0=sl,
                      scalar1=cols["l0_gam"][:, m:m + 1],
                      scalar2=cols["l0_bet"][:, m:m + 1],
                      op0=ALU.mult, op1=ALU.add),
                      inc=s_ve)
                nsv = C.n(s_ve)
                C.dma("gpsimd", s0_d[m * 128:(m + 1) * 128, qc * QC:(qc + 1) * QC],
                      sl, waits=[(s_ve, nsv)], inc=s_sp)
                C.mem[("s_war", m % 2)] = C.n(s_sp)
            nsv = C.n(s_ve)
            C.mem[("q_war", qc % 2)] = npv
            C.mem[("p01_free",)] = nsv
            C.mem[("ctx_war", 0)] = ny
            C.mem[("ctx_war", 1)] = ny

    B_xn = 65536
    B_h = 69664
    B_g = 86048
    B_s = 94240
    B_x = 102432
    B_x1 = 110624
    B_sg = 114720

    def fillop():
        C.ins("vector", lambda v: v.tensor_copy(out=fillA[:], in_=fillB[:]))

    def phase_BP():
        fence_pe = C.n(s_pe)
        fence_ve = C.n(s_ve)
        fence_ac = C.n(s_ac)
        BP_stage = 65536
        BP_pos = 81920
        BP_neg = 90112
        BP_wqb = 98304

        def w_slices():
            yield ("w1", lambda t: l0_w1[t * 128:(t + 1) * 128, :], 32, 1024, 16)
            yield ("w2", lambda t: l0_w2[(t // 2) * 128:(t // 2 + 1) * 128,
                                         (t % 2) * 2048:(t % 2) * 2048 + 2048],
                   16, 2048, 48)

        for nm, srcf, ntile, fd, cb in w_slices():
            for t in range(ntile):
                stg = av(BP_stage + (t % 2) * 8192, fd, F32)
                C.dma("scalar", stg, srcf(t),
                      waits=[(s_pe, fence_pe),
                             (s_ve, C.mem.get(("bps", t % 2), fence_ve))], inc=s_w)
                C.ins("vector", lambda v, stg=stg, cb=cb, t=t: v.reduce_sum(
                      out=smf[:, cb + t:cb + t + 1], in_=stg, axis=AX.X,
                      apply_absolute_value=True),
                      waits=[(s_w, C.n(s_w)), (s_ac, fence_ac)], inc=s_ve)
                C.mem[("bps", t % 2)] = C.n(s_ve)
        fillop()
        C.ins("vector", lambda v: v.reduce_sum(out=smf[:, 15:16],
              in_=smf[:, 16:48], axis=AX.X))
        C.ins("vector", lambda v: v.reduce_sum(out=smf[:, 14:15],
              in_=smf[:, 48:64], axis=AX.X), inc=s_ve)
        nacc = C.n(s_ve)
        C.ins("tensor", lambda te: te.transpose(P[0][0:1, 0:128],
              in_=smf[:, 15:16], identity=sidf[:]),
              waits=[(s_ve, nacc)])
        C.ins("tensor", lambda te: te.transpose(P[0][0:1, 128:256],
              in_=smf[:, 14:15], identity=sidf[:]), inc=s_pe)
        ntp = C.n(s_pe)
        C.ins("vector", lambda v: v.reduce_sum(out=smr[:, 0:1],
              in_=P[0][0:1, 0:128], axis=AX.X), waits=[(s_pe, ntp)])
        C.ins("vector", lambda v: v.reduce_sum(out=smr[:, 1:2],
              in_=P[0][0:1, 128:256], axis=AX.X), inc=s_ve)
        ntot = C.n(s_ve)
        C.ins("scalar", lambda sc: sc.copy(out=smr[:, 2:3], in_=smr[:, 0:1]),
              waits=[(s_ve, ntot)])
        C.ins("scalar", lambda sc: sc.copy(out=smr[:, 3:4], in_=smr[:, 1:2]),
              inc=s_ac)
        C.ins("vector", lambda v: v.tensor_scalar(out=smr[:, 4:5], in0=smr[:, 2:3],
              scalar1=1.0 / (DFF * D), scalar2=1e-5, op0=ALU.mult, op1=ALU.add),
              waits=[(s_ac, C.n(s_ac))])
        C.ins("vector", lambda v: v.tensor_scalar(out=smr[:, 5:6], in0=smr[:, 3:4],
              scalar1=1.0 / (DFF * D), scalar2=1e-5, op0=ALU.mult, op1=ALU.add),
              inc=s_ve)
        C.ins("scalar", lambda sc: sc.copy(out=smr[:, 6:7], in_=smr[:, 4:5]),
              waits=[(s_ve, C.n(s_ve))])
        C.ins("scalar", lambda sc: sc.copy(out=smr[:, 7:8], in_=smr[:, 5:6]),
              inc=s_ac)
        C.ins("vector", lambda v: v.reciprocal(out=smr[:, 8:9], in_=smr[:, 6:7]),
              waits=[(s_ac, C.n(s_ac))])
        C.ins("vector", lambda v: v.reciprocal(out=smr[:, 9:10], in_=smr[:, 7:8]),
              inc=s_ve)
        nrg = C.n(s_ve)
        C.ins("tensor", lambda te: te.matmul(P[0][:, 0:1], lhsT=onerow_f[:],
              rhs=smr[:, 8:9], start=True, stop=True), waits=[(s_ve, nrg)])
        C.ins("tensor", lambda te: te.matmul(P[0][:, 1:2], lhsT=onerow_f[:],
              rhs=smr[:, 9:10], start=True, stop=True), inc=s_pe)
        nbg = C.n(s_pe)
        C.ins("vector", lambda v: v.tensor_copy(out=smf[:, 12:14], in_=P[0][:, 0:2]),
              waits=[(s_pe, nbg)], inc=s_ve)
        nrgc = C.n(s_ve)

        for nm, srcf, ntile, fd, cb in w_slices():
            rg = smf[:, 12:13] if nm == "w1" else smf[:, 13:14]
            for t in range(ntile):
                stg = av(BP_stage + (t % 2) * 8192, fd, F32)
                C.dma("scalar", stg, srcf(t),
                      waits=[(s_ve, C.mem.get(("bps2", t % 2), nrgc))], inc=s_w)
                nst = C.n(s_w)
                pos = av(BP_pos, fd, F32)
                neg = av(BP_neg, fd, F32)
                wqb = av(BP_wqb + (t % 2) * 4096, fd, BF16)
                C.ins("vector", lambda v, stg=stg, pos=pos, rg=rg: v.tensor_scalar(
                      out=pos, in0=stg, scalar1=rg, scalar2=0.5,
                      op0=ALU.mult, op1=ALU.is_gt), waits=[(s_w, nst)])
                C.ins("vector", lambda v, stg=stg, neg=neg, rg=rg: v.tensor_scalar(
                      out=neg, in0=stg, scalar1=rg, scalar2=-0.5,
                      op0=ALU.mult, op1=ALU.is_lt))
                C.ins("vector", lambda v, pos=pos, neg=neg, wqb=wqb: v.tensor_tensor(
                      out=wqb, in0=pos, in1=neg, op=ALU.subtract),
                      waits=[(s_pe, C.mem.get(("bpt", t % 2), 0))], inc=s_ve)
                C.mem[("bps2", t % 2)] = C.n(s_ve)
                nwq = C.n(s_ve)
                ntr = fd // 128
                for j in range(ntr):
                    C.ins("tensor", lambda te, j=j, t=t, wqb=wqb: te.transpose(
                          Pb[2 + (t % 2)][:, (j % 8) * 128:(j % 8) * 128 + 128],
                          in_=wqb[:, j * 128:(j + 1) * 128], identity=sidb[:]),
                          waits=[(s_ve, nwq),
                                 (s_ve, C.mem.get(("bpe", t % 2), 0))],
                          inc=s_pe if (j % 8 == 7 or j == ntr - 1) else None)
                    if j % 8 == 7 or j == ntr - 1:
                        npe = C.n(s_pe)
                        j0 = (j // 8) * 8
                        nj = j - j0 + 1
                        stg2 = av(BP_wqb + 8192 + (t % 2) * 2048, 1024, BF16)
                        C.ins("vector", lambda v, stg2=stg2, nj=nj, t=t: v.tensor_copy(
                              out=stg2[:, 0:nj * 128], in_=Pb[2 + (t % 2)][:, 0:nj * 128]),
                              waits=[(s_pe, npe),
                                     (s_sp, C.mem.get(("bpw", t % 2), 0))], inc=s_ve)
                        C.mem[("bpe", t % 2)] = C.n(s_ve)
                        nbe = C.n(s_ve)
                        for jj in range(nj):
                            if nm == "w1":
                                dst = w1qT_d[(j0 + jj) * 128:(j0 + jj + 1) * 128,
                                             t * 128:(t + 1) * 128]
                            else:
                                fi = (t % 2) * 16 + j0 + jj
                                dst = w2qT_d[fi * 128:(fi + 1) * 128,
                                             (t // 2) * 128:(t // 2) * 128 + 128]
                            C.dma("gpsimd", dst, stg2[:, jj * 128:(jj + 1) * 128],
                                  waits=[(s_ve, nbe)], inc=s_sp)
                        C.mem[("bpw", t % 2)] = C.n(s_sp)
                C.mem[("bpt", t % 2)] = C.n(s_pe)

    def phase_B():
        fence_pe = C.n(s_pe)
        fence_ve = C.n(s_ve)
        fence_sp = C.n(s_sp)
        w2qT = av(0, 32 * 1024, BF16).rearrange("p (j d) -> p j d", d=1024)
        for j in range(NFT):
            C.dma("sync", w2qT[:, j, :], w2qT_d[j * 128:(j + 1) * 128, :],
                  waits=[(s_pe, fence_pe), (s_ve, fence_ve), (s_sp, fence_sp)],
                  inc=s_in)
        nw2 = C.n(s_in)
        for tc in range(16):
            hv = av(B_h, 32 * 256, BF16).rearrange("p (f c) -> p f c", c=256)
            for j in range(NFT):
                C.dma("scalar", hv[:, j, :],
                      h_d[j * 128:(j + 1) * 128, tc * 256:(tc + 1) * 256],
                      waits=[(s_sp, fence_sp), (s_pe, C.mem.get(("bh_done",), 0))],
                      inc=s_w)
            nhv = C.n(s_w)
            sc_ = av(B_s, 8 * 256, F32).rearrange("p (k c) -> p k c", c=256)
            for k in range(NDT):
                C.dma("gpsimd", sc_[:, k, :],
                      s0_d[k * 128:(k + 1) * 128, tc * 256:(tc + 1) * 256],
                      waits=[(s_ve, C.mem.get(("bs_war",), fence_ve))], inc=s_sp)
            nsc = C.n(s_sp)
            gv = av(B_g, 8 * 256, F32).rearrange("p (k c) -> p k c", c=256)
            for m in range(NDT):
                pslot = P[1][:, (m % 2) * 512:(m % 2) * 512 + 256]
                for j in range(NFT):
                    C.ins("tensor", lambda te, m=m, j=j, pslot=pslot:
                          te.matmul(pslot, lhsT=w2qT[:, j, m * 128:(m + 1) * 128],
                                    rhs=hv[:, j, :], start=(j == 0), stop=(j == NFT - 1)),
                          waits=[(s_in, nw2), (s_w, nhv),
                                 (s_ac, C.mem.get(("bgo", m % 2), 0))],
                          inc=s_pe if j == NFT - 1 else None)
                ng = C.n(s_pe)
                C.ins("scalar", lambda sc2, m=m, pslot=pslot: sc2.activation(
                      out=gv[:, m, :], in_=pslot, func=AF.Silu,
                      bias=cols["l0_b2"][:, m:m + 1]),
                      waits=[(s_pe, ng), (s_ve, C.mem.get(("bg_war",), 0))], inc=s_ac)
                C.mem[("bgo", m % 2)] = C.n(s_ac)
            C.mem[("bh_done",)] = C.n(s_pe)
            ngd = C.n(s_ac)
            sgv = av(B_sg, 8 * 256, F32).rearrange("p (k c) -> p k c", c=256)
            for m in range(NDT):
                C.ins("vector", lambda v, m=m: v.tensor_tensor(
                      out=sgv[:, m, :], in0=gv[:, m, :], in1=sc_[:, m, :], op=ALU.mult),
                      waits=[(s_ac, ngd), (s_sp, nsc),
                             (s_pe, C.mem.get(("bsg_pe",), 0))],
                      inc=s_ve if m == NDT - 1 else None)
            C.mem[("bs_war",)] = C.n(s_ve)
            C.mem[("bg_war",)] = C.n(s_ve)
            nsg = C.n(s_ve)
            for pt in range(2):
                xbuf = av(B_x + pt * 4096, 1024, F32)
                C.dma("sync", xbuf, x_d[tc * 256 + pt * 128: tc * 256 + pt * 128 + 128, :],
                      waits=[(s_ve, C.mem.get(("bx_war", pt), 0))], inc=s_in)
                nxb = C.n(s_in)
                for m in range(NDT):
                    C.ins("tensor", lambda te, m=m, pt=pt: te.transpose(
                          P[3][:, m * 128:(m + 1) * 128],
                          in_=sgv[:, m, pt * 128:pt * 128 + 128], identity=sidf[:]),
                          waits=[(s_ve, nsg), (s_ve, C.mem.get(("bx1_done",), 0))],
                          inc=s_pe if m == NDT - 1 else None)
                ntp = C.n(s_pe)
                C.mem[("bsg_pe",)] = ntp
                x1b = av(B_x1, 1024, F32)
                C.ins("vector", lambda v, pt=pt, xbuf=xbuf, x1b=x1b: v.tensor_tensor(
                      out=x1b, in0=xbuf, in1=P[3][:], op=ALU.add),
                      waits=[(s_pe, ntp), (s_in, nxb),
                             (s_sp, C.mem.get(("bx1_war",), 0))], inc=s_ve)
                C.mem[("bx_war", pt)] = C.n(s_ve)
                C.mem[("bx1_done",)] = C.n(s_ve)
                nx1 = C.n(s_ve)
                C.dma("gpsimd", x1_d[tc * 256 + pt * 128: tc * 256 + pt * 128 + 128, :],
                      x1b, waits=[(s_ve, nx1)], inc=s_sp)
                C.mem[("bx1_war",)] = C.n(s_sp)

    def emit_fp4(src, dst_bf, W, bu, sep=None, pre_done=False, dst_waits=()):
        nb = W // 16
        sep = sep or fillop
        if not pre_done:
            C.ins("vector", lambda v: v.reduce_max(
                  out=bu["bm"], in_=src.rearrange("p (b k) -> p b k", k=16),
                  axis=AX.X, apply_absolute_value=True))
            sep()
            C.ins("vector", lambda v: v.tensor_scalar(out=bu["sc"], in0=bu["bm"],
                  scalar1=1.0 / 6.0, scalar2=1e-12, op0=ALU.mult, op1=ALU.max))
            sep()
            C.ins("vector", lambda v: v.reciprocal(out=bu["rsc"], in_=bu["sc"]))
            sep()
        C.ins("vector", lambda v: v.tensor_copy(out=bu["scb"], in_=bu["sc"]))
        rscb = bu["rsc"].rearrange("p (b o) -> p b o", o=1).broadcast_to([128, nb, 16])
        scb = bu["scb"].rearrange("p (b o) -> p b o", o=1).broadcast_to([128, nb, 16])
        src3 = src.rearrange("p (b k) -> p b k", k=16)
        zz3 = bu["zz"].rearrange("p (b k) -> p b k", k=16)
        C.ins("vector", lambda v: v.tensor_tensor(out=zz3, in0=src3, in1=rscb,
              op=ALU.mult), waits=[(s_ac, C.n(s_ac))], inc=s_ve)
        nz = C.n(s_ve)
        C.ins("scalar", lambda sc2: sc2.activation(out=bu["aa"], in_=bu["zz"],
              func=AF.Abs), waits=[(s_ve, nz)])
        C.ins("scalar", lambda sc2: sc2.activation(out=bu["sgn"], in_=bu["zz"],
              func=AF.Sign), inc=s_ac)
        na = C.n(s_ac)
        aa, T1, T2, T3 = bu["aa"], bu["T1"], bu["T2"], bu["T3"]
        C.ins("vector", lambda v: v.tensor_scalar(out=T1, in0=aa, scalar1=0.25,
              scalar2=None, op0=ALU.is_gt), waits=[(s_ac, na)])
        C.ins("vector", lambda v: v.scalar_tensor_tensor(out=T2, in0=aa,
              scalar=0.75, in1=T1, op0=ALU.is_gt, op1=ALU.add))
        C.ins("vector", lambda v: v.scalar_tensor_tensor(out=T1, in0=aa,
              scalar=1.25, in1=T2, op0=ALU.is_gt, op1=ALU.add))
        C.ins("vector", lambda v: v.scalar_tensor_tensor(out=T2, in0=aa,
              scalar=1.75, in1=T1, op0=ALU.is_gt, op1=ALU.add))
        C.ins("vector", lambda v: v.tensor_scalar(out=T1, in0=aa, scalar1=2.5,
              scalar2=None, op0=ALU.is_gt))
        C.ins("vector", lambda v: v.scalar_tensor_tensor(out=T3, in0=aa,
              scalar=3.5, in1=T1, op0=ALU.is_gt, op1=ALU.add))
        C.ins("vector", lambda v: v.tensor_scalar(out=T1, in0=aa, scalar1=5.0,
              scalar2=2.0, op0=ALU.is_gt, op1=ALU.mult))
        C.ins("vector", lambda v: v.scalar_tensor_tensor(out=T2, in0=T2,
              scalar=0.5, in1=T3, op0=ALU.mult, op1=ALU.add))
        C.ins("vector", lambda v: v.tensor_tensor(out=T3, in0=T2, in1=T1,
              op=ALU.add))
        C.ins("vector", lambda v: v.tensor_tensor(out=T1, in0=T3, in1=bu["sgn"],
              op=ALU.mult))
        dst3 = dst_bf.rearrange("p (b k) -> p b k", k=16)
        t13 = T1.rearrange("p (b k) -> p b k", k=16)
        return C.ins("vector", lambda v: v.tensor_tensor(out=dst3, in0=t13,
              in1=scb, op=ALU.mult), waits=list(dst_waits), inc=s_ve)

    CP_x = 0
    CP_xn1 = 8192
    CP_sq = 12320
    CP_xnb = 16544
    CP_stg = 18592
    CP_q = 22688
    CP_qs = 47296
    CP_xq = 48128
    CM_wa = 51200
    CM_wb = 67584
    CM_xn = 83968
    CM_ks = 100352
    CM_vs = 108544
    CM_st0 = 116736
    CM_state = 120832
    CF_q = 0
    CF_out = 8192
    CF_t = 16384
    CF_s1 = 32768
    C3_stg = 0
    C3_q = 24576
    C3_wqb = 92160
    C3_qs = 103424
    I3_stg = 0
    I3_wqb = 8192
    I3_q = 12288
    I3_qs = 24576
    I3_srcb = 25600
    I3_stg3 = 27648

    def phase_C():
        fence_pe = C.n(s_pe)
        fence_sp = C.n(s_sp)
        fence_ve = C.n(s_ve)
        fence_ac = C.n(s_ac)
        wa = av(CM_wa, 8 * 1024, BF16).rearrange("p (k c) -> p k c", c=1024)
        wb = av(CM_wb, 8 * 1024, BF16).rearrange("p (k c) -> p k c", c=1024)
        nc1_ve = C.n(s_ve)
        nc1_pe = C.n(s_pe)
        nc1_ac = C.n(s_ac)
        for wi_ in range(2):
            for half_ in range(2):
                C.mem[("c_ke", wi_, half_)] = nc1_ve
        C.mem[("c_sacc",)] = nc1_ve

        def load_l1_w(dst, src, scale, with_I):
            for k in range(NDT):
                stg = av2(CF_t + (k % 2) * 4096, 1024, F32)
                C.dma("scalar", stg, src[k * 128:(k + 1) * 128, :],
                      waits=[(s_ve, C.mem.get(("c_lw", k % 2), nc1_ve)),
                             (s_pe, C.mem.get(("c_lwp",), nc1_pe))], inc=s_w)
                nw = C.n(s_w)
                C.ins("vector", lambda v, stg=stg, dst=dst, k=k, scale=scale:
                      v.tensor_scalar(out=dst[:, k, :], in0=stg, scalar1=scale,
                                      scalar2=None, op0=ALU.mult),
                      waits=[(s_w, nw), (s_pe, C.mem.get(("c_lwp",), nc1_pe))])
                if with_I:
                    C.ins("vector", lambda v, stg=stg, dst=dst, k=k, scale=scale:
                          v.scalar_tensor_tensor(
                              out=dst[:, k, k * 128:(k + 1) * 128],
                              in0=stg[:, k * 128:(k + 1) * 128], scalar=scale,
                              op0=ALU.mult, op1=ALU.add,
                              in1=sidf[:] if scale == 1.0 else sid64[:]))
                C.ins("vector", lambda v:
                      v.tensor_copy(out=fillA[:, 0:1], in_=fillB[:, 0:1]), inc=s_ve)
                C.mem[("c_lw", k % 2)] = C.n(s_ve)

        xncv = lambda i: av(CM_xn + i * 8192, 8 * 512, BF16).rearrange(
            "p (k c) -> p k c", c=512)
        ksv = av(CM_ks, 4 * 1024, BF16).rearrange("p (a c) -> p a c", c=1024)
        vsv = av(CM_vs, 4 * 1024, BF16).rearrange("p (a c) -> p a c", c=1024)

        def c2_dma(ch):
            xnc = xncv(ch % 2)
            for k in range(NDT):
                C.dma("scalar", xnc[:, k, :],
                      xn1T_d[k * 128:(k + 1) * 128, ch * 512:(ch + 1) * 512],
                      waits=[(s_sp, C.n(s_sp)),
                             (s_pe, C.mem.get(("c_xw", ch % 2), 0))], inc=s_w)
            C.mem[("c_xin", ch % 2)] = C.n(s_w)

        def c2_compute(ch, brow, vrow, stk):
            xnc = xncv(ch % 2)
            nlw = C.mem.get(("c_nlw",), 0)
            nxi = C.mem[("c_xin", ch % 2)]
            for sub in range(4):
                for half in range(2):
                    for wi, (wv_, dstv, bro) in enumerate(
                            ((wa, ksv, brow), (wb, vsv, vrow))):
                        pslot = P[wi][:, (half % 2) * 512:(half % 2) * 512 + 512]
                        for k in range(NDT):
                            C.ins("tensor", lambda te, k=k, sub=sub, half=half,
                                  wv_=wv_, pslot=pslot, xnc=xnc:
                                  te.matmul(pslot,
                                            lhsT=xnc[:, k, sub * 128:(sub + 1) * 128],
                                            rhs=wv_[:, k, half * 512:(half + 1) * 512],
                                            start=(k == 0), stop=False),
                                  waits=[(s_ve, nlw), (s_w, nxi),
                                         (s_ac, nc1_ac),
                                         (s_ve, C.mem.get(("c_ke", wi, half), 0))])
                        C.ins("tensor", lambda te, pslot=pslot, bro=bro, half=half:
                              te.matmul(pslot, lhsT=onerow_b[:],
                                        rhs=bro[:, half * 512:(half + 1) * 512],
                                        start=False, stop=True),
                              inc=s_pe)
                        npk = C.n(s_pe)
                        C.ins("vector", lambda v, dstv=dstv, sub=sub, half=half,
                              pslot=pslot: v.tensor_copy(
                                  out=dstv[:, sub, half * 512:(half + 1) * 512],
                                  in_=pslot),
                              waits=[(s_pe, npk)], inc=s_ve)
                        C.mem[("c_ke", wi, half)] = C.n(s_ve)
                C.mem[("c_nkv", sub)] = C.n(s_ve)
            pst = P[2]
            for h in range(H):
                for sub in range(4):
                    C.ins("tensor", lambda te, h=h, sub=sub, pst=pst:
                          te.matmul(pst[0:64, h * 64:(h + 1) * 64],
                                    lhsT=ksv[:, sub, h * 64:(h + 1) * 64],
                                    rhs=vsv[:, sub, h * 64:(h + 1) * 64],
                                    start=(sub == 0), stop=(sub == 3)),
                          waits=[(s_ve, C.mem[("c_nkv", 3)]),
                                 (s_ve, C.mem.get(("c_sacc",), 0))],
                          inc=s_pe if h == H - 1 else None)
            nsp = C.n(s_pe)
            if ch == 0:
                C.ins("vector", lambda v, pst=pst, stk=stk: v.tensor_copy(
                      out=stk, in_=pst[0:64, :]), waits=[(s_pe, nsp)], inc=s_ve)
            else:
                C.ins("vector", lambda v, pst=pst, stk=stk: v.tensor_tensor(
                      out=stk, in0=pst[0:64, :], in1=stk, op=ALU.add),
                      waits=[(s_pe, nsp)], inc=s_ve)
            C.mem[("c_sacc",)] = C.n(s_ve)
            C.mem[("c_xw", ch % 2)] = C.n(s_pe)

        def emit_c2_chunk(ch, brow, vrow, stk):
            c2_dma(ch)
            c2_compute(ch, brow, vrow, stk)

        qi3 = {k: av(I3_q + i * 2048, 1024, BF16) for i, k in
               enumerate(("zz", "aa", "sgn", "T1", "T2", "T3"))}
        qi3s = {"bm": av(I3_qs, 64, F32), "sc": av(I3_qs + 256, 64, F32),
                "rsc": av(I3_qs + 512, 64, F32), "scb": av(I3_qs + 768, 64, BF16)}
        i3_srcb = av(I3_srcb, 1024, BF16)
        i3 = {"d": 0, "q": 0, "t": 0}
        NI3 = 44

        def c3i_step(n):
            for _ in range(n):
                u = i3["d"]
                if u < NI3:
                    wsrc = l1_wg if u < 22 else l1_wu
                    t3 = u % 22
                    stg = av(I3_stg + (u % 2) * 4096, 1024, F32)
                    C.dma("scalar", stg, wsrc[t3 * 128:(t3 + 1) * 128, :],
                          waits=[(s_ve, C.mem.get(("i3s", u % 2), 0))], inc=s_w)
                    C.mem[("i3d", u % 2)] = C.n(s_w)
                    i3["d"] += 1
                u = i3["q"]
                if u < NI3 and (u <= i3["d"] - 2 or i3["d"] == NI3):
                    stg = av(I3_stg + (u % 2) * 4096, 1024, F32)
                    wqb = av(I3_wqb + (u % 2) * 2048, 1024, BF16)
                    C.ins("vector", lambda v, stg=stg: v.tensor_copy(
                          out=i3_srcb, in_=stg),
                          waits=[(s_w, C.mem[("i3d", u % 2)]), (s_ac, C.n(s_ac))])
                    emit_fp4(i3_srcb, wqb, 1024, {**qi3, **qi3s},
                             sep=lambda: C.ins("vector", lambda v: v.tensor_copy(
                                 out=fillA[:], in_=fillB[:])),
                             dst_waits=[(s_ve, C.mem.get(("i3e", u % 2), 0))])
                    C.mem[("i3s", u % 2)] = C.n(s_ve)
                    C.mem[("i3q", u % 2)] = C.n(s_ve)
                    i3["q"] += 1
                u = i3["t"]
                if u < NI3 and (u <= i3["q"] - 2 or i3["q"] == NI3):
                    t3 = u % 22
                    wdst_d = wgqT_d if u < 22 else wuqT_d
                    wqb = av(I3_wqb + (u % 2) * 2048, 1024, BF16)
                    for j in range(8):
                        C.ins("tensor", lambda te, j=j, wqb=wqb: te.transpose(
                              Pb[3][:, 1024 + j * 128:1024 + (j + 1) * 128],
                              in_=wqb[:, j * 128:(j + 1) * 128], identity=sidb[:]),
                              waits=[(s_ve, C.mem[("i3q", u % 2)]),
                                     (s_ve, C.mem.get(("i3ev",), 0))],
                              inc=s_pe if j == 7 else None)
                    npe3 = C.n(s_pe)
                    stg3i = av(I3_stg3, 1024, BF16)
                    C.ins("vector", lambda v, stg3i=stg3i: v.tensor_copy(
                          out=stg3i, in_=Pb[3][:, 1024:2048]),
                          waits=[(s_pe, npe3), (s_sp, C.mem.get(("i3w",), 0))],
                          inc=s_ve)
                    C.mem[("i3e", u % 2)] = C.n(s_ve)
                    C.mem[("i3ev",)] = C.n(s_ve)
                    nev3 = C.n(s_ve)
                    for j in range(8):
                        C.dma("gpsimd",
                              wdst_d[j * 128:(j + 1) * 128, t3 * 128:(t3 + 1) * 128],
                              stg3i[:, j * 128:(j + 1) * 128],
                              waits=[(s_ve, nev3)], inc=s_sp)
                    C.mem[("i3w",)] = C.n(s_sp)
                    i3["t"] += 1

        C.mem[("c_lwp",)] = C.n(s_pe)
        load_l1_w(wa, l1_wk0, 1.0 / 64.0, True)
        load_l1_w(wb, l1_wv0, 1.0 / 64.0, True)
        C.mem[("c_nlw",)] = C.n(s_ve)

        qb = {k: av(CP_q + i * 2048, 1024, BF16) for i, k in
              enumerate(("zz", "aa", "sgn", "T1", "T2", "T3"))}
        qs = {k: av(CP_qs + i * 256, 64, F32) for i, k in
              enumerate(("bm", "sc", "rsc"))}
        qs["scb"] = av(CP_qs + 768, 64, BF16)
        for t in range(NT + 1):
            b = (t % 2) * 8
            smc = lambda i, b=b: smf[:, b + i:b + i + 1]
            u = t - 1
            bu_ = (u % 2) * 8
            smu = lambda i, bu_=bu_: smf[:, bu_ + i:bu_ + i + 1]
            xt = av(CP_x + (t % 2) * 4096, 1024, F32)
            xu = av(CP_x + (u % 2) * 4096, 1024, F32)
            xn1 = av(CP_xn1, 1024, F32)
            sq = av(CP_sq, 1024, F32)
            xnb = av(CP_xnb, 1024, BF16)
            xq = av(CP_xq, 1024, BF16)
            if t < NT:
                C.dma("sync", xt, x1_d[t * 128:(t + 1) * 128, :],
                      waits=[(s_ve, C.mem.get(("c_napply", t - 2), fence_ve)),
                             (s_sp, C.mem.get(("c_x1w", t), fence_sp))], inc=s_in)
                n_in = C.n(s_in)
                C.ins("vector", lambda v: v.tensor_tensor(out=sq, in0=xt, in1=xt,
                      op=ALU.mult), waits=[(s_in, n_in), (s_pe, fence_pe),
                                           (s_ac, fence_ac)])
                C.ins("vector", lambda v: v.reduce_sum(out=smc(0), in_=xt, axis=AX.X))
                C.ins("vector", lambda v: v.reduce_sum(out=smc(1), in_=sq, axis=AX.X))
                C.ins("vector", lambda v: v.tensor_scalar(out=smc(2), in0=smc(0),
                      scalar1=1.0 / D, scalar2=None, op0=ALU.mult))
            if t == 0:
                C.ins("vector", lambda v: v.tensor_copy(out=av(CP_xn1, 1024, F32), in_=xt))
            if t >= 1:
                C.ins("vector", lambda v: v.tensor_scalar(out=xn1, in0=xu,
                      scalar1=smu(2), scalar2=smu(7), op0=ALU.subtract, op1=ALU.mult),
                      waits=[(s_ac, C.mem.get(("c_quant_ac", u - 1), 0))], inc=s_ve)
                C.mem[("c_napply", u)] = C.n(s_ve)
            if t < NT:
                C.ins("vector", lambda v: v.tensor_scalar(out=smc(3), in0=smc(2),
                      scalar1=smc(2), scalar2=-1.0, op0=ALU.mult, op1=ALU.mult))
            if t >= 1:
                C.ins("vector", lambda v: v.tensor_copy(out=xnb, in_=xn1), inc=s_ve)
                nxnb = C.n(s_ve)
                C.ins("vector", lambda v: v.reduce_max(
                      out=qs["bm"], in_=xnb.rearrange("p (b k) -> p b k", k=16),
                      axis=AX.X, apply_absolute_value=True))
            if t == 0:
                C.ins("vector", lambda v: v.tensor_copy(out=av(CP_sq, 1024, F32), in_=xt))
            if t < NT:
                C.ins("vector", lambda v: v.tensor_scalar(out=smc(4), in0=smc(1),
                      scalar1=1.0 / D, scalar2=1e-5, op0=ALU.mult, op1=ALU.add))
            if t == 0:
                C.ins("vector", lambda v: v.tensor_copy(out=av(CP_xnb, 1024, BF16), in_=xt))
            if t >= 1:
                for k in range(NDT):
                    C.ins("tensor", lambda te, k=k: te.transpose(
                          Pb[3][:, k * 128:(k + 1) * 128],
                          in_=xnb[:, k * 128:(k + 1) * 128], identity=sidb[:]),
                          waits=[(s_ve, nxnb), (s_ve, C.mem.get(("c_tw", 0), 0))],
                          inc=s_pe if k == NDT - 1 else None)
                ntp1 = C.n(s_pe)
                stg1 = av(CP_stg, 1024, BF16)
                C.ins("vector", lambda v, stg1=stg1: v.tensor_copy(out=stg1, in_=Pb[3][:, 0:1024]),
                      waits=[(s_pe, ntp1), (s_sp, C.mem.get(("c_sw", 0), 0))], inc=s_ve)
                C.mem[("c_tw", 0)] = C.n(s_ve)
                nst1 = C.n(s_ve)
                for k in range(NDT):
                    C.dma("gpsimd", xn1T_d[k * 128:(k + 1) * 128, u * 128:(u + 1) * 128],
                          stg1[:, k * 128:(k + 1) * 128], waits=[(s_ve, nst1)], inc=s_sp)
                C.mem[("c_sw", 0)] = C.n(s_sp)
            if t < NT:
                C.ins("vector", lambda v: v.tensor_tensor(out=smc(5), in0=smc(4),
                      in1=smc(3), op=ALU.add), inc=s_ve)
                C.ins("scalar", lambda sc2: sc2.activation(out=smc(6), in_=smc(5),
                      func=AF.Sqrt), waits=[(s_ve, C.n(s_ve))], inc=s_ac)
            if t >= 1:
                C.ins("vector", lambda v: v.tensor_scalar(out=qs["sc"], in0=qs["bm"],
                      scalar1=1.0 / 6.0, scalar2=1e-12, op0=ALU.mult, op1=ALU.max))
                fillop()
                C.ins("vector", lambda v: v.reciprocal(out=qs["rsc"], in_=qs["sc"]))
                fillop()
                emit_fp4(xnb, xq, 1024, {**qb, **qs}, pre_done=True,
                         dst_waits=[(s_ve, C.mem.get(("c_tw", 1), 0))])
                nq = C.n(s_ve)
                C.mem[("c_quant_ac", u)] = C.n(s_ac)
                for k in range(NDT):
                    C.ins("tensor", lambda te, k=k: te.transpose(
                          Pb[3][:, 1024 + k * 128:1024 + (k + 1) * 128],
                          in_=xq[:, k * 128:(k + 1) * 128], identity=sidb[:]),
                          waits=[(s_ve, nq), (s_ve, C.mem.get(("c_tw", 1), 0))],
                          inc=s_pe if k == NDT - 1 else None)
                ntp2 = C.n(s_pe)
                stg2 = av(CP_stg + 2048, 1024, BF16)
                C.ins("vector", lambda v, stg2=stg2: v.tensor_copy(out=stg2, in_=Pb[3][:, 1024:2048]),
                      waits=[(s_pe, ntp2), (s_sp, C.mem.get(("c_sw", 1), 0))], inc=s_ve)
                C.mem[("c_tw", 1)] = C.n(s_ve)
                nst2 = C.n(s_ve)
                for k in range(NDT):
                    C.dma("gpsimd", xq1T_d[k * 128:(k + 1) * 128, u * 128:(u + 1) * 128],
                          stg2[:, k * 128:(k + 1) * 128], waits=[(s_ve, nst2)], inc=s_sp)
                C.mem[("c_sw", 1)] = C.n(s_sp)
            if t < NT:
                C.ins("vector", lambda v: v.reciprocal(out=smc(7), in_=smc(6)),
                      waits=[(s_ac, C.n(s_ac))])
            if t % 4 == 0 and t >= 4:
                ch_t = t // 4 - 1
                c2_dma(ch_t)
                if ch_t >= 1:
                    c2_compute(ch_t - 1, rows["l1_bk0"], rows["l1_bv0"],
                               av(CM_st0, 1024, F32, parts=64))
        c2_compute(NQC - 1, rows["l1_bk0"], rows["l1_bv0"],
                   av(CM_st0, 1024, F32, parts=64))

        C.mem[("c_lwp",)] = C.n(s_pe)
        load_l1_w(wa, l1_wk1, 1.0 / 64.0, True)
        load_l1_w(wb, l1_wv1, 1.0 / 64.0, True)
        C.mem[("c_nlw",)] = C.n(s_ve)
        c2_dma(0)
        for ch in range(NQC):
            if ch + 1 < NQC:
                c2_dma(ch + 1)
            c2_compute(ch, rows["l1_bk1"], rows["l1_bv1"],
                       av2(40960, 1024, F32, parts=64))
            c3i_step(3)
        nsboth = C.n(s_ve)
        for h in range(H):
            C.ins("vector", lambda v, h=h: v.tensor_tensor(
                  out=av(CM_state, 8 * 64, BF16)[(h % 2) * 64:(h % 2) * 64 + 64,
                                                 (h // 2) * 64:(h // 2) * 64 + 64],
                  in0=av2(40960, 1024, F32, parts=64)[:, h * 64:(h + 1) * 64],
                  in1=av(CM_st0, 1024, F32, parts=64)[:, h * 64:(h + 1) * 64],
                  op=ALU.mult),
                  waits=[(s_ve, nsboth)], inc=s_ve if h == H - 1 else None)
        nstate = C.n(s_ve)

        C.mem[("c_lwp",)] = C.n(s_pe)
        load_l1_w(wa, l1_wq, 1.0, True)
        load_l1_w(wb, l1_wo, 1.0, False)
        nlw2 = C.n(s_ve)
        qbf = av2(CF_q, 8 * 512, BF16).rearrange("p (k c) -> p k c", c=512)
        outv = av2(CF_out, 8 * 512, BF16).rearrange("p (k c) -> p k c", c=512)
        tstg = av2(CF_t, 8 * 512, F32).rearrange("p (k c) -> p k c", c=512)
        s1c = av2(CF_s1, 8 * 512, BF16).rearrange("p (k c) -> p k c", c=512)
        y2v2 = ksv
        lnov = vsv
        for ch in range(NQC):
            xnc = xncv(ch % 2)
            for k in range(NDT):
                C.dma("sync", xnc[:, k, :],
                      xn1T_d[k * 128:(k + 1) * 128, ch * 512:(ch + 1) * 512],
                      waits=[(s_pe, C.mem.get(("c_xw", ch % 2), 0))], inc=s_in)
            nxi = C.n(s_in)
            for m in range(NDT):
                pslot = P[0][:, (m % 2) * 512:(m % 2) * 512 + 512]
                for k in range(NDT):
                    C.ins("tensor", lambda te, m=m, k=k, pslot=pslot, xnc=xnc:
                          te.matmul(pslot, lhsT=wa[:, k, m * 128:(m + 1) * 128],
                                    rhs=xnc[:, k, :], start=(k == 0), stop=(k == NDT - 1)),
                          waits=[(s_ve, nlw2), (s_in, nxi),
                                 (s_ve, C.mem.get(("c_qe", m % 2), 0))],
                          inc=s_pe if k == NDT - 1 else None)
                nq_ = C.n(s_pe)
                C.ins("vector", lambda v, m=m, pslot=pslot: v.tensor_scalar(
                      out=qbf[:, m, :], in0=pslot,
                      scalar1=cols["l1_bq"][:, m:m + 1], scalar2=None, op0=ALU.add),
                      waits=[(s_pe, nq_)], inc=s_ve)
                C.mem[("c_qe", m % 2)] = C.n(s_ve)
            nqb = C.n(s_ve)
            for h in range(H):
                hb = (h % 2) * 64
                pslot = P[1][0:64, (h % 2) * 512:(h % 2) * 512 + 512]
                C.ins("tensor", lambda te, h=h, hb=hb, pslot=pslot:
                      te.matmul(pslot,
                                lhsT=av(CM_state, 8 * 64, BF16)[hb:hb + 64,
                                     (h // 2) * 64:(h // 2) * 64 + 64],
                                rhs=qbf[hb:hb + 64, h // 2, :],
                                start=True, stop=True),
                      waits=[(s_ve, max(nstate, nqb)),
                             (s_ve, C.mem.get(("c_oe", h % 2), 0))],
                      inc=s_pe)
                noh = C.n(s_pe)
                C.ins("vector", lambda v, h=h, hb=hb, pslot=pslot: v.tensor_copy(
                      out=outv[hb:hb + 64, h // 2, :], in_=pslot),
                      waits=[(s_pe, noh)], inc=s_ve)
                C.mem[("c_oe", h % 2)] = C.n(s_ve)
            nov = C.n(s_ve)
            for m in range(NDT):
                C.ins("vector", lambda v, m=m: v.tensor_tensor(
                      out=y2v2[:, m // 2, (m % 2) * 512:(m % 2) * 512 + 512],
                      in0=outv[:, m, :], in1=outv[:, m, :], op=ALU.mult),
                      inc=s_ve if m == NDT - 1 else None)
            ny2 = C.n(s_ve)
            for m in range(NDT):
                C.ins("tensor", lambda te, m=m: te.matmul(
                      P[2][0:1, 0:512], lhsT=onecol_b[:], rhs=outv[:, m, :],
                      start=(m == 0), stop=(m == NDT - 1)),
                      waits=[(s_ve, nov), (s_ve, C.mem.get(("c_rowr",), 0))],
                      inc=s_pe if m == NDT - 1 else None)
            for m in range(NDT):
                C.ins("tensor", lambda te, m=m: te.matmul(
                      P[2][0:1, 512:1024], lhsT=onecol_b[:],
                      rhs=y2v2[:, m // 2, (m % 2) * 512:(m % 2) * 512 + 512],
                      start=(m == 0), stop=(m == NDT - 1)),
                      waits=[(s_ve, ny2)],
                      inc=s_pe if m == NDT - 1 else None)
            nstt = C.n(s_pe)
            C.ins("vector", lambda v: v.tensor_scalar(
                  out=smr[:, QC:2 * QC], in0=P[2][0:1, 0:512],
                  scalar1=1.0 / D, scalar2=None, op0=ALU.mult),
                  waits=[(s_pe, nstt)])
            C.ins("vector", lambda v: v.tensor_tensor(
                  out=smr[:, 0:QC], in0=smr[:, QC:2 * QC], in1=smr[:, QC:2 * QC],
                  op=ALU.mult))
            C.ins("vector", lambda v: v.tensor_scalar(
                  out=smr[:, 2 * QC:3 * QC], in0=P[2][0:1, 512:1024],
                  scalar1=1.0 / D, scalar2=None, op0=ALU.mult))
            C.ins("vector", lambda v: v.scalar_tensor_tensor(
                  out=smr[:, 2 * QC:3 * QC], in0=smr[:, 2 * QC:3 * QC],
                  scalar=1e-5, in1=smr[:, 0:QC],
                  op0=ALU.add, op1=ALU.subtract), inc=s_ve)
            C.mem[("c_rowr",)] = C.n(s_ve)
            C.ins("scalar", lambda sc2: sc2.activation(
                  out=smr[:, 0:QC], in_=smr[:, 2 * QC:3 * QC], func=AF.Sqrt),
                  waits=[(s_ve, C.n(s_ve))], inc=s_ac)
            C.ins("vector", lambda v: v.reciprocal(
                  out=smr[:, 2 * QC:3 * QC], in_=smr[:, 0:QC]),
                  waits=[(s_ac, C.n(s_ac))], inc=s_ve)
            nrr = C.n(s_ve)
            C.ins("tensor", lambda te: te.matmul(P[2][:, 0:512], lhsT=onerow_f[:],
                  rhs=smr[:, QC:2 * QC], start=True, stop=True),
                  waits=[(s_ve, nrr)])
            C.ins("tensor", lambda te: te.matmul(P[3][:, 0:512], lhsT=onerow_f[:],
                  rhs=smr[:, 2 * QC:3 * QC], start=True, stop=True), inc=s_pe)
            nbc = C.n(s_pe)
            for m in range(NDT):
                C.ins("vector", lambda v, m=m: v.tensor_tensor(
                      out=tstg[:, m, :], in0=outv[:, m, :], in1=P[2][:, 0:512],
                      op=ALU.subtract), waits=[(s_pe, nbc)])
                C.ins("vector", lambda v, m=m: v.tensor_tensor(
                      out=tstg[:, m, :], in0=tstg[:, m, :], in1=P[3][:, 0:512],
                      op=ALU.mult))
                C.ins("vector", lambda v, m=m: v.tensor_scalar(
                      out=lnov[:, m // 2, (m % 2) * 512:(m % 2) * 512 + 512],
                      in0=tstg[:, m, :],
                      scalar1=cols["l1_lng"][:, m:m + 1],
                      scalar2=cols["l1_lnb"][:, m:m + 1],
                      op0=ALU.mult, op1=ALU.add),
                      inc=s_ve if m == NDT - 1 else None)
            nln = C.n(s_ve)
            for m in range(NDT):
                pslot = P[1][:, (m % 2) * 512:(m % 2) * 512 + 512]
                for k in range(NDT):
                    C.ins("tensor", lambda te, m=m, k=k, pslot=pslot:
                          te.matmul(pslot, lhsT=wb[:, k, m * 128:(m + 1) * 128],
                                    rhs=lnov[:, k // 2, (k % 2) * 512:(k % 2) * 512 + 512],
                                    start=(k == 0), stop=(k == NDT - 1)),
                          waits=[(s_ve, nln), (s_ve, C.mem.get(("c_s1e", m % 2), 0))],
                          inc=s_pe if k == NDT - 1 else None)
                ns1 = C.n(s_pe)
                C.ins("vector", lambda v, m=m, pslot=pslot: v.tensor_scalar(
                      out=s1c[:, m, :], in0=pslot,
                      scalar1=cols["l1_bo"][:, m:m + 1], scalar2=None, op0=ALU.add),
                      waits=[(s_pe, ns1), (s_sp, C.mem.get(("c_s1w",), 0))], inc=s_ve)
                C.mem[("c_s1e", m % 2)] = C.n(s_ve)
            ns1c = C.n(s_ve)
            for m in range(NDT):
                C.dma("gpsimd", s1_d[m * 128:(m + 1) * 128, ch * 512:(ch + 1) * 512],
                      s1c[:, m, :], waits=[(s_ve, ns1c)], inc=s_sp)
            C.mem[("c_s1w",)] = C.n(s_sp)
            C.mem[("c_xw", ch % 2)] = C.n(s_pe)
            c3i_step(3)
        while i3["t"] < NI3:
            c3i_step(1)

        q3 = {k: av(C3_q + i * 5632, 2816, BF16) for i, k in
              enumerate(("zz", "aa", "sgn", "T1", "T2", "T3", "srcb"))}
        q3s = {k: av(C3_qs + i * 704, 176, F32) for i, k in
              enumerate(("bm", "sc", "rsc"))}
        q3s["scb"] = av(C3_qs + 3 * 704, 176, BF16)
        nfc_ve = C.n(s_ve)
        for wi, (wsrc, wdst_d, nrow, fd) in enumerate(
                ((l1_wd, wdqT_d, 8, 2816),)):
            for t in range(nrow):
                stg = av(C3_stg + (t % 2) * 11264, fd, F32)
                C.dma("scalar", stg, wsrc[t * 128:(t + 1) * 128, :],
                      waits=[(s_ve, C.mem.get(("c3s", t % 2), nfc_ve))], inc=s_w)
                nw3 = C.n(s_w)
                wqb = av(C3_wqb + (t % 2) * 5632, fd, BF16)
                srcb = av(109696, 2816, BF16)[:, 0:fd]
                C.ins("vector", lambda v, stg=stg, srcb=srcb: v.tensor_copy(
                      out=srcb, in_=stg), waits=[(s_w, nw3), (s_ac, C.n(s_ac))])
                sub = {k: (q3[k][:, 0:fd] if k in q3 else None) for k in q3}
                sub.update({k: q3s[k][:, 0:fd // 16] for k in q3s})
                emit_fp4(srcb, wqb, fd, sub,
                         sep=lambda: C.ins("vector", lambda v: v.tensor_copy(
                             out=fillA[:], in_=fillB[:])),
                         dst_waits=[(s_ve, C.mem.get(("c3e", t % 2), 0))])
                C.mem[("c3s", t % 2)] = C.n(s_ve)
                nq3 = C.n(s_ve)
                ntr = fd // 128
                for j in range(ntr):
                    C.ins("tensor", lambda te, j=j, t=t, wqb=wqb: te.transpose(
                          Pb[2 + (t % 2)][:, (j % 8) * 128:(j % 8) * 128 + 128],
                          in_=wqb[:, j * 128:(j + 1) * 128], identity=sidb[:]),
                          waits=[(s_ve, nq3), (s_ve, C.mem.get(("c3e", t % 2), 0))],
                          inc=s_pe if (j % 8 == 7 or j == ntr - 1) else None)
                    if j % 8 == 7 or j == ntr - 1:
                        npe3 = C.n(s_pe)
                        j0 = (j // 8) * 8
                        nj = j - j0 + 1
                        stg3 = av(105568, 2048, BF16)
                        C.ins("vector", lambda v, nj=nj, t=t, stg3=stg3: v.tensor_copy(
                              out=stg3[:, 0:nj * 128], in_=Pb[2 + (t % 2)][:, 0:nj * 128]),
                              waits=[(s_pe, npe3), (s_sp, C.mem.get(("c3w",), 0))],
                              inc=s_ve)
                        C.mem[("c3e", t % 2)] = C.n(s_ve)
                        nev3 = C.n(s_ve)
                        for jj in range(nj):
                            C.dma("gpsimd",
                                  wdst_d[(j0 + jj) * 128:(j0 + jj + 1) * 128,
                                         t * 128:(t + 1) * 128],
                                  stg3[:, jj * 128:(jj + 1) * 128],
                                  waits=[(s_ve, nev3)], inc=s_sp)
                        C.mem[("c3w",)] = C.n(s_sp)

    D_h = 90112
    D_z = 117760
    D_a = 118784
    D_sgn = 119808
    D_T1 = 120832
    D_T2 = 121856
    D_T3 = 122880
    D_hq = 93184
    D_hqT = 95232
    D_sm = 123904

    def phase_D():
        fence_pe = C.n(s_pe)
        fence_ve = C.n(s_ve)
        fence_sp = C.n(s_sp)
        wgq = av(0, 8 * 2816, BF16).rearrange("p (k c) -> p k c", c=2816)
        wuq = av(45056, 8 * 2816, BF16).rearrange("p (k c) -> p k c", c=2816)
        wdq = av2(0, 22 * 1024, BF16).rearrange("p (j c) -> p j c", c=1024)
        for k in range(NDT):
            C.dma("sync", wgq[:, k, :], wgqT_d[k * 128:(k + 1) * 128, :],
                  waits=[(s_pe, fence_pe), (s_ve, fence_ve), (s_sp, fence_sp)],
                  inc=s_in)
            C.dma("sync", wuq[:, k, :], wuqT_d[k * 128:(k + 1) * 128, :], inc=s_in)
        for j in range(NIT):
            C.dma("sync", wdq[:, j, :], wdqT_d[j * 128:(j + 1) * 128, :], inc=s_in)
        nwl = C.n(s_in)
        qd = {"zz": av(D_z, 512, BF16), "aa": av(D_a, 512, BF16),
              "sgn": av(D_sgn, 512, BF16), "T1": av(D_T1, 512, BF16),
              "T2": av(D_T2, 512, BF16), "T3": av(D_T3, 512, BF16),
              "bm": av(D_sm, 32, F32), "sc": av(D_sm + 128, 32, F32),
              "rsc": av(D_sm + 256, 32, F32),
              "scb": av(D_sm + 384, 32, BF16)}
        xqc = av2(49152, 8 * 256, BF16).rearrange("p (k c) -> p k c", c=256)
        s1c = av2(45056, 8 * 256, BF16).rearrange("p (k c) -> p k c", c=256)
        gsg = av2(57344, 8 * 256, F32).rearrange("p (k c) -> p k c", c=256)
        hqTs = lambda i: av(D_hqT + i * 11264, 22 * 256, BF16).rearrange(
            "p (j c) -> p j c", c=256)
        sg1s = lambda i: av(D_hqT + i * 11264, 8 * 256, F32).rearrange(
            "p (k c) -> p k c", c=256)
        ICW = [512, 512, 512, 512, 512, 256]
        for tc in range(17):
            if tc < 16:
                hqT = hqTs(tc % 2)
                for k in range(NDT):
                    C.dma("scalar", xqc[:, k, :],
                          xq1T_d[k * 128:(k + 1) * 128, tc * 256:(tc + 1) * 256],
                          waits=[(s_in, nwl), (s_pe, C.mem.get(("d_xqw",), 0))],
                          inc=s_w)
                nxq = C.n(s_w)
                for pt in range(2):
                    for ic in range(6):
                        W = ICW[ic]
                        i0_ = ic * 512
                        pg = P[0][:, (ic % 2) * 512:(ic % 2) * 512 + W]
                        pu = P[1][:, (ic % 2) * 512:(ic % 2) * 512 + W]
                        for k in range(NDT):
                            C.ins("tensor", lambda te, k=k, pg=pg, pt=pt, i0_=i0_, W=W:
                                  te.matmul(pg, lhsT=xqc[:, k, pt * 128:(pt + 1) * 128],
                                            rhs=wgq[:, k, i0_:i0_ + W],
                                            start=(k == 0), stop=(k == NDT - 1)),
                                  waits=[(s_w, nxq),
                                         (s_ac, C.mem.get(("d_sil", ic % 2), 0))],
                                  inc=s_pe if k == NDT - 1 else None)
                            C.ins("tensor", lambda te, k=k, pu=pu, pt=pt, i0_=i0_, W=W:
                                  te.matmul(pu, lhsT=xqc[:, k, pt * 128:(pt + 1) * 128],
                                            rhs=wuq[:, k, i0_:i0_ + W],
                                            start=(k == 0), stop=(k == NDT - 1)),
                                  waits=[(s_ve, C.mem.get(("d_hm", ic % 2), 0))],
                                  inc=s_pe if k == NDT - 1 else None)
                        ngu = C.n(s_pe)
                        gsil = av(D_h, 512, F32)[:, 0:W]
                        hbuf = av(D_h + 2048, 512, BF16)[:, 0:W]
                        C.ins("scalar", lambda sc2, pg=pg, gsil=gsil: sc2.activation(
                              out=gsil, in_=pg, func=AF.Silu),
                              waits=[(s_pe, ngu), (s_ve, C.mem.get(("d_gs",), 0))],
                              inc=s_ac)
                        C.mem[("d_sil", ic % 2)] = C.n(s_ac)
                        nsil = C.n(s_ac)
                        C.ins("vector", lambda v, pu=pu, gsil=gsil, hbuf=hbuf:
                              v.tensor_tensor(out=hbuf, in0=pu, in1=gsil, op=ALU.mult),
                              waits=[(s_ac, nsil)], inc=s_ve)
                        C.mem[("d_gs",)] = C.n(s_ve)
                        C.mem[("d_hm", ic % 2)] = C.n(s_ve)
                        hq = av(D_hq + (ic % 2) * 1024, 512, BF16)[:, 0:W]
                        sub = {k2: (v2[:, 0:W] if k2 in ("zz", "aa", "sgn", "T1",
                                                          "T2", "T3")
                                    else v2[:, 0:W // 16]) for k2, v2 in qd.items()}
                        emit_fp4(hbuf, hq, W, sub,
                                 dst_waits=[(s_ve, C.mem.get(("d_hqe", ic % 2), 0))])
                        C.mem[("d_nhq", ic)] = C.n(s_ve)

                        def emit_tp(ic2, pt=pt, tc=tc, hqT=hqT):
                            W2 = ICW[ic2]
                            ntr = W2 // 128
                            hq2 = av(D_hq + (ic2 % 2) * 1024, 512, BF16)[:, 0:W2]
                            for j in range(ntr):
                                C.ins("tensor", lambda te, j=j, ic2=ic2, hq2=hq2: te.transpose(
                                      Pb[2][:, (ic2 % 2) * 1024 + j * 128:(ic2 % 2) * 1024 + (j + 1) * 128],
                                      in_=hq2[:, j * 128:(j + 1) * 128], identity=sidb[:]),
                                      waits=[(s_ve, C.mem[("d_nhq", ic2)]),
                                             (s_ve, C.mem.get(("d_hqe2", ic2 % 2), 0))],
                                      inc=s_pe if j == ntr - 1 else None)
                            ntp = C.n(s_pe)
                            C.ins("vector", lambda v, ic2=ic2, pt=pt, ntr=ntr, hqT=hqT:
                                  v.tensor_copy(
                                      out=hqT[:, ic2 * 4:ic2 * 4 + ntr, pt * 128:(pt + 1) * 128],
                                      in_=Pb[2][:, (ic2 % 2) * 1024:(ic2 % 2) * 1024 + ntr * 128]
                                      .rearrange("p (a c) -> p a c", c=128)),
                                  waits=[(s_pe, ntp),
                                         (s_pe, C.mem.get(("d_hqTw", tc % 2), 0))],
                                  inc=s_ve)
                            C.mem[("d_hqe", ic2 % 2)] = C.n(s_ve)
                            C.mem[("d_hqe2", ic2 % 2)] = C.n(s_ve)
                        if ic >= 1:
                            emit_tp(ic - 1)
                    emit_tp(5)
                C.mem[("d_xqw",)] = C.n(s_pe)
                C.mem[("d_nhqT", tc)] = C.n(s_ve)
            if tc >= 1:
                u = tc - 1
                hqT = hqTs(u % 2)
                sg1 = sg1s(u % 2)
                for k in range(NDT):
                    C.dma("gpsimd", s1c[:, k, :],
                          s1_d[k * 128:(k + 1) * 128, u * 256:(u + 1) * 256],
                          waits=[(s_ve, C.mem.get(("d_s1w",), 0))], inc=s_sp)
                ns1 = C.n(s_sp)
                nhqT = C.mem[("d_nhqT", u)]
                for m in range(NDT):
                    pd = P[3][:, (m % 2) * 512:(m % 2) * 512 + 256]
                    for j in range(NIT):
                        C.ins("tensor", lambda te, m=m, j=j, pd=pd, hqT=hqT:
                              te.matmul(pd, lhsT=wdq[:, j, m * 128:(m + 1) * 128],
                                        rhs=hqT[:, j, :], start=(j == 0),
                                        stop=(j == NIT - 1)),
                              waits=[(s_ve, nhqT),
                                     (s_ac, C.mem.get(("d_g1", m % 2), 0)),
                                     (s_ve, C.mem.get(("d_x2",), 0))],
                              inc=s_pe if j == NIT - 1 else None)
                    nd = C.n(s_pe)
                    C.ins("scalar", lambda sc2, m=m, pd=pd: sc2.activation(
                          out=gsg[:, m, :], in_=pd, func=AF.Silu),
                          waits=[(s_pe, nd), (s_ve, C.mem.get(("d_sgm",), 0))],
                          inc=s_ac)
                    C.mem[("d_g1", m % 2)] = C.n(s_ac)
                    C.mem[("d_g1m", m)] = C.n(s_ac)
                ndown = C.n(s_pe)
                for m in range(NDT):
                    C.ins("vector", lambda v, m=m, sg1=sg1: v.tensor_tensor(
                          out=sg1[:, m, :], in0=gsg[:, m, :], in1=s1c[:, m, :],
                          op=ALU.mult),
                          waits=[(s_ac, C.mem[("d_g1m", m)]), (s_sp, ns1),
                                 (s_pe, ndown)],
                          inc=s_ve if m == NDT - 1 else None)
                C.mem[("d_s1w",)] = C.n(s_ve)
                C.mem[("d_sgm",)] = C.n(s_ve)
                nsg = C.n(s_ve)
                for pt in range(2):
                    x1b = av2(53248, 1024, F32)
                    C.dma("sync", x1b,
                          x1_d[u * 256 + pt * 128:u * 256 + pt * 128 + 128, :],
                          waits=[(s_ve, C.mem.get(("d_x1w",), 0))], inc=s_in)
                    nx1 = C.n(s_in)
                    for m in range(NDT):
                        C.ins("tensor", lambda te, m=m, pt=pt, sg1=sg1: te.transpose(
                              P[3][:, m * 128:(m + 1) * 128],
                              in_=sg1[:, m, pt * 128:pt * 128 + 128], identity=sidf[:]),
                              waits=[(s_ve, nsg), (s_ve, C.mem.get(("d_x2",), 0)),
                                     (s_ac, C.mem[("d_g1m", NDT - 1)])],
                              inc=s_pe if m == NDT - 1 else None)
                    ntx = C.n(s_pe)
                    x2b = x1b
                    C.ins("vector", lambda v, x2b=x2b: v.tensor_tensor(
                          out=x2b, in0=x2b, in1=P[3][:], op=ALU.add),
                          waits=[(s_pe, ntx), (s_in, nx1),
                                 (s_out, C.mem.get(("d_ow",), 0))], inc=s_ve)
                    C.mem[("d_x2",)] = C.n(s_ve)
                    C.mem[("d_x1w",)] = C.n(s_ve)
                    nx2 = C.n(s_ve)
                    C.dma("sync", out_d[u * 256 + pt * 128:u * 256 + pt * 128 + 128, :],
                          x2b, waits=[(s_ve, nx2)], inc=s_out)
                    C.mem[("d_ow",)] = C.n(s_out)
                C.mem[("d_hqTw", u % 2)] = C.n(s_pe)

    def program():
        preamble()
        if "A" in phases:
            phase_A()
        if "BP" in phases:
            phase_BP()
        if "ATT" in phases:
            phase_ATT()
        if "B" in phases:
            phase_B()
        if "C" in phases:
            phase_C()
        if "D" in phases:
            phase_D()
        C.ins("sync", lambda e: e.nop(),
              waits=[(s_out, C.n(s_out)), (s_sp, C.n(s_sp)),
                     (s_in, C.n(s_in))])

    with nc.Block() as block:
        @block.sync
        def _(e):
            C.begin("sync", e); program()

        @block.vector
        def _(e):
            C.begin("vector", e); program()

        @block.tensor
        def _(e):
            C.begin("tensor", e); program()

        @block.scalar
        def _(e):
            C.begin("scalar", e); program()

        @block.gpsimd
        def _(e):
            C.begin("gpsimd", e); program()

    es.close()
    nc.finalize()
    return nc


def _prep_inputs(inputs):
    f = np.float32
    asc = np.ascontiguousarray
    base = {
        "l0_wq": asc(inputs["L0_attn_Wq"].reshape(D, D), dtype=f),
        "l0_wk": asc(inputs["L0_attn_Wk"].reshape(D, D), dtype=f),
        "l0_wv": asc(inputs["L0_attn_Wv"].reshape(D, D), dtype=f),
        "l0_bq": asc(inputs["L0_attn_bq"].reshape(D), dtype=f),
        "l0_bk": asc(inputs["L0_attn_bk"].reshape(D), dtype=f),
        "l0_bv": asc(inputs["L0_attn_bv"].reshape(D), dtype=f),
        "l0_wo": asc(inputs["L0_attn_Wo"].reshape(D, D), dtype=f),
        "l0_bo": asc(inputs["L0_attn_bo"], dtype=f),
        "l0_gam": asc(inputs["L0_attn_gamma"], dtype=f),
        "l0_bet": asc(inputs["L0_attn_beta"], dtype=f),
        "l0_w1": asc(inputs["L0_ffn_W1"], dtype=f),
        "l0_b1": asc(inputs["L0_ffn_b1"], dtype=f),
        "l0_w2": asc(inputs["L0_ffn_W2"], dtype=f),
        "l0_b2": asc(inputs["L0_ffn_b2"], dtype=f),
        "l1_wq": asc(inputs["L1_attn_Wq"], dtype=f),
        "l1_bq": asc(inputs["L1_attn_bq"], dtype=f),
        "l1_wk0": asc(inputs["L1_attn_Wk"][0], dtype=f),
        "l1_wk1": asc(inputs["L1_attn_Wk"][1], dtype=f),
        "l1_bk0": asc(inputs["L1_attn_bk"][0], dtype=f),
        "l1_bk1": asc(inputs["L1_attn_bk"][1], dtype=f),
        "l1_wv0": asc(inputs["L1_attn_Wv"][0], dtype=f),
        "l1_wv1": asc(inputs["L1_attn_Wv"][1], dtype=f),
        "l1_bv0": asc(inputs["L1_attn_bv"][0], dtype=f),
        "l1_bv1": asc(inputs["L1_attn_bv"][1], dtype=f),
        "l1_wo": asc(inputs["L1_attn_Wo"], dtype=f),
        "l1_bo": asc(inputs["L1_attn_bo"], dtype=f),
        "l1_lng": asc(inputs["L1_attn_ln_g"], dtype=f),
        "l1_lnb": asc(inputs["L1_attn_ln_b"], dtype=f),
        "l1_wg": asc(inputs["L1_ffn_w_gate"], dtype=f),
        "l1_wu": asc(inputs["L1_ffn_w_up"], dtype=f),
        "l1_wd": asc(inputs["L1_ffn_w_down"], dtype=f),
        "idn_f": np.eye(128, dtype=f),
        "idn_b": np.eye(128, dtype=ml_dtypes.bfloat16),
    }
    x = np.ascontiguousarray(np.asarray(inputs["L0_attn_x"], dtype=f))
    maps = []
    for b in range(8):
        m = dict(base)
        m["x"] = asc(x[b])
        maps.append(m)
    return maps


def kernel(**inputs):
    nc = build_program()
    in_maps = _prep_inputs(inputs)
    res = run_bass_kernel_spmd(nc, in_maps, core_ids=list(range(8)))
    return np.stack([np.asarray(res.results[b]["out"], dtype=np.float32)
                     for b in range(8)], axis=0)
